# version 08
# baseline (speedup 1.0000x reference)
import numpy as np
import ml_dtypes
from contextlib import ExitStack

import concourse.bass as bass
import concourse.mybir as mybir
from concourse.bass import IndirectOffsetOnAxis
from concourse.bass_utils import run_bass_kernel_spmd

F32 = mybir.dt.float32
BF16 = mybir.dt.bfloat16
U32 = mybir.dt.uint32
AF = mybir.ActivationFunctionType
ALU = mybir.AluOpType
AX = mybir.AxisListType.X

D = 1024
DT = 8
CH = 512
LN_EPS = 1e-5


class Sem:
    __slots__ = ("h", "n")

    def __init__(self, h):
        self.h = h
        self.n = 0

    def inc(self, inst, k=1):
        inst.then_inc(self.h, k)
        self.n += k
        return self.n


def wait(eng, *toks):
    for t in toks:
        if isinstance(t, tuple):
            eng.wait_ge(t[0].h, t[1])
        else:
            eng.wait_ge(t.h, t.n)


class P:
    def __init__(self, nc, block, stack):
        self.nc = nc
        self.b = block
        self.stack = stack
        self.scopes = [stack]
        self._sems = {}

    def sem(self, name):
        if name not in self._sems:
            h = self.stack.enter_context(self.nc.semaphore(name))
            self._sems[name] = Sem(h)
        return self._sems[name]

    def sbuf(self, name, shape, dtype):
        return self.scopes[-1].enter_context(self.nc.sbuf_tensor(name, shape, dtype))


def emit_block_barrier(p):
    tv, vs, sv, apt = p.sem("tv"), p.sem("vs"), p.sem("sv"), p.sem("apt")
    vt, qS, qG = p.sem("vt"), p.sem("qS"), p.sem("qG")
    gq = p.sem("gq")
    toks = (tv, vs, sv, apt, vt, (qS, qS.n), (qG, qG.n), (gq, gq.n))
    for sec in (p.b.sync, p.b.gpsimd, p.b.vector, p.b.scalar, p.b.tensor):
        @sec
        def _(e):
            wait(e, *toks)


def new_bass():
    return bass.Bass("TRN2", target_bir_lowering=False, debug=False,
                     detect_race_conditions=False)



def emit_consts(p, cident_ap):
    c = {}
    c["ones128"] = p.sbuf("c_ones128", [128, 1], BF16)
    c["ones_row"] = p.sbuf("c_ones_row", [1, CH], BF16)
    c["ones1x128"] = p.sbuf("c_ones1x128", [1, 128], BF16)
    c["ones128f"] = p.sbuf("c_ones128f", [128, 1], F32)
    c["ones1x128f"] = p.sbuf("c_ones1x128f", [1, 128], F32)
    c["identity"] = p.sbuf("c_identity", [128, 128], BF16)
    s = p.sem("const")
    qS = p.sem("qS")

    @p.b.sync
    def _(e):
        qS.inc(e.dma_start(out=c["identity"][:], in_=cident_ap), 16)

    @p.b.vector
    def _(e):
        e.memset(c["ones128"][:], 1.0)
        e.memset(c["ones_row"][:], 1.0)
        e.memset(c["ones128f"][:], 1.0)
        e.memset(c["ones1x128f"][:], 1.0)
        s.inc(e.memset(c["ones1x128"][:], 1.0))
    return c



def make_ln_banks(p, stack, tag="lnb"):
    nc = p.nc
    ec = stack.enter_context
    return {
        "ps1": ec(nc.psum_tensor(tag + "_ps1", [1, CH], F32)),
        "ps2": ec(nc.psum_tensor(tag + "_ps2", [1, CH], F32)),
        "pA": ec(nc.psum_tensor(tag + "_pA", [128, CH], F32)),
        "pB": ec(nc.psum_tensor(tag + "_pB", [128, CH], F32)),
        "xcb": ec(nc.sbuf_tensor(tag + "_xcb", [128, DT * CH], BF16)),
        "sq": ec(nc.sbuf_tensor(tag + "_sq", [128, DT * CH], BF16)),
        "rows": ec(nc.sbuf_tensor(tag + "_rows", [1, 7 * CH], F32)),
        "rowsb": ec(nc.sbuf_tensor(tag + "_rowsb", [1, 2 * CH], BF16)),
        "tmp": ec(nc.sbuf_tensor(tag + "_tmp", [128, CH], BF16)),
    }


def emit_ln_chunk(p, c, banks, src, dst, gb=None, nch=CH, f32=False):
    vt = p.sem("vt")
    tv = p.sem("tv")
    vs = p.sem("vs")
    sv = p.sem("sv")
    apt = p.sem("apt")
    ps1, ps2, pA, pB = banks["ps1"], banks["ps2"], banks["pA"], banks["pB"]
    xcb, sq, rows, rowsb, tmp = (banks["xcb"], banks["sq"], banks["rows"],
                                 banks["rowsb"], banks["tmp"])
    ones_col = c["ones128f"] if f32 else c["ones128"]
    ones_lhs = c["ones1x128f"] if f32 else c["ones1x128"]
    mu = rows[:, 0 * CH:0 * CH + nch]
    m2 = rows[:, 1 * CH:1 * CH + nch]
    var = rows[:, 2 * CH:2 * CH + nch]
    vre = rows[:, 3 * CH:3 * CH + nch]
    srt = rows[:, 4 * CH:4 * CH + nch]
    rr = rows[:, 5 * CH:5 * CH + nch]
    B0 = rows[:, 6 * CH:6 * CH + nch]
    rb = rowsb[:, 0:nch]
    B0b = rowsb[:, CH:CH + nch]

    @p.b.vector
    def _(e):
        wait(e, (tv, tv.n))
        for i in range(DT):
            e.tensor_copy(out=xcb[:, i * CH:i * CH + nch], in_=src(i))
        vt.inc(e.tensor_tensor(out=sq[:], in0=xcb[:], in1=xcb[:], op=ALU.mult))

    @p.b.tensor
    def _(e):
        wait(e, (vt, vt.n))
        for i in range(DT):
            e.matmul(out=ps1[:, 0:nch], lhsT=ones_col[:],
                     rhs=xcb[:, i * CH:i * CH + nch],
                     start=(i == 0), stop=(i == DT - 1))
        for i in range(DT):
            m = e.matmul(out=ps2[:, 0:nch], lhsT=ones_col[:],
                         rhs=sq[:, i * CH:i * CH + nch],
                         start=(i == 0), stop=(i == DT - 1))
        tv.inc(m)

    @p.b.vector
    def _(e):
        wait(e, (tv, tv.n))
        e.tensor_scalar(out=mu, in0=ps1[:, 0:nch], scalar1=1.0 / D,
                        scalar2=None, op0=ALU.mult)
        e.tensor_scalar(out=m2, in0=ps2[:, 0:nch], scalar1=1.0 / D,
                        scalar2=None, op0=ALU.mult)
        e.scalar_tensor_tensor(out=var, in0=mu, scalar=-1.0, in1=mu,
                               op0=ALU.mult, op1=ALU.mult)
        e.tensor_add(out=var, in0=m2, in1=var)
        vs.inc(e.tensor_scalar(out=vre, in0=var, scalar1=LN_EPS, scalar2=None,
                               op0=ALU.add))

    @p.b.scalar
    def _(e):
        wait(e, (vs, vs.n))
        sv.inc(e.activation(out=srt, in_=vre, func=AF.Sqrt))

    @p.b.vector
    def _(e):
        wait(e, (sv, sv.n))
        e.reciprocal(out=rr, in_=srt)
        e.drain()
        e.scalar_tensor_tensor(out=B0, in0=mu, scalar=-1.0, in1=rr,
                               op0=ALU.mult, op1=ALU.mult)
        if not f32:
            e.tensor_copy(out=rb, in_=rr)
            e.tensor_copy(out=B0b, in_=B0)
        vt.inc(e.drain())

    for i in range(DT):
        @p.b.tensor
        def _(e, i=i):
            wait(e, (vt, vt.n))
            if i > 0:
                wait(e, (apt, apt.n))
            if gb is None:
                e.matmul(out=pA[:, 0:nch], lhsT=ones_lhs[:],
                         rhs=rr if f32 else rb, start=True, stop=True)
                m = e.matmul(out=pB[:, 0:nch], lhsT=ones_lhs[:],
                             rhs=B0 if f32 else B0b, start=True, stop=True)
            else:
                g_row, b_row = gb
                gseg = g_row[:, i * 128:(i + 1) * 128]
                bseg = b_row[:, i * 128:(i + 1) * 128]
                e.matmul(out=pA[:, 0:nch], lhsT=gseg, rhs=rb, start=True, stop=True)
                e.matmul(out=pB[:, 0:nch], lhsT=gseg, rhs=B0b, start=True, stop=False)
                m = e.matmul(out=pB[:, 0:nch], lhsT=bseg,
                             rhs=c["ones_row"][:, 0:nch], start=False, stop=True)
            tv.inc(m)

        @p.b.vector
        def _(e, i=i):
            wait(e, (tv, tv.n))
            e.tensor_tensor(out=tmp[:, 0:nch], in0=xcb[:, i * CH:i * CH + nch],
                            in1=pA[:, 0:nch], op=ALU.mult)
            apt.inc(e.tensor_tensor(out=dst(i), in0=tmp[:, 0:nch],
                                    in1=pB[:, 0:nch], op=ALU.add))


def emit_preln(p, c, T, xin_hbm, dst, banks, gb=None, xc=None):
    NCH = T // CH
    qS = p.sem("qS")
    apt = p.sem("apt")
    src4 = xin_hbm.rearrange("(i p) (n f) -> p i n f", p=128, f=CH)
    apt_base = apt.n
    for ci in range(NCH):
        @p.b.sync
        def _(e, ci=ci):
            if ci >= 2:
                wait(e, (apt, apt_base + (ci - 1) * DT))
            qS.inc(e.dma_start(
                out=xc[ci % 2][:].rearrange("p (i f) -> p i f", f=CH),
                in_=src4[:, :, ci, :]), 16)
        ld_tok = qS.n

        def src(i, ci=ci):
            return xc[ci % 2][:, i * CH:(i + 1) * CH]

        def dst_sl(i, ci=ci):
            return dst[:, i * T + ci * CH: i * T + (ci + 1) * CH]

        @p.b.vector
        def _(e, tok=ld_tok):
            wait(e, (qS, tok))
        emit_ln_chunk(p, c, banks, src, dst_sl, gb=gb)



def _np_ln(x, g=None, b=None, axis=0):
    mu = x.mean(axis=axis, keepdims=True)
    var = ((x - mu) ** 2).mean(axis=axis, keepdims=True)
    y = (x - mu) / np.sqrt(var + LN_EPS)
    if g is not None:
        y = y * g[:, None] + b[:, None]
    return y



def emit_l0_attn(p, c, T, xin, xout, w, phases=("A", "BC", "D1", "D2")):
    nc = p.nc
    NCH = T // CH
    NQ = T // 128
    qS = p.sem("qS")
    qG = p.sem("qG")
    vt = p.sem("vt")
    tv = p.sem("tv")
    vs = p.sem("vs")
    sv = p.sem("sv")
    apt = p.sem("apt")

    aT = p.sbuf("l0a_aT", [128, DT * T], BF16)
    grow = p.sbuf("l0a_grows", [1, 4 * D], BF16)
    ln1g, ln1b = grow[:, 0:D], grow[:, D:2 * D]
    ln2g, ln2b = grow[:, 2 * D:3 * D], grow[:, 3 * D:4 * D]
    qkvb = p.sbuf("l0a_qkvb", [128, 24], F32)
    qbs = p.sbuf("l0a_qbs", [128, 8], F32)
    mask4_0 = p.sbuf("l0a_mask40", [128, 1024], BF16)
    mask4_g = p.sbuf("l0a_mask4g", [128, 1024], BF16)

    @p.b.gpsimd
    def _(e):
        qG.inc(e.dma_start(out=grow[:, 0:D], in_=w["ln1g"]), 16)
        qG.inc(e.dma_start(out=grow[:, D:2 * D], in_=w["ln1b"]), 16)
        qG.inc(e.dma_start(out=grow[:, 2 * D:3 * D], in_=w["ln2g"]), 16)
        qG.inc(e.dma_start(out=grow[:, 3 * D:4 * D], in_=w["ln2b"]), 16)
        qG.inc(e.dma_start(out=qkvb[:, 0:8], in_=w["qb2"]), 16)
        qG.inc(e.dma_start(out=qkvb[:, 8:16], in_=w["kb2"]), 16)
        qG.inc(e.dma_start(out=qkvb[:, 16:24], in_=w["vb2"]), 16)
    smalls_tok = qG.n

    @p.b.sync
    def _(e):
        qS.inc(e.dma_start(out=mask4_0[:], in_=w["mask4_0"]), 16)
        qS.inc(e.dma_start(out=mask4_g[:], in_=w["mask4_g"]), 16)
    mask_tok = qS.n

    @p.b.vector
    def _(e):
        wait(e, (qG, smalls_tok))
        vt.inc(e.tensor_scalar(out=qbs[:], in0=qkvb[:, 0:8], scalar1=0.125,
                               scalar2=None, op0=ALU.mult))
    qbs_tok = vt.n

    with ExitStack() as ph_u:
        uT = ph_u.enter_context(nc.sbuf_tensor("l0a_uT", [128, DT * T], BF16))

        with ExitStack() as ph:
            banks = make_ln_banks(p, ph, tag="l0a_ln")
            xc = [ph.enter_context(nc.sbuf_tensor("l0a_xc%d" % i,
                                                  [128, DT * CH], F32))
                  for i in range(2)]
            emit_preln(p, c, T, xin, uT, banks, gb=None, xc=xc)
        uT_tok = p.sem("apt").n

        if "DUMPU" in phases:
            @p.b.gpsimd
            def _(e):
                wait(e, (apt, uT_tok))
                qG.inc(e.dma_start(
                    out=xout.rearrange("(i p) t -> p i t", p=128),
                    in_=uT[:].rearrange("p (i t) -> p i t", t=T)), 16)

            @p.b.gpsimd
            def _(e):
                wait(e, (qG, qG.n))
            return

        with ExitStack() as ph:
            ec = ph.enter_context
            wbuf = [ec(nc.sbuf_tensor("l0a_w%d" % i, [128, 3 * DT * 128], BF16))
                    for i in range(2)]
            QT = ec(nc.sbuf_tensor("l0a_QT", [128, T], BF16))
            KT = ec(nc.sbuf_tensor("l0a_KT", [128, T], BF16))
            VT = ec(nc.sbuf_tensor("l0a_VT", [128, T], BF16))
            Vtok = ec(nc.sbuf_tensor("l0a_Vtok", [128, NQ * 128], BF16))
            Pn4 = ec(nc.sbuf_tensor("l0a_Pn4", [128, 1024], BF16))
            PTs = ec(nc.sbuf_tensor("l0a_PTs", [128, 1024], BF16))
            nmrow = ec(nc.sbuf_tensor("l0a_nm", [128, 12], F32))
            pS4 = [ec(nc.psum_tensor("l0a_pS%d" % i, [128, 512], F32))
                   for i in range(2)]
            pT = ec(nc.psum_tensor("l0a_pT", [128, 1024], BF16))
            pT2 = [ec(nc.psum_tensor("l0a_pT2%d" % i, [128, 128], BF16))
                   for i in range(2)]
            pO = ec(nc.psum_tensor("l0a_pO", [128, 512], F32))
            pQ, pK, pV = pS4[0], pS4[1], pO
            ptc_tok = [0]
            oev_tok = [0]

            import os as _os
            for pp in range(8 if "BC" in phases else 0):
                @p.b.gpsimd
                def _(e, pp=pp):
                    if pp >= 2:
                        wait(e, (tv, tv.n))
                    else:
                        wait(e, (apt, uT_tok))
                    wsl = slice(pp * 128, (pp + 1) * 128)
                    for j, wn in enumerate(("wq", "wk", "wv")):
                        src = w[wn].rearrange("(i p) n -> p i n", p=128)[:, :, wsl]
                        qG.inc(e.dma_start(
                            out=wbuf[pp % 2][:, j * DT * 128:(j + 1) * DT * 128]
                            .rearrange("p (i n) -> p i n", n=128),
                            in_=src), 16)
                wtok = qG.n

                for ci in range(NCH):
                    @p.b.tensor
                    def _(e, pp=pp, ci=ci, wtok=wtok):
                        wait(e, (qG, wtok), (apt, uT_tok), (vt, vt.n))
                        if ci == 0:
                            wait(e, (sv, sv.n), (vs, vs.n))
                        wb = wbuf[pp % 2]
                        for j, pdst in enumerate((pQ, pK, pV)):
                            for i in range(DT):
                                m = e.matmul(
                                    out=pdst[:],
                                    lhsT=wb[:, j * DT * 128 + i * 128:
                                            j * DT * 128 + (i + 1) * 128],
                                    rhs=uT[:, i * T + ci * CH:
                                           i * T + (ci + 1) * CH],
                                    start=(i == 0), stop=(i == DT - 1))
                            tv.inc(m)

                    @p.b.vector
                    def _(e, pp=pp, ci=ci):
                        wait(e, (tv, tv.n), (vt, qbs_tok))
                        sl = slice(ci * CH, (ci + 1) * CH)
                        e.tensor_scalar(out=QT[:, sl], in0=pQ[:], scalar1=0.125,
                                        scalar2=qbs[:, pp:pp + 1], op0=ALU.mult,
                                        op1=ALU.add)
                        e.tensor_scalar(out=KT[:, sl], in0=pK[:],
                                        scalar1=qkvb[:, 8 + pp:9 + pp],
                                        scalar2=None, op0=ALU.add)
                        vt.inc(e.tensor_scalar(out=VT[:, sl], in0=pV[:],
                                               scalar1=qkvb[:, 16 + pp:17 + pp],
                                               scalar2=None, op0=ALU.add))

                for tt in range(0 if _os.environ.get("L0A_NOVT") else NQ):
                    @p.b.tensor
                    def _(e, tt=tt, pp=pp):
                        wait(e, (vt, vt.n))
                        if tt >= 2:
                            wait(e, (apt, apt.n - 1))
                        tv.inc(e.transpose(
                            out=pT2[tt % 2][:],
                            in_=VT[:, tt * 128:(tt + 1) * 128],
                            identity=c["identity"][:]))

                    @p.b.vector
                    def _(e, tt=tt, pp=pp):
                        wait(e, (tv, tv.n))
                        apt.inc(e.tensor_copy(
                            out=Vtok[:, tt * 128:(tt + 1) * 128],
                            in_=pT2[tt % 2][:]))

                for hh in range(2):
                    hsl = slice(hh * 64, (hh + 1) * 64)
                    for qg in range(NQ // 4):
                        qbl = [qg * 4 + k for k in range(4)]

                        @p.b.tensor
                        def _(e, hh=hh, qg=qg, qbl=qbl, hsl=hsl):
                            wait(e, (sv, sv.n), (qS, mask_tok))
                            if qg == 0 and hh == 0:
                                wait(e, (apt, apt.n), (vs, vs.n))
                            for k, qb in enumerate(qbl):
                                kcols = slice(0, 256) if qb == 0 else \
                                    slice((qb - 1) * 128, (qb + 1) * 128)
                                m = e.matmul(
                                    out=pS4[k // 2][:, (k % 2) * 256:
                                                    (k % 2) * 256 + 256],
                                    lhsT=QT[hsl, qb * 128:(qb + 1) * 128],
                                    rhs=KT[hsl, kcols],
                                    start=True, stop=True)
                                tv.inc(m)

                        @p.b.vector
                        def _(e, qg=qg):
                            wait(e, (tv, tv.n))
                            e.reduce_max(out=nmrow[:, 0:2],
                                         in_=pS4[0][:].rearrange(
                                             "p (a b) -> p a b", b=256),
                                         axis=AX, negate=True)
                            e.reduce_max(out=nmrow[:, 2:4],
                                         in_=pS4[1][:].rearrange(
                                             "p (a b) -> p a b", b=256),
                                         axis=AX, negate=True)
                            vs.inc(e.drain())

                        @p.b.scalar
                        def _(e, qg=qg):
                            wait(e, (vs, vs.n))
                            for k in range(4):
                                m = e.activation(
                                    out=Pn4[:, k * 256:(k + 1) * 256],
                                    in_=pS4[k // 2][:, (k % 2) * 256:
                                                    (k % 2) * 256 + 256],
                                    func=AF.Exp, bias=nmrow[:, k:k + 1],
                                    scale=1.0)
                            sv.inc(m)

                        @p.b.vector
                        def _(e, qg=qg):
                            wait(e, (sv, sv.n))
                            mk = mask4_0 if qg == 0 else mask4_g
                            e.tensor_tensor(out=Pn4[:], in0=Pn4[:], in1=mk[:],
                                            op=ALU.mult)
                            e.drain()
                            e.reduce_sum(out=nmrow[:, 4:8],
                                         in_=Pn4[:].rearrange(
                                             "p (a b) -> p a b", b=256),
                                         axis=AX)
                            e.drain()
                            e.reciprocal(out=nmrow[:, 8:12], in_=nmrow[:, 4:8])
                            vs.inc(e.drain())

                        @p.b.vector
                        def _(e, qg=qg):
                            wait(e, (vs, vs.n))
                            for k in range(4):
                                m = e.tensor_scalar(
                                    out=Pn4[:, k * 256:(k + 1) * 256],
                                    in0=Pn4[:, k * 256:(k + 1) * 256],
                                    scalar1=nmrow[:, 8 + k:9 + k],
                                    scalar2=None, op0=ALU.mult)
                            vs.inc(m)

                        @p.b.tensor
                        def _(e, qg=qg):
                            wait(e, (vs, vs.n), (sv, ptc_tok[0]))
                            for z in range(8):
                                m = e.transpose(
                                    out=pT[:, z * 128:(z + 1) * 128],
                                    in_=Pn4[:, z * 128:(z + 1) * 128],
                                    identity=c["identity"][:])
                            tv.inc(m)

                        @p.b.scalar
                        def _(e, qg=qg):
                            wait(e, (tv, tv.n))
                            sv.inc(e.copy(out=PTs[:], in_=pT[:]))
                        ptc_tok[0] = p.sem("sv").n

                        @p.b.tensor
                        def _(e, hh=hh, qg=qg, qbl=qbl, hsl=hsl):
                            wait(e, (sv, sv.n), (vs, oev_tok[0]))
                            tp = (0, 64) if hh == 1 else None
                            for k, qb in enumerate(qbl):
                                for bb in range(2):
                                    tt = bb if qb == 0 else qb - 1 + bb
                                    lhs = Vtok[:, tt * 128 + hh * 64:
                                               tt * 128 + hh * 64 + 64]
                                    m = e.matmul(
                                        out=pO[hsl, k * 128:(k + 1) * 128],
                                        lhsT=lhs,
                                        rhs=PTs[:, (k * 2 + bb) * 128:
                                                (k * 2 + bb) * 128 + 128],
                                        start=(bb == 0), stop=(bb == 1),
                                        tile_position=tp)
                                tv.inc(m)

                        @p.b.vector
                        def _(e, hh=hh, qg=qg, pp=pp, hsl=hsl):
                            wait(e, (tv, tv.n))
                            m = e.tensor_copy(
                                out=aT[hsl, pp * T + qg * 512:
                                       pp * T + (qg + 1) * 512],
                                in_=pO[hsl, :])
                            vs.inc(m)
                            apt.inc(e.drain())
                        oev_tok[0] = p.sem("vs").n

        if "DUMPA" in phases:
            @p.b.gpsimd
            def _(e):
                wait(e, (apt, apt.n), (tv, tv.n), (vs, vs.n), (sv, sv.n))
                qG.inc(e.dma_start(
                    out=xout.rearrange("(i p) t -> p i t", p=128),
                    in_=aT[:].rearrange("p (i t) -> p i t", t=T)), 16)

            @p.b.gpsimd
            def _(e):
                wait(e, (qG, qG.n))
            return
        if "D1" not in phases:
            return
        with ExitStack() as ph:
            banks = make_ln_banks(p, ph, tag="l0a_ln1")
            zA = ph.enter_context(nc.sbuf_tensor("l0a_zA", [128, DT * CH], F32))

            @p.b.tensor
            def _(e):
                wait(e, (qG, smalls_tok))

            for ci in range(NCH):
                @p.b.vector
                def _(e, ci=ci):
                    import os
                    wait(e, (apt, apt.n))
                    for i in range(DT):
                        if os.environ.get("D1ZCOPY"):
                            e.tensor_copy(
                                out=zA[:, i * CH:(i + 1) * CH],
                                in_=aT[:, i * T + ci * CH: i * T + (ci + 1) * CH])
                        else:
                            e.scalar_tensor_tensor(
                                out=zA[:, i * CH:(i + 1) * CH],
                                in0=uT[:, i * T + ci * CH: i * T + (ci + 1) * CH],
                                scalar=1.0,
                                in1=aT[:, i * T + ci * CH: i * T + (ci + 1) * CH],
                                op0=ALU.mult, op1=ALU.add)

                def src(i, ci=ci):
                    import os
                    if os.environ.get("D1NOADD"):
                        return aT[:, i * T + ci * CH: i * T + (ci + 1) * CH]
                    return zA[:, i * CH:(i + 1) * CH]

                def dstf(i, ci=ci):
                    return aT[:, i * T + ci * CH: i * T + (ci + 1) * CH]

                import os
                emit_ln_chunk(p, c, banks, src, dstf, gb=None if os.environ.get('D1PLAIN') else (ln1g, ln1b))

    if "DUMPX" in phases:
        @p.b.gpsimd
        def _(e):
            wait(e, (apt, apt.n), (tv, tv.n), (vs, vs.n), (sv, sv.n))
            qG.inc(e.dma_start(
                out=xout.rearrange("(i p) t -> p i t", p=128),
                in_=aT[:].rearrange("p (i t) -> p i t", t=T)), 16)

        @p.b.gpsimd
        def _(e):
            wait(e, (qG, qG.n))
        return
    if "D2" not in phases:
        return
    with ExitStack() as ph:
        ec = ph.enter_context
        banks = make_ln_banks(p, ph, tag="l0a_ln2")
        wff1 = ec(nc.sbuf_tensor("l0a_wff1", [128, DT * 2048], BF16))
        wff2 = [ec(nc.sbuf_tensor("l0a_wff2_%d" % i, [128, 16 * 128], BF16))
                for i in range(2)]
        ffb = ec(nc.sbuf_tensor("l0a_ffb", [128, 24], F32))
        hT = ec(nc.sbuf_tensor("l0a_hT", [128, 16 * CH], BF16))
        zA = ec(nc.sbuf_tensor("l0a_zA2", [128, DT * CH], F32))
        zB = zA
        xc2 = ec(nc.sbuf_tensor("l0a_xc2", [128, DT * CH], F32))
        pH2 = [ec(nc.psum_tensor("l0a_pH%d" % i, [128, CH], F32))
               for i in range(2)]
        pF2 = [ec(nc.psum_tensor("l0a_pF%d" % i, [128, CH], F32))
               for i in range(2)]
        hsil_tok = [0, 0]
        fev_tok = [0, 0]

        @p.b.gpsimd
        def _(e):
            wait(e, (tv, tv.n), (apt, apt.n), (vs, vs.n), (sv, sv.n))
            qG.inc(e.dma_start(out=wff1[:].rearrange("p (i n) -> p i n", n=2048),
                               in_=w["ff1w"].rearrange("(i p) n -> p i n", p=128)), 16)
            qG.inc(e.dma_start(out=ffb[:, 0:16], in_=w["ff1b2"]), 16)
            qG.inc(e.dma_start(out=ffb[:, 16:24], in_=w["ff2b2"]), 16)
        ff_tok = qG.n
        wff2_tok = [0, 0]
        wff2_rd = [0, 0]

        src4 = xin.rearrange("(i p) (n f) -> p i n f", p=128, f=CH)
        dst4 = xout.rearrange("(i p) (n f) -> p i n f", p=128, f=CH)

        for ci in range(NCH):
            @p.b.sync
            def _(e, ci=ci):
                if ci > 0:
                    wait(e, (qS, qS.n), (vt, vt.n))
                else:
                    wait(e, (tv, tv.n), (vs, vs.n), (sv, sv.n), (apt, apt.n))
                qS.inc(e.dma_start(
                    out=xc2[:].rearrange("p (i f) -> p i f", f=CH),
                    in_=src4[:, :, ci, :]), 16)
            x_tok = qS.n

            for ii in range(16):
                @p.b.tensor
                def _(e, ci=ci, ii=ii):
                    wait(e, (qG, ff_tok), (sv, hsil_tok[ii % 2]))
                    if ii == 0:
                        wait(e, (apt, apt.n))
                    for i in range(DT):
                        m = e.matmul(
                            out=pH2[ii % 2][:],
                            lhsT=wff1[:, i * 2048 + ii * 128:
                                      i * 2048 + (ii + 1) * 128],
                            rhs=aT[:, i * T + ci * CH: i * T + (ci + 1) * CH],
                            start=(i == 0), stop=(i == DT - 1))
                    tv.inc(m)

                @p.b.scalar
                def _(e, ci=ci, ii=ii):
                    wait(e, (tv, tv.n), (qG, ff_tok))
                    sv.inc(e.activation(out=hT[:, ii * CH:(ii + 1) * CH],
                                        in_=pH2[ii % 2][:], func=AF.Relu,
                                        bias=ffb[:, ii:ii + 1], scale=1.0))
                hsil_tok[ii % 2] = p.sem("sv").n

            for jd in range(DT):
                jp = jd % 2

                @p.b.gpsimd
                def _(e, ci=ci, jd=jd, jp=jp):
                    import os as _o
                    if _o.environ.get("L0A_SERW"):
                        wait(e, (tv, tv.n))
                    elif wff2_rd[jp]:
                        wait(e, (tv, wff2_rd[jp]))
                    qG.inc(e.dma_start(
                        out=wff2[jp][:],
                        in_=w["ff2wp"][jd]), 16)
                wff2_tok[jp] = qG.n

                @p.b.tensor
                def _(e, ci=ci, jd=jd, jp=jp, wtok=wff2_tok[jp]):
                    wait(e, (sv, sv.n))
                    wait(e, (vs, fev_tok[jd % 2]))
                    wait(e, (qG, wtok))
                    for ii in range(16):
                        m = e.matmul(
                            out=pF2[jd % 2][:],
                            lhsT=wff2[jp][:, ii * 128:(ii + 1) * 128],
                            rhs=hT[:, ii * CH:(ii + 1) * CH],
                            start=(ii == 0), stop=(ii == 15))
                    tv.inc(m)
                wff2_rd[jp] = p.sem("tv").n

                @p.b.vector
                def _(e, ci=ci, jd=jd, x_tok=x_tok):
                    wait(e, (tv, tv.n), (qS, x_tok))
                    vs.inc(e.scalar_tensor_tensor(
                        out=zA[:, jd * CH:(jd + 1) * CH], in0=pF2[jd % 2][:],
                        scalar=ffb[:, 16 + jd:17 + jd],
                        in1=aT[:, jd * T + ci * CH: jd * T + (ci + 1) * CH],
                        op0=ALU.add, op1=ALU.add))
                fev_tok[jd % 2] = p.sem("vs").n

            if _os.environ.get("L0A_D2FF"):
                if _os.environ.get("L0A_DUMPH") and ci == NCH - 1:
                    @p.b.gpsimd
                    def _(e, ci=ci):
                        wait(e, (sv, sv.n))
                        qG.inc(e.dma_start(out=xout[0:128, :],
                                           in_=hT[:, 0:T]), 16)
                        qG.inc(e.dma_start(out=xout[128:256, :],
                                           in_=aT[:, 0:T]), 16)
                    continue

                @p.b.sync
                def _(e, ci=ci):
                    wait(e, (vs, vs.n))
                    qS.inc(e.dma_start(out=dst4[:, :, ci, :],
                                       in_=zA[:].rearrange("p (i f) -> p i f",
                                                           f=CH)), 16)
                continue
            emit_ln_chunk(p, c, banks,
                          lambda i: zA[:, i * CH:(i + 1) * CH],
                          lambda i: zB[:, i * CH:(i + 1) * CH],
                          gb=(ln2g, ln2b))
            emit_ln_chunk(p, c, banks,
                          lambda i: zB[:, i * CH:(i + 1) * CH],
                          lambda i: zB[:, i * CH:(i + 1) * CH],
                          gb=None)

            @p.b.vector
            def _(e, ci=ci, x_tok=x_tok):
                wait(e, (qS, x_tok), (apt, apt.n))
                for i in range(DT):
                    m = e.scalar_tensor_tensor(out=zA[:, i * CH:(i + 1) * CH],
                                               in0=zB[:, i * CH:(i + 1) * CH],
                                               scalar=1.0,
                                               in1=xc2[:, i * CH:(i + 1) * CH],
                                               op0=ALU.mult, op1=ALU.add)
                vt.inc(m)

            @p.b.sync
            def _(e, ci=ci):
                wait(e, (vt, vt.n))
                qS.inc(e.dma_start(out=dst4[:, :, ci, :],
                                   in_=zA[:].rearrange("p (i f) -> p i f", f=CH)),
                       16)

    @p.b.sync
    def _(e):
        wait(e, (qS, qS.n))



def _bias_cols(b, ncols):
    return np.ascontiguousarray(np.asarray(b, dtype=np.float32)
                                .reshape(ncols, 128).T)


def pack_l0a(inp, pref="L0_attn_"):
    g = lambda n: np.asarray(inp[pref + n], dtype=np.float32)
    w = {}
    for nm, key in (("wq", "q_proj"), ("wk", "k_proj"), ("wv", "v_proj")):
        w[nm] = np.ascontiguousarray(g(key).transpose(1, 0, 2).reshape(D, D))
    for nm, key in (("qb2", "q_bias"), ("kb2", "k_bias"), ("vb2", "v_bias")):
        w[nm] = np.ascontiguousarray(
            g(key).reshape(8, 2, 64).transpose(1, 2, 0).reshape(128, 8))
    for nm, key in (("ln1g", "ln1_g"), ("ln1b", "ln1_b"),
                    ("ln2g", "ln2_g"), ("ln2b", "ln2_b")):
        w[nm] = g(key).reshape(1, D)
    w["ff1w"] = g("ff1_w")
    w["ff2wp"] = np.ascontiguousarray(
        g("ff2_w").reshape(16, 128, 8, 128).transpose(2, 1, 0, 3)
        .reshape(8, 128, 16 * 128))
    w["ff1b2"] = _bias_cols(g("ff1_b"), 16)
    w["ff2b2"] = _bias_cols(g("ff2_b"), 8)
    causal = np.tril(np.ones((128, 128), dtype=np.float32))
    gen = np.concatenate([np.ones((128, 128), np.float32), causal], axis=1)
    q0 = np.concatenate([causal, np.zeros((128, 128), np.float32)], axis=1)
    w["mask4_g"] = np.concatenate([gen] * 4, axis=1).astype(ml_dtypes.bfloat16)
    w["mask4_0"] = np.concatenate([q0, gen, gen, gen],
                                  axis=1).astype(ml_dtypes.bfloat16)
    return w


L0A_SHAPES = {
    "wq": ([D, D], F32), "wk": ([D, D], F32), "wv": ([D, D], F32),
    "qb2": ([128, 8], F32), "kb2": ([128, 8], F32), "vb2": ([128, 8], F32),
    "ln1g": ([1, D], F32), "ln1b": ([1, D], F32),
    "ln2g": ([1, D], F32), "ln2b": ([1, D], F32),
    "ff1w": ([D, 2048], F32), "ff2wp": ([8, 128, 2048], F32),
    "ff1b2": ([128, 16], F32), "ff2b2": ([128, 8], F32),
    "mask4_0": ([128, 1024], BF16), "mask4_g": ([128, 1024], BF16),
}


def declare_weights(nc, shapes, prefix=""):
    return {k: nc.dram_tensor(prefix + k, sh, dt, kind="ExternalInput").ap()
            for k, (sh, dt) in shapes.items()}


def build_l0a_test(T, phases=("A", "BC", "D1", "D2")):
    nc = new_bass()
    xin = nc.dram_tensor("xin", [D, T], F32, kind="ExternalInput").ap()
    cident = nc.dram_tensor("cident", [128, 128], BF16, kind="ExternalInput").ap()
    xo = nc.dram_tensor("xout", [D, T], F32, kind="ExternalOutput").ap()
    w = declare_weights(nc, L0A_SHAPES)
    with ExitStack() as st:
        block = st.enter_context(nc.Block())
        p = P(nc, block, st)
        c = emit_consts(p, cident)
        emit_l0_attn(p, c, T, xin, xo, w, phases=phases)
    return nc



def emit_sandwich_tail(p, c, T, blk_hbm, xin, xout, tag):
    nc = p.nc
    NCH = T // CH
    qS = p.sem("qS")
    vt = p.sem("vt")
    apt = p.sem("apt")
    with ExitStack() as ph:
        banks = make_ln_banks(p, ph, tag=tag + "_ln")
        bc = [ph.enter_context(nc.sbuf_tensor(tag + "_bc%d" % i, [128, DT * CH], F32))
              for i in range(2)]
        xc = ph.enter_context(nc.sbuf_tensor(tag + "_xc", [128, DT * CH], F32))
        zB = ph.enter_context(nc.sbuf_tensor(tag + "_zB", [128, DT * CH], F32))
        blk4 = blk_hbm.rearrange("(i p) (n f) -> p i n f", p=128, f=CH)
        src4 = xin.rearrange("(i p) (n f) -> p i n f", p=128, f=CH)
        dst4 = xout.rearrange("(i p) (n f) -> p i n f", p=128, f=CH)
        qG = p.sem("qG")
        tv = p.sem("tv")
        vs = p.sem("vs")
        sv = p.sem("sv")
        for ci in range(NCH):
            @p.b.sync
            def _(e, ci=ci):
                if ci < 2:
                    wait(e, (qG, qG.n), (tv, tv.n), (vs, vs.n), (sv, sv.n),
                         (apt, apt.n))
                else:
                    wait(e, (apt, apt.n))
                qS.inc(e.dma_start(
                    out=bc[ci % 2][:].rearrange("p (i f) -> p i f", f=CH),
                    in_=blk4[:, :, ci, :]), 16)
                if ci > 0:
                    wait(e, (vt, vt.n))
                qS.inc(e.dma_start(
                    out=xc[:].rearrange("p (i f) -> p i f", f=CH),
                    in_=src4[:, :, ci, :]), 16)
            ld_tok = qS.n

            @p.b.vector
            def _(e, tok=ld_tok):
                wait(e, (qS, tok))
            emit_ln_chunk(p, c, banks,
                          lambda i, ci=ci: bc[ci % 2][:, i * CH:(i + 1) * CH],
                          lambda i: zB[:, i * CH:(i + 1) * CH], gb=None)

            @p.b.vector
            def _(e, ci=ci):
                wait(e, (apt, apt.n))
                for i in range(DT):
                    m = e.scalar_tensor_tensor(
                        out=zB[:, i * CH:(i + 1) * CH],
                        in0=zB[:, i * CH:(i + 1) * CH], scalar=1.0,
                        in1=xc[:, i * CH:(i + 1) * CH],
                        op0=ALU.mult, op1=ALU.add)
                vt.inc(m)

            @p.b.sync
            def _(e, ci=ci):
                wait(e, (vt, vt.n))
                qS.inc(e.dma_start(out=dst4[:, :, ci, :],
                                   in_=zB[:].rearrange("p (i f) -> p i f", f=CH)),
                       16)
    @p.b.sync
    def _(e):
        wait(e, (qS, qS.n))



def emit_l0_ffn(p, c, T, xin, xout, w, macc):
    nc = p.nc
    NCH = T // CH
    NQ = T // 128
    qS = p.sem("qS")
    qG = p.sem("qG")
    vt = p.sem("vt")
    tv = p.sem("tv")
    vs = p.sem("vs")
    sv = p.sem("sv")
    apt = p.sem("apt")

    xfT = p.sbuf("l0f_xfT", [128, DT * T], BF16)
    wT = p.sbuf("l0f_wT", [8, T], BF16)
    bia = p.sbuf("l0f_bia", [128, 256 + 32 + 2 + 8], F32)
    be1c = bia[:, 0:256]
    bs1c = bia[:, 256:288]
    br1c = bia[:, 288:290]
    bs2c = bia[:, 290:298]
    br2c = p.sbuf("l0f_br2c", [8, 1], F32)
    be2a = p.sbuf("l0f_be2a", [8, 1024], BF16)
    csel8 = p.sbuf("l0f_csel8", [8, 8 * 128], BF16)

    @p.b.gpsimd
    def _(e):
        qG.inc(e.dma_start(out=be1c, in_=w["be1c"]), 16)
        qG.inc(e.dma_start(out=bs1c, in_=w["bs1c"]), 16)
        qG.inc(e.dma_start(out=br1c, in_=w["br1c"]), 16)
        qG.inc(e.dma_start(out=bs2c, in_=w["bs2c"]), 16)
        qG.inc(e.dma_start(out=br2c[:], in_=w["br2c"]), 16)
        qG.inc(e.dma_start(out=be2a[:], in_=w["be2a"]), 16)
        qG.inc(e.dma_start(out=csel8[:], in_=w["csel8"]), 16)
    smalls_tok = qG.n

    with ExitStack() as ph:
        banks = make_ln_banks(p, ph, tag="l0f_ln")
        xc = [ph.enter_context(nc.sbuf_tensor("l0f_xc%d" % i, [128, DT * CH], F32))
              for i in range(2)]
        emit_preln(p, c, T, xin, xfT, banks, gb=None, xc=xc)
    xf_tok = p.sem("apt").n

    with ExitStack() as ph:
        ec = ph.enter_context
        wr1f = ec(nc.sbuf_tensor("l0f_wr1f", [128, DT * 256], F32))
        wr2f = ec(nc.sbuf_tensor("l0f_wr2f", [128, 16], F32))
        xc3 = [ec(nc.sbuf_tensor("l0f_xc3%d" % i, [128, DT * CH], F32))
               for i in range(2)]
        xf32 = ec(nc.sbuf_tensor("l0f_xf32", [128, DT * CH], F32))
        globals()["_dbg_xf32"] = xf32
        r1c = ec(nc.sbuf_tensor("l0f_r1c", [128, 2 * CH], F32))
        globals()["_dbg_r1c"] = r1c
        lgT = ec(nc.sbuf_tensor("l0f_lgT", [8, T], F32))
        lgt = ec(nc.sbuf_tensor("l0f_lgt", [128, NQ * 8], F32))
        globals()['_dbg_lgt'] = lgt
        gsc = ec(nc.sbuf_tensor("l0f_gsc", [128, 40], F32))
        gates = gsc[:, 0:8]
        top8 = gsc[:, 8:16]
        ind1 = gsc[:, 16:24]
        ind2 = gsc[:, 24:32]
        nm = gsc[:, 32:40]
        wtok = ec(nc.sbuf_tensor("l0f_wtok", [128, 8], BF16))
        banksf = {
            "ps1": ec(nc.psum_tensor("l0f_fps1", [1, CH], F32)),
            "ps2": ec(nc.psum_tensor("l0f_fps2", [1, CH], F32)),
            "pA": ec(nc.psum_tensor("l0f_fpA", [128, CH], F32)),
            "pB": ec(nc.psum_tensor("l0f_fpB", [128, CH], F32)),
            "xcb": ec(nc.sbuf_tensor("l0f_fxcb", [128, DT * CH], F32)),
            "sq": ec(nc.sbuf_tensor("l0f_fsq", [128, DT * CH], F32)),
            "rows": ec(nc.sbuf_tensor("l0f_frows", [1, 7 * CH], F32)),
            "rowsb": ec(nc.sbuf_tensor("l0f_frowsb", [1, 2 * CH], BF16)),
            "tmp": ec(nc.sbuf_tensor("l0f_ftmp", [128, CH], F32)),
        }
        pR = ec(nc.psum_tensor("l0f_pR", [128, CH], F32))
        pL = ec(nc.psum_tensor("l0f_pL", [8, CH], F32))
        pX = ec(nc.psum_tensor("l0f_pX", [128, 8], F32))
        pXb = ec(nc.psum_tensor("l0f_pXb", [8, 128], BF16))

        @p.b.sync
        def _(e):
            wait(e, (apt, xf_tok))
            qS.inc(e.dma_start(out=wr1f[:].rearrange("p (i n) -> p i n", n=256),
                               in_=w["Wr1"].rearrange("(i p) n -> p i n", p=128)),
                   16)
            qS.inc(e.dma_start(out=wr2f[:].rearrange("p (i n) -> p i n", n=8),
                               in_=w["Wr2"].rearrange("(i p) n -> p i n", p=128)),
                   16)
        r_tok = qS.n

        src4r = xin.rearrange("(i p) (n f) -> p i n f", p=128, f=CH)
        for ci in range(NCH):
            @p.b.sync
            def _(e, ci=ci):
                if ci >= 2:
                    wait(e, (tv, tv.n))
                else:
                    wait(e, (apt, xf_tok))
                qS.inc(e.dma_start(
                    out=xc3[ci % 2][:].rearrange("p (i f) -> p i f", f=CH),
                    in_=src4r[:, :, ci, :]), 16)
            ld_tok = qS.n

            @p.b.vector
            def _(e, tok=ld_tok, ci=ci):
                wait(e, (qS, tok))
                if ci > 0:
                    wait(e, (tv, tv.n))

            emit_ln_chunk(p, c, banksf,
                          lambda i, ci=ci: xc3[ci % 2][:, i * CH:(i + 1) * CH],
                          lambda i: xf32[:, i * CH:(i + 1) * CH],
                          gb=None, f32=True)

            for it in range(2):
                @p.b.tensor
                def _(e, it=it, ci=ci):
                    wait(e, (apt, apt.n), (qS, r_tok), (sv, sv.n))
                    for i in range(DT):
                        m = e.matmul(
                            out=pR[:],
                            lhsT=wr1f[:, i * 256 + it * 128:
                                      i * 256 + (it + 1) * 128],
                            rhs=xf32[:, i * CH:(i + 1) * CH],
                            start=(i == 0), stop=(i == DT - 1))
                    tv.inc(m)

                @p.b.scalar
                def _(e, it=it, ci=ci):
                    wait(e, (tv, tv.n), (qG, smalls_tok))
                    sv.inc(e.activation(out=r1c[:, it * CH:(it + 1) * CH],
                                        in_=pR[:], func=AF.Relu,
                                        bias=br1c[:, it:it + 1], scale=1.0))

            @p.b.tensor
            def _(e, ci=ci):
                wait(e, (sv, sv.n), (vs, vs.n))
                for it in range(2):
                    m = e.matmul(out=pL[:],
                                 lhsT=wr2f[:, it * 8:(it + 1) * 8],
                                 rhs=r1c[:, it * CH:(it + 1) * CH],
                                 start=(it == 0), stop=(it == 1))
                tv.inc(m)

            @p.b.vector
            def _(e, ci=ci):
                wait(e, (tv, tv.n))
                vs.inc(e.tensor_scalar(out=lgT[:, ci * CH:(ci + 1) * CH],
                                       in0=pL[:], scalar1=br2c[:, 0:1],
                                       scalar2=None, op0=ALU.add))

        for tt in range(NQ):
            @p.b.tensor
            def _(e, tt=tt):
                wait(e, (vs, vs.n))
                if tt >= 1:
                    wait(e, (apt, apt.n))
                tv.inc(e.transpose(out=pX[:], in_=lgT[:, tt * 128:(tt + 1) * 128],
                                   identity=c["identf"][0:8, 0:8]))

            @p.b.vector
            def _(e, tt=tt):
                wait(e, (tv, tv.n))
                apt.inc(e.tensor_copy(out=lgt[:, tt * 8:(tt + 1) * 8], in_=pX[:]))

        for tt in range(NQ):
            @p.b.vector
            def _(e, tt=tt):
                wait(e, (apt, apt.n))
                sl = lgt[:, tt * 8:(tt + 1) * 8]
                e.reduce_max(out=nm[:, 0:1], in_=sl, axis=AX, negate=True)
                vs.inc(e.drain())

            @p.b.scalar
            def _(e, tt=tt):
                wait(e, (vs, vs.n))
                sv.inc(e.activation(out=gates, in_=lgt[:, tt * 8:(tt + 1) * 8],
                                    func=AF.Exp, bias=nm[:, 0:1], scale=1.0))

            @p.b.vector
            def _(e, tt=tt):
                wait(e, (sv, sv.n))
                e.reduce_sum(out=nm[:, 1:2], in_=gates, axis=AX)
                e.drain()
                e.reciprocal(out=nm[:, 2:3], in_=nm[:, 1:2])
                vs.inc(e.drain())

            @p.b.vector
            def _(e, tt=tt):
                wait(e, (vs, vs.n))
                e.tensor_scalar(out=gates, in0=gates, scalar1=nm[:, 2:3],
                                scalar2=None, op0=ALU.mult)
                e.drain()
                e.max(out=top8, in_=gates)
                vs.inc(e.drain())

            @p.b.scalar
            def _(e, tt=tt):
                wait(e, (vs, vs.n))
                e.activation(out=nm[:, 3:4], in_=top8[:, 0:1], func=AF.Exp,
                             scale=0.5)
                sv.inc(e.activation(out=nm[:, 4:5], in_=top8[:, 1:2], func=AF.Exp,
                                    scale=0.5))

            @p.b.vector
            def _(e, tt=tt):
                wait(e, (sv, sv.n))
                e.tensor_add(out=nm[:, 5:6], in0=nm[:, 3:4], in1=nm[:, 4:5])
                e.drain()
                e.reciprocal(out=nm[:, 6:7], in_=nm[:, 5:6])
                vs.inc(e.drain())

            @p.b.vector
            def _(e, tt=tt):
                wait(e, (vs, vs.n))
                e.tensor_scalar(out=nm[:, 3:4], in0=nm[:, 3:4],
                                scalar1=nm[:, 6:7], scalar2=None, op0=ALU.mult)
                e.tensor_scalar(out=nm[:, 4:5], in0=nm[:, 4:5],
                                scalar1=nm[:, 6:7], scalar2=None, op0=ALU.mult)
                e.tensor_scalar(out=ind1, in0=gates, scalar1=top8[:, 0:1],
                                scalar2=None, op0=ALU.is_equal)
                e.tensor_scalar(out=ind2, in0=gates, scalar1=top8[:, 1:2],
                                scalar2=None, op0=ALU.is_equal)
                e.scalar_tensor_tensor(out=ind2, in0=ind1, scalar=-1.0, in1=ind2,
                                       op0=ALU.mult, op1=ALU.add)
                e.tensor_scalar_max(ind2, ind2, 0.0)
                vs.inc(e.drain())

            @p.b.vector
            def _(e, tt=tt):
                wait(e, (vs, vs.n))
                e.tensor_scalar(out=ind2, in0=ind2, scalar1=nm[:, 4:5],
                                scalar2=None, op0=ALU.mult)
                e.scalar_tensor_tensor(out=wtok[:], in0=ind1, scalar=nm[:, 3:4],
                                       in1=ind2, op0=ALU.mult, op1=ALU.add)
                vs.inc(e.drain())

            @p.b.tensor
            def _(e, tt=tt):
                wait(e, (vs, vs.n))
                if tt >= 1:
                    wait(e, (apt, apt.n))
                tv.inc(e.transpose(out=pXb[0:8, 0:128], in_=wtok[:],
                                   identity=c["identity"][:]))

            @p.b.vector
            def _(e, tt=tt):
                wait(e, (tv, tv.n))
                apt.inc(e.tensor_copy(out=wT[0:8, tt * 128:(tt + 1) * 128],
                                      in_=pXb[0:8, 0:128]))

        import os as _os
        if _os.environ.get("L0F_DBGR"):
            dbg = nc.dram_tensor("dbg", [512, 64], F32,
                                 kind="ExternalOutput").ap()

            @p.b.gpsimd
            def _(e):
                wait(e, (apt, apt.n), (tv, tv.n), (vs, vs.n), (sv, sv.n))
                qG.inc(e.dma_start(out=dbg[0:128, 0:NQ * 8], in_=lgt[:]), 16)
                qG.inc(e.dma_start(out=dbg[128:136, 0:NQ * 8],
                                   in_=wT[0:8, 0:NQ * 8]), 16)
                qG.inc(e.dma_start(out=dbg[256:384, 0:40], in_=gsc[:]), 16)
                qG.inc(e.dma_start(out=dbg[384:512, 0:8], in_=wtok[:]), 16)

    with ExitStack() as ph:
        ec = ph.enter_context
        wE1 = [ec(nc.sbuf_tensor("l0f_wE1_%d" % i, [128, DT * 1024], BF16))
               for i in range(2)]
        wE2 = [ec(nc.sbuf_tensor("l0f_wE2_%d" % i, [128, DT * 1024], BF16))
               for i in range(2)]
        hq = ec(nc.sbuf_tensor("l0f_hq", [128, DT * CH], BF16))
        stage = [ec(nc.sbuf_tensor("l0f_st%d" % i, [128, DT * CH], F32))
                 for i in range(2)]
        wbc = ec(nc.sbuf_tensor("l0f_wbc", [128, T], BF16))
        pH = [ec(nc.psum_tensor("l0f_pH%d" % i, [128, CH], F32)) for i in range(4)]
        pF2 = [ec(nc.psum_tensor("l0f_pF%d" % i, [128, CH], F32)) for i in range(2)]
        pW = ec(nc.psum_tensor("l0f_pW", [128, CH], F32))
        silu_tok = [0, 0]
        evac_tok = [0, 0]

        macc4 = macc.rearrange("(i p) (n f) -> p i n f", p=128, f=CH)
        region_tok = [0] * NCH
        step_tok = []
        stage_tok = [0, 0]

        for ee in range(9):
            is_sh = (ee == 0)
            exp_i = ee - 1

            if not is_sh:
                for ci in range(NCH):
                    @p.b.tensor
                    def _(e, ci=ci, exp_i=exp_i):
                        wait(e, (apt, apt.n), (vs, vs.n))
                        tv.inc(e.matmul(out=pW[:],
                                        lhsT=csel8[:, exp_i * 128:
                                                   (exp_i + 1) * 128],
                                        rhs=wT[0:8, ci * CH:(ci + 1) * CH],
                                        start=True, stop=True))

                    @p.b.vector
                    def _(e, ci=ci):
                        wait(e, (tv, tv.n))
                        apt.inc(e.tensor_copy(out=wbc[:, ci * CH:(ci + 1) * CH],
                                              in_=pW[:]))

            for Iq in range(4):
                step = ee * 4 + Iq
                par = step % 2

                @p.b.gpsimd
                def _(e, Iq=Iq, par=par, is_sh=is_sh, exp_i=exp_i, step=step):
                    if step >= 2:
                        wait(e, (tv, step_tok[step - 2]))
                    else:
                        wait(e, (apt, apt.n), (tv, tv.n), (vs, vs.n), (sv, sv.n))
                    if is_sh:
                        s1 = w["Ws1"].rearrange("(i p) n -> p i n", p=128)[
                            :, :, Iq * 1024:(Iq + 1) * 1024]
                        s2 = w["Ws2"][Iq * 1024:(Iq + 1) * 1024].rearrange(
                            "(i p) n -> p i n", p=128)
                    else:
                        s1 = w["We1"][exp_i].rearrange("(i p) n -> p i n", p=128)[
                            :, :, Iq * 1024:(Iq + 1) * 1024]
                        s2 = w["We2"][exp_i][Iq * 1024:(Iq + 1) * 1024].rearrange(
                            "(i p) n -> p i n", p=128)
                    qG.inc(e.dma_start(
                        out=wE1[par][:].rearrange("p (i n) -> p i n", n=1024),
                        in_=s1), 16)
                    qG.inc(e.dma_start(
                        out=wE2[par][:].rearrange("p (i n) -> p i n", n=1024),
                        in_=s2), 16)
                w_tok = qG.n

                for tc in range(NCH):
                    spar = (step * NCH + tc) % 2
                    for half in range(2):
                        @p.b.tensor
                        def _(e, tc=tc, half=half, par=par, w_tok=w_tok,
                              step=step):
                            wait(e, (qG, w_tok), (sv, silu_tok[0]))
                            if step == 0 and tc == 0 and half == 0:
                                wait(e, (apt, apt.n), (vs, vs.n))
                            for ii in range(half * 4, half * 4 + 4):
                                for i in range(DT):
                                    m = e.matmul(
                                        out=pH[ii % 4][:],
                                        lhsT=wE1[par][:, i * 1024 + ii * 128:
                                                      i * 1024 + (ii + 1) * 128],
                                        rhs=xfT[:, i * T + tc * CH:
                                                i * T + (tc + 1) * CH],
                                        start=(i == 0), stop=(i == DT - 1))
                                tv.inc(m)

                        @p.b.scalar
                        def _(e, Iq=Iq, half=half, is_sh=is_sh, exp_i=exp_i,
                              base_tv=p.sem("tv").n):
                            wait(e, (qG, smalls_tok))
                            for k, ii in enumerate(range(half * 4, half * 4 + 4)):
                                wait(e, (tv, base_tv - 3 + k))
                                if is_sh:
                                    bcol = bs1c[:, Iq * 8 + ii:Iq * 8 + ii + 1]
                                else:
                                    bcol = be1c[:, exp_i * 32 + Iq * 8 + ii:
                                                exp_i * 32 + Iq * 8 + ii + 1]
                                sv.inc(e.activation(
                                    out=hq[:, ii * CH:(ii + 1) * CH],
                                    in_=pH[ii % 4][:], func=AF.Silu,
                                    bias=bcol, scale=1.0))
                        silu_tok[0] = p.sem("sv").n

                    for jp2 in range(4):
                        extra = is_sh and Iq == 0

                        @p.b.tensor
                        def _(e, tc=tc, jp2=jp2, par=par, extra=extra):
                            wait(e, (sv, sv.n), (vs, evac_tok[0]))
                            for jd in (jp2 * 2, jp2 * 2 + 1):
                                pf_ = pF2[jd % 2]
                                for ii in range(8):
                                    m = e.matmul(
                                        out=pf_[:],
                                        lhsT=wE2[par][:, ii * 1024 + jd * 128:
                                                      ii * 1024 + (jd + 1) * 128],
                                        rhs=hq[:, ii * CH:(ii + 1) * CH],
                                        start=(ii == 0),
                                        stop=(ii == 7 and not extra))
                                if extra:
                                    m = e.matmul(
                                        out=pf_[:],
                                        lhsT=be2a[:, jd * 128:(jd + 1) * 128],
                                        rhs=wT[:, tc * CH:(tc + 1) * CH],
                                        start=False, stop=True)
                                tv.inc(m)

                        @p.b.vector
                        def _(e, tc=tc, jp2=jp2, spar=spar, is_sh=is_sh, Iq=Iq,
                              base_tv=p.sem("tv").n):
                            wait(e, (qG, stage_tok[spar]))
                            for k, jd in enumerate((jp2 * 2, jp2 * 2 + 1)):
                                wait(e, (tv, base_tv - 1 + k))
                                pf_ = pF2[jd % 2]
                                dst = stage[spar][:, jd * CH:(jd + 1) * CH]
                                if is_sh and Iq == 0:
                                    m = e.tensor_scalar(
                                        out=dst, in0=pf_[:],
                                        scalar1=bs2c[:, jd:jd + 1], scalar2=None,
                                        op0=ALU.add)
                                elif is_sh:
                                    m = e.tensor_copy(out=dst, in_=pf_[:])
                                else:
                                    m = e.tensor_tensor(
                                        out=dst, in0=wbc[:, tc * CH:(tc + 1) * CH],
                                        in1=pf_[:], op=ALU.mult)
                                vs.inc(m)
                        evac_tok[0] = p.sem("vs").n

                    @p.b.gpsimd
                    def _(e, tc=tc, spar=spar, step=step):
                        wait(e, (vs, vs.n), (qG, region_tok[tc]))
                        qG.inc(e.dma_start(
                            out=macc4[:, :, tc, :],
                            in_=stage[spar][:].rearrange("p (i f) -> p i f", f=CH),
                            accum_op=(ALU.bypass if step == 0 else ALU.add)), 16)
                    region_tok[tc] = qG.n
                    stage_tok[spar] = qG.n
                step_tok.append(p.sem("tv").n)

        @p.b.gpsimd
        def _(e):
            wait(e, (qG, qG.n))

    import os
    if os.environ.get("L0F_DUMPM"):
        @p.b.gpsimd
        def _(e):
            wait(e, (qG, qG.n), (tv, tv.n), (vs, vs.n), (sv, sv.n), (apt, apt.n))
            qG.inc(e.dma_start(out=xout, in_=macc), 16)

        @p.b.gpsimd
        def _(e):
            wait(e, (qG, qG.n))
        return
    if os.environ.get("L0F_DUMPXF"):
        @p.b.gpsimd
        def _(e):
            wait(e, (qG, qG.n), (tv, tv.n), (vs, vs.n), (sv, sv.n), (apt, apt.n))
            qG.inc(e.dma_start(out=xout[0:128, 0:CH], in_=_dbg_xf32[:, 0:CH]), 16)
            qG.inc(e.dma_start(out=xout[128:256, 0:CH], in_=_dbg_r1c[:, 0:CH]), 16)

        @p.b.gpsimd
        def _(e):
            wait(e, (qG, qG.n))
        return
    if os.environ.get("L0F_DUMPL"):
        @p.b.gpsimd
        def _(e):
            wait(e, (qG, qG.n), (tv, tv.n), (vs, vs.n), (sv, sv.n), (apt, apt.n))
            qG.inc(e.dma_start(out=xout[0:128, 0:NQ * 8], in_=_dbg_lgt[:]), 16)

        @p.b.gpsimd
        def _(e):
            wait(e, (qG, qG.n))
        return
    if os.environ.get("L0F_DUMPW"):
        @p.b.gpsimd
        def _(e):
            wait(e, (qG, qG.n), (tv, tv.n), (vs, vs.n), (sv, sv.n), (apt, apt.n))
            qG.inc(e.dma_start(out=xout[0:8, :], in_=wT[:]), 16)

        @p.b.gpsimd
        def _(e):
            wait(e, (qG, qG.n))
        return
    emit_sandwich_tail(p, c, T, macc, xin, xout, tag="l0f_tail")


L0F_SHAPES = {
    "Wr1": ([D, 256], F32), "br1c": ([128, 2], F32),
    "Wr2": ([256, 8], F32), "br2c": ([8, 1], F32),
    "Ws1": ([D, 4096], F32), "bs1c": ([128, 32], F32),
    "Ws2": ([4096, D], F32), "bs2c": ([128, 8], F32),
    "We1": ([8, D, 4096], F32), "be1c": ([128, 256], F32),
    "We2": ([8, 4096, D], F32), "be2a": ([8, D], F32),
    "csel8": ([8, 8 * 128], BF16),
}


def pack_l0f(inp, pref="L0_ffn_"):
    g = lambda n: np.asarray(inp[pref + n], dtype=np.float32)
    w = {}
    w["Wr1"] = g("Wr1")
    w["br1c"] = _bias_cols(g("br1"), 2)
    w["Wr2"] = g("Wr2")
    w["br2c"] = g("br2").reshape(8, 1)
    w["Ws1"] = g("Ws1")
    w["bs1c"] = _bias_cols(g("bs1"), 32)
    w["Ws2"] = g("Ws2")
    w["bs2c"] = _bias_cols(g("bs2"), 8)
    w["We1"] = g("We1")
    w["be1c"] = np.ascontiguousarray(
        g("be1").reshape(8, 32, 128).transpose(2, 0, 1).reshape(128, 256))
    w["We2"] = g("We2")
    w["be2a"] = g("be2")
    sel = np.zeros((8, 8, 128), dtype=np.float32)
    for e in range(8):
        sel[e, e, :] = 1.0
    w["csel8"] = sel.reshape(8, 8 * 128).astype(ml_dtypes.bfloat16)
    return w


def build_l0f_test(T):
    nc = new_bass()
    xin = nc.dram_tensor("xin", [D, T], F32, kind="ExternalInput").ap()
    cident = nc.dram_tensor("cident", [128, 128], BF16, kind="ExternalInput").ap()
    cidentf = nc.dram_tensor("cidentf", [128, 128], F32, kind="ExternalInput").ap()
    xo = nc.dram_tensor("xout", [D, T], F32, kind="ExternalOutput").ap()
    macc = nc.dram_tensor("macc", [D, T], F32, kind="Internal").ap()
    w = declare_weights(nc, L0F_SHAPES)
    with ExitStack() as st:
        block = st.enter_context(nc.Block())
        p = P(nc, block, st)
        c = emit_consts(p, cident)
        c["identf"] = p.sbuf("c_identf", [128, 128], F32)
        qS = p.sem("qS")

        @p.b.sync
        def _(e):
            qS.inc(e.dma_start(out=c["identf"][:], in_=cidentf), 16)

        emit_l0_ffn(p, c, T, xin, xo, w, macc)
    return nc



def emit_l1_attn(p, c, T, xin, xout, w, macc):
    nc = p.nc
    NCH = T // CH
    NQ = T // 128
    qS = p.sem("qS")
    qG = p.sem("qG")
    vt = p.sem("vt")
    tv = p.sem("tv")
    vs = p.sem("vs")
    sv = p.sem("sv")
    apt = p.sem("apt")

    hlf65 = p.sbuf("l1a_hlf65", [128, 65], BF16)
    rf2 = p.sbuf("l1a_rf2", [128, 64], BF16)
    boutc = p.sbuf("l1a_boutc", [128, 8], F32)
    ph_ao = ExitStack()
    aoT = ph_ao.enter_context(nc.sbuf_tensor("l1a_aoT", [128, DT * T], BF16))
    ph_xf = ExitStack()
    xfT = ph_xf.enter_context(nc.sbuf_tensor("l1a_xfT", [128, DT * T], BF16))

    @p.b.gpsimd
    def _(e):
        qG.inc(e.dma_start(out=rf2[0:64, :], in_=w["rfT"]), 16)
        qG.inc(e.dma_start(out=rf2[64:128, :], in_=w["rfT"]), 16)
        qG.inc(e.dma_start(out=boutc[:], in_=w["boutc"]), 16)
    smalls_tok = qG.n

    @p.b.vector
    def _(e):
        e.memset(hlf65[:], 0.0)
        e.memset(hlf65[0:64, 0:1], 1.0)
        vt.inc(e.memset(hlf65[64:128, 64:65], 1.0))

    with ExitStack() as ph:
        banks = make_ln_banks(p, ph, tag="l1a_ln")
        xc = [ph.enter_context(nc.sbuf_tensor("l1a_xc%d" % i, [128, DT * CH], F32))
              for i in range(2)]
        emit_preln(p, c, T, xin, xfT, banks, gb=None, xc=xc)
    xf_tok = p.sem("apt").n

    with ExitStack() as ph:
        ec = ph.enter_context
        wb = [ec(nc.sbuf_tensor("l1a_w%d" % i, [128, 3 * DT * 128], BF16))
              for i in range(2)]
        QTc = ec(nc.sbuf_tensor("l1a_QTc", [128, CH], BF16))
        KTc = ec(nc.sbuf_tensor("l1a_KTc", [128, CH], BF16))
        sqc = ec(nc.sbuf_tensor("l1a_sqc", [128, CH], BF16))
        pjk = ec(nc.sbuf_tensor("l1a_pjk", [128, CH], BF16))
        Vtok = ec(nc.sbuf_tensor("l1a_Vtok", [128, NQ * 130], BF16))
        ephiq = ec(nc.sbuf_tensor("l1a_ephiq", [128, T], BF16))
        ektok = ec(nc.sbuf_tensor("l1a_ektok", [128, NQ * 128], BF16))
        aout = ec(nc.sbuf_tensor("l1a_aout", [128, NQ * 128], BF16))
        nrow = [ec(nc.sbuf_tensor("l1a_nrow%d" % i, [1, 4 * CH], F32))
                for i in range(2)]
        ncols = ec(nc.sbuf_tensor("l1a_ncols", [128, 4 * NQ], F32))
        kvs = ec(nc.sbuf_tensor("l1a_kvs", [128, 65], BF16))
        dcol = ec(nc.sbuf_tensor("l1a_dcol", [128, 4], F32))
        ones11f = ec(nc.sbuf_tensor("l1a_ones11f", [1, 1], F32))
        pQ = ec(nc.psum_tensor("l1a_pQ", [128, CH], F32))
        pK = ec(nc.psum_tensor("l1a_pK", [128, CH], F32))
        pV = ec(nc.psum_tensor("l1a_pV", [128, CH], F32))
        pN = ec(nc.psum_tensor("l1a_pN", [128, CH], F32))
        pP = ec(nc.psum_tensor("l1a_pP", [128, CH], F32))
        pT_ = ec(nc.psum_tensor("l1a_pT", [128, 128], BF16))
        pA = ec(nc.psum_tensor("l1a_pA", [128, 65], F32))

        @p.b.vector
        def _(e):
            vt.inc(e.memset(ones11f[:], 1.0))

        for hp in range(8):
            @p.b.gpsimd
            def _(e, hp=hp):
                if hp < 2:
                    wait(e, (apt, xf_tok))
                else:
                    wait(e, (tv, tv.n))
                for j in range(3):
                    src = w["wqkvT"].rearrange("(i p) n -> p i n", p=128)[
                        :, :, slice(j * 1024 + hp * 128, j * 1024 + (hp + 1) * 128)]
                    qG.inc(e.dma_start(
                        out=wb[hp % 2][:, j * DT * 128:(j + 1) * DT * 128]
                        .rearrange("p (i n) -> p i n", n=128),
                        in_=src), 16)
            w_tok = qG.n

            @p.b.vector
            def _(e, hp=hp):
                wait(e, (tv, tv.n))
                vt.inc(e.memset(Vtok[:], 1.0))

            for ci in range(NCH):
                @p.b.tensor
                def _(e, hp=hp, ci=ci, w_tok=w_tok):
                    wait(e, (qG, w_tok), (apt, xf_tok), (vt, vt.n),
                         (sv, sv.n), (vs, vs.n))
                    for j, pdst in enumerate((pQ, pK, pV)):
                        for i in range(DT):
                            m = e.matmul(
                                out=pdst[:],
                                lhsT=wb[hp % 2][:, j * DT * 128 + i * 128:
                                                j * DT * 128 + (i + 1) * 128],
                                rhs=xfT[:, i * T + ci * CH: i * T + (ci + 1) * CH],
                                start=(i == 0), stop=(i == DT - 1))
                        tv.inc(m)

                @p.b.vector
                def _(e, ci=ci):
                    wait(e, (tv, tv.n))
                    e.tensor_copy(out=QTc[:], in_=pQ[:])
                    e.tensor_copy(out=KTc[:], in_=pK[:])
                    vs.inc(e.tensor_tensor(out=sqc[:], in0=QTc[:], in1=QTc[:],
                                           op=ALU.mult))

                @p.b.tensor
                def _(e, ci=ci):
                    wait(e, (vs, vs.n))
                    tv.inc(e.matmul(out=pN[0:65, :], lhsT=hlf65[:], rhs=sqc[:],
                                    start=True, stop=True))

                @p.b.vector
                def _(e, ci=ci):
                    wait(e, (tv, tv.n))
                    e.tensor_copy(out=nrow[0][:, 0:CH],
                                  in_=pN[0:1, :])
                    e.tensor_copy(out=nrow[1][:, 0:CH],
                                  in_=pN[64:65, :])
                    vs.inc(e.tensor_tensor(out=sqc[:], in0=KTc[:], in1=KTc[:],
                                           op=ALU.mult))

                @p.b.tensor
                def _(e, ci=ci):
                    wait(e, (vs, vs.n))
                    tv.inc(e.matmul(out=pN[0:65, :], lhsT=hlf65[:], rhs=sqc[:],
                                    start=True, stop=True))

                @p.b.vector
                def _(e, ci=ci):
                    wait(e, (tv, tv.n))
                    e.tensor_copy(out=nrow[0][:, 2 * CH:3 * CH],
                                  in_=pN[0:1, :])
                    vs.inc(e.tensor_copy(out=nrow[1][:, 2 * CH:3 * CH],
                                         in_=pN[64:65, :]))

                @p.b.scalar
                def _(e, ci=ci):
                    wait(e, (vs, vs.n))
                    for h2 in range(2):
                        m = e.activation(out=nrow[h2][:, CH:2 * CH],
                                         in_=nrow[h2][:, 0:CH], func=AF.Sqrt)
                        m = e.activation(out=nrow[h2][:, 3 * CH:4 * CH],
                                         in_=nrow[h2][:, 2 * CH:3 * CH],
                                         func=AF.Sqrt)
                    sv.inc(m)

                @p.b.vector
                def _(e, ci=ci):
                    wait(e, (sv, sv.n))
                    for h2 in range(2):
                        for sl in (CH, 3 * CH):
                            m = e.tensor_scalar(
                                out=nrow[h2][:, sl: sl + CH],
                                in0=nrow[h2][:, sl: sl + CH],
                                scalar1=-1.0, scalar2=None, op0=ALU.mult)
                    vs.inc(m)

                @p.b.tensor
                def _(e, ci=ci):
                    wait(e, (vs, vs.n), (apt, apt.n))
                    for s4 in range(4):
                        h2 = s4 % 2
                        sl = CH if s4 < 2 else 3 * CH
                        for st4 in range(CH // 128):
                            m = e.matmul(
                                out=pP[:, s4 * 4 + st4: s4 * 4 + st4 + 1],
                                lhsT=nrow[h2][:, sl + st4 * 128:
                                              sl + (st4 + 1) * 128],
                                rhs=ones11f[:], start=True, stop=True)
                    tv.inc(m)

                @p.b.vector
                def _(e, ci=ci):
                    wait(e, (tv, tv.n))
                    dst = ncols[:].rearrange("p (s t) -> p s t", t=NQ)[
                        :, :, ci * 4:(ci + 1) * 4]
                    vs.inc(e.tensor_copy(
                        out=dst, in_=pP[:, 0:16].rearrange("p (s t) -> p s t", t=4)))

                for h2 in range(2):
                    @p.b.tensor
                    def _(e, h2=h2, ci=ci):
                        wait(e, (vs, vs.n), (sv, sv.n))
                        hs = slice(h2 * 64, (h2 + 1) * 64)
                        tp = (64, 64) if h2 == 1 else None
                        tv.inc(e.matmul(out=pP[hs, 0:CH], lhsT=rf2[hs, :],
                                        rhs=QTc[hs, :], start=True, stop=True,
                                        tile_position=tp))

                    @p.b.scalar
                    def _(e, h2=h2, ci=ci):
                        wait(e, (tv, tv.n))
                        hs = slice(h2 * 64, (h2 + 1) * 64)
                        sv.inc(e.activation(out=ephiq[hs, ci * CH:(ci + 1) * CH],
                                            in_=pP[hs, 0:CH], func=AF.Exp,
                                            scale=0.125))

                for h2 in range(2):
                    @p.b.tensor
                    def _(e, h2=h2, ci=ci):
                        wait(e, (sv, sv.n), (vs, vs.n))
                        hs = slice(h2 * 64, (h2 + 1) * 64)
                        tp = (64, 64) if h2 == 1 else None
                        tv.inc(e.matmul(out=pP[hs, 0:CH], lhsT=rf2[hs, :],
                                        rhs=KTc[hs, :], start=True, stop=True,
                                        tile_position=tp))

                    @p.b.vector
                    def _(e, h2=h2, ci=ci):
                        wait(e, (tv, tv.n))
                        hs = slice(h2 * 64, (h2 + 1) * 64)
                        vs.inc(e.tensor_copy(out=pjk[hs, :], in_=pP[hs, 0:CH]))

                @p.b.vector
                def _(e, ci=ci):
                    wait(e, (tv, tv.n))
                    vs.inc(e.tensor_copy(out=QTc[:], in_=pV[:]))

                for st4 in range(CH // 128):
                    @p.b.tensor
                    def _(e, ci=ci, st4=st4):
                        wait(e, (vs, vs.n), (apt, apt.n), (sv, sv.n))
                        tv.inc(e.transpose(out=pT_[:],
                                           in_=QTc[:, st4 * 128:(st4 + 1) * 128],
                                           identity=c["identity"][:]))

                    @p.b.vector
                    def _(e, ci=ci, st4=st4):
                        wait(e, (tv, tv.n))
                        ti = ci * 4 + st4
                        dstv = Vtok[:, ti * 130: ti * 130 + 130].rearrange(
                            "p (h x) -> p h x", x=65)[:, :, 0:64]
                        apt.inc(e.tensor_copy(
                            out=dstv,
                            in_=pT_[:].rearrange("p (h x) -> p h x", x=64)))

                for h2 in range(2):
                    for st4 in range(CH // 128):
                        @p.b.tensor
                        def _(e, h2=h2, ci=ci, st4=st4):
                            wait(e, (sv, sv.n), (apt, apt.n))
                            hs = slice(h2 * 64, (h2 + 1) * 64)
                            tv.inc(e.transpose(
                                out=pT_[:, 0:64],
                                in_=pjk[hs, st4 * 128:(st4 + 1) * 128],
                                identity=c["identity"][hs, hs]))

                        @p.b.scalar
                        def _(e, h2=h2, ci=ci, st4=st4):
                            wait(e, (tv, tv.n), (vs, vs.n))
                            ti = ci * 4 + st4
                            sv.inc(e.activation(
                                out=ektok[:, ti * 128 + h2 * 64:
                                          ti * 128 + h2 * 64 + 64],
                                in_=pT_[:, 0:64], func=AF.Exp,
                                bias=ncols[:, (2 + h2) * NQ + ti:
                                           (2 + h2) * NQ + ti + 1],
                                scale=0.125))

            for h2 in range(2):
                @p.b.tensor
                def _(e, h2=h2, hp=hp):
                    wait(e, (sv, sv.n), (apt, apt.n), (vs, vs.n))
                    tp = (0, 64) if h2 == 1 else None
                    for ti in range(NQ):
                        m = e.matmul(
                            out=pA[h2 * 64:(h2 + 1) * 64, :],
                            lhsT=ektok[:, ti * 128 + h2 * 64:
                                       ti * 128 + h2 * 64 + 64],
                            rhs=Vtok[:, ti * 130 + h2 * 65:
                                     ti * 130 + (h2 + 1) * 65],
                            start=(ti == 0), stop=(ti == NQ - 1),
                            tile_position=tp)
                    tv.inc(m)

                @p.b.vector
                def _(e, h2=h2):
                    wait(e, (tv, tv.n))
                    hs = slice(h2 * 64, (h2 + 1) * 64)
                    vs.inc(e.tensor_copy(out=kvs[hs, :], in_=pA[hs, :]))

            for ti in range(NQ):
                for h2 in range(2):
                    hs = slice(h2 * 64, (h2 + 1) * 64)

                    @p.b.tensor
                    def _(e, h2=h2, ti=ti, hs=hs):
                        wait(e, (vs, vs.n))
                        tv.inc(e.matmul(out=pA[0:128, :],
                                        lhsT=ephiq[hs, ti * 128:(ti + 1) * 128],
                                        rhs=kvs[hs, :], start=True, stop=True))

                    @p.b.scalar
                    def _(e, h2=h2, ti=ti):
                        wait(e, (tv, tv.n), (vs, vs.n))
                        sv.inc(e.activation(out=dcol[:, 0:1],
                                            in_=ncols[:, h2 * NQ + ti:
                                                      h2 * NQ + ti + 1],
                                            func=AF.Exp))

                    @p.b.vector
                    def _(e, h2=h2, ti=ti):
                        wait(e, (sv, sv.n))
                        e.tensor_scalar(out=dcol[:, 1:2], in0=pA[:, 64:65],
                                        scalar1=dcol[:, 0:1], scalar2=1e-6,
                                        op0=ALU.mult, op1=ALU.add)
                        e.drain()
                        e.reciprocal(out=dcol[:, 2:3], in_=dcol[:, 1:2])
                        e.drain()
                        e.tensor_tensor(out=dcol[:, 3:4], in0=dcol[:, 0:1],
                                        in1=dcol[:, 2:3], op=ALU.mult)
                        vs.inc(e.drain())

                    @p.b.vector
                    def _(e, h2=h2, ti=ti):
                        wait(e, (vs, vs.n))
                        vs.inc(e.tensor_scalar(
                            out=aout[:, ti * 128 + h2 * 64:
                                     ti * 128 + h2 * 64 + 64],
                            in0=pA[:, 0:64], scalar1=dcol[:, 3:4],
                            scalar2=None, op0=ALU.mult))

            for ti in range(NQ):
                @p.b.tensor
                def _(e, ti=ti, hp=hp):
                    wait(e, (vs, vs.n), (apt, apt.n))
                    tv.inc(e.transpose(out=pT_[:],
                                       in_=aout[:, ti * 128:(ti + 1) * 128],
                                       identity=c["identity"][:]))

                @p.b.vector
                def _(e, ti=ti, hp=hp):
                    wait(e, (tv, tv.n))
                    apt.inc(e.tensor_copy(
                        out=aoT[:, hp * T + ti * 128: hp * T + (ti + 1) * 128],
                        in_=pT_[:]))

    ph_xf.close()

    with ExitStack() as ph:
        ec = ph.enter_context
        wo = ec(nc.sbuf_tensor("l1a_wo", [128, DT * 1024], BF16))
        stg = [ec(nc.sbuf_tensor("l1a_stg%d" % i, [128, DT * CH], F32))
               for i in range(2)]
        pO = ec(nc.psum_tensor("l1a_pO", [128, CH], F32))
        macc4 = macc.rearrange("(i p) (n f) -> p i n f", p=128, f=CH)
        stg_tok = [0, 0]

        @p.b.gpsimd
        def _(e):
            wait(e, (tv, tv.n), (vs, vs.n), (sv, sv.n), (apt, apt.n))
            qG.inc(e.dma_start(out=wo[:].rearrange("p (i n) -> p i n", n=1024),
                               in_=w["woutT"].rearrange("(i p) n -> p i n", p=128)),
                   16)
        wo_tok = qG.n

        for ci in range(NCH):
            spar = ci % 2
            for jd in range(DT):
                @p.b.tensor
                def _(e, ci=ci, jd=jd, wo_tok=wo_tok):
                    wait(e, (qG, wo_tok), (vs, vs.n))
                    for i in range(DT):
                        m = e.matmul(
                            out=pO[:],
                            lhsT=wo[:, i * 1024 + jd * 128:
                                    i * 1024 + (jd + 1) * 128],
                            rhs=aoT[:, i * T + ci * CH: i * T + (ci + 1) * CH],
                            start=(i == 0), stop=(i == DT - 1))
                    tv.inc(m)

                @p.b.vector
                def _(e, ci=ci, jd=jd, spar=spar):
                    wait(e, (tv, tv.n), (qG, stg_tok[spar]), (qG, smalls_tok))
                    vs.inc(e.tensor_scalar(
                        out=stg[spar][:, jd * CH:(jd + 1) * CH], in0=pO[:],
                        scalar1=boutc[:, jd:jd + 1], scalar2=None, op0=ALU.add))

            @p.b.gpsimd
            def _(e, ci=ci, spar=spar):
                wait(e, (vs, vs.n))
                qG.inc(e.dma_start(
                    out=macc4[:, :, ci, :],
                    in_=stg[spar][:].rearrange("p (i f) -> p i f", f=CH)), 16)
            stg_tok[spar] = qG.n

    ph_ao.close()
    emit_sandwich_tail(p, c, T, macc, xin, xout, tag="l1a_tail")


L1A_SHAPES = {
    "wqkvT": ([D, 3 * D], F32), "rfT": ([64, 64], F32),
    "woutT": ([D, D], F32), "boutc": ([128, 8], F32),
}


def pack_l1a(inp, pref="L1_attn_"):
    g = lambda n: np.asarray(inp[pref + n], dtype=np.float32)
    return {
        "wqkvT": np.ascontiguousarray(g("w_qkv").T),
        "rfT": np.ascontiguousarray(g("random_features").T),
        "woutT": np.ascontiguousarray(g("w_out").T),
        "boutc": _bias_cols(g("b_out"), 8),
    }


def build_l1a_test(T):
    nc = new_bass()
    xin = nc.dram_tensor("xin", [D, T], F32, kind="ExternalInput").ap()
    cident = nc.dram_tensor("cident", [128, 128], BF16, kind="ExternalInput").ap()
    xo = nc.dram_tensor("xout", [D, T], F32, kind="ExternalOutput").ap()
    macc = nc.dram_tensor("macc", [D, T], F32, kind="Internal").ap()
    w = declare_weights(nc, L1A_SHAPES)
    with ExitStack() as st:
        block = st.enter_context(nc.Block())
        p = P(nc, block, st)
        c = emit_consts(p, cident)
        emit_l1_attn(p, c, T, xin, xo, w, macc)
    return nc

def emit_l1_ffn(p, c, T, xin, xout, w, macc, due):
    nc = p.nc
    NCH = T // CH
    NQ = T // 128
    qS = p.sem("qS")
    qG = p.sem("qG")
    vt = p.sem("vt")
    tv = p.sem("tv")
    vs = p.sem("vs")
    sv = p.sem("sv")
    apt = p.sem("apt")
    gq = p.sem("gq")

    xfT = p.sbuf("l1f_xfT", [128, DT * T], BF16)
    iota64 = p.sbuf("l1f_iota64", [128, 64], F32)
    bia = p.sbuf("l1f_bia", [128, 72], F32)
    bgc = bia[:, 0:32]
    buc = bia[:, 32:64]
    bdc = bia[:, 64:72]

    @p.b.gpsimd
    def _(e):
        qG.inc(e.dma_start(out=iota64[:], in_=w["iota64"]), 16)
        qG.inc(e.dma_start(out=bgc, in_=w["bgc"]), 16)
        qG.inc(e.dma_start(out=buc, in_=w["buc"]), 16)
        qG.inc(e.dma_start(out=bdc, in_=w["bdc"]), 16)
        qG.inc(e.dma_start(out=due[:, 0:1024], in_=w["down_embed"]), 16)
        qG.inc(e.dma_start(out=due[:, 1024:2048], in_=w["up_embed"]), 16)
    smalls_tok = qG.n

    with ExitStack() as ph:
        banks = make_ln_banks(p, ph, tag="l1f_ln")
        xc = [ph.enter_context(nc.sbuf_tensor("l1f_xc%d" % i, [128, DT * CH], F32))
              for i in range(2)]
        emit_preln(p, c, T, xin, xfT, banks, gb=None, xc=xc)
    xf_tok = p.sem("apt").n

    with ExitStack() as ph:
        ec = ph.enter_context
        wq_ = ec(nc.sbuf_tensor("l1f_wq", [128, DT * 512], BF16))
        wsim = ec(nc.sbuf_tensor("l1f_wsim", [128, 4 * 512], BF16))
        qT = ec(nc.sbuf_tensor("l1f_qT", [128, 4 * T], BF16))
        simt = ec(nc.sbuf_tensor("l1f_simt", [128, 512], F32))
        vals = ec(nc.sbuf_tensor("l1f_vals", [128, 64], F32))
        ixf = ec(nc.sbuf_tensor("l1f_ixf", [128, 64], F32))
        i8buf = ec(nc.sbuf_tensor("l1f_i8", [128, 8], U32))
        comb = ec(nc.sbuf_tensor("l1f_comb", [128, 64], F32))
        oh = ec(nc.sbuf_tensor("l1f_oh", [128, 512], F32))
        allix = ec(nc.sbuf_tensor("l1f_allix", [128, 64], F32))
        sc8 = ec(nc.sbuf_tensor("l1f_sc8", [128, 8], F32))
        pk8 = ec(nc.sbuf_tensor("l1f_pk8", [128, 24], F32))
        sel = ec(nc.sbuf_tensor("l1f_sel", [128, 16], F32))
        idxf = ec(nc.sbuf_tensor("l1f_idxf", [128, 32], F32))
        idxu = ec(nc.sbuf_tensor("l1f_idxu", [128, 32], U32))
        soft = ec(nc.sbuf_tensor("l1f_soft", [128, 32], F32))
        srow = ec(nc.sbuf_tensor("l1f_srow", [128, 4], F32))
        xftok = ec(nc.sbuf_tensor("l1f_xftok", [128, 1024], BF16))
        dueb = [ec(nc.sbuf_tensor("l1f_due%d" % i, [128, 2048], BF16))
                for i in range(8)]
        dscr = ec(nc.sbuf_tensor("l1f_dscr", [128, 1024], BF16))
        wrp = ec(nc.sbuf_tensor("l1f_wrp", [128, 12], F32))
        estb = ec(nc.sbuf_tensor("l1f_estb", [128, 1024], BF16))
        esT = ec(nc.sbuf_tensor("l1f_esT", [128, 1024], F32))
        pS = ec(nc.psum_tensor("l1f_pS", [128, 512], F32))
        pT_ = ec(nc.psum_tensor("l1f_pT", [128, 128], BF16))
        pQp = ec(nc.psum_tensor("l1f_pQp", [128, 512], F32))

        @p.b.gpsimd
        def _(e):
            wait(e, (apt, xf_tok))
            qG.inc(e.dma_start(out=wq_[:].rearrange("p (i n) -> p i n", n=512),
                               in_=w["Wq"].rearrange("(i p) n -> p i n", p=128)),
                   16)
            qG.inc(e.dma_start(out=wsim[:].rearrange("p (i n) -> p i n", n=512),
                               in_=w["wsim"].rearrange("(i p) n -> p i n", p=128)),
                   16)
        wq_tok = qG.n

        for kt in range(4):
            for ci in range(NCH):
                @p.b.tensor
                def _(e, kt=kt, ci=ci):
                    wait(e, (qG, wq_tok), (vs, vs.n))
                    for i in range(DT):
                        m = e.matmul(
                            out=pQp[:],
                            lhsT=wq_[:, i * 512 + kt * 128:
                                     i * 512 + (kt + 1) * 128],
                            rhs=xfT[:, i * T + ci * CH: i * T + (ci + 1) * CH],
                            start=(i == 0), stop=(i == DT - 1))
                    tv.inc(m)

                @p.b.vector
                def _(e, kt=kt, ci=ci):
                    wait(e, (tv, tv.n))
                    vs.inc(e.tensor_copy(
                        out=qT[:, kt * T + ci * CH: kt * T + (ci + 1) * CH],
                        in_=pQp[:]))

        due_tok = [0] * 8
        for ti in range(NQ):
            @p.b.tensor
            def _(e, ti=ti):
                wait(e, (vs, vs.n))
                for kt in range(4):
                    m = e.matmul(out=pS[:],
                                 lhsT=qT[:, kt * T + ti * 128:
                                         kt * T + (ti + 1) * 128],
                                 rhs=wsim[:, kt * 512:(kt + 1) * 512],
                                 start=(kt == 0), stop=(kt == 3))
                tv.inc(m)

            @p.b.vector
            def _(e, ti=ti):
                wait(e, (tv, tv.n))
                e.tensor_copy(out=simt[:], in_=pS[:])
                vs.inc(e.drain())

            @p.b.vector
            def _(e, ti=ti):
                wait(e, (vs, vs.n))
                for g in range(8):
                    e.max(out=vals[:, g * 8:(g + 1) * 8],
                          in_=simt[:, g * 64:(g + 1) * 64])
                    e.drain()
                    e.max_index(out=i8buf[:],
                                in_max=vals[:, g * 8:(g + 1) * 8],
                                in_values=simt[:, g * 64:(g + 1) * 64])
                    e.drain()
                    e.tensor_copy(out=ixf[:, g * 8:(g + 1) * 8], in_=i8buf[:])
                vs.inc(e.drain())

            for h in range(4):
                @p.b.vector
                def _(e, ti=ti, h=h):
                    wait(e, (vs, vs.n), (sv, sv.n))
                    sx = vals[:, h * 8:(h + 1) * 8]
                    sy = vals[:, (4 + h) * 8:(5 + h) * 8]
                    e.tensor_tensor(out=comb[:].rearrange("p (a b) -> p a b", b=8),
                                    in0=sx.unsqueeze(2).to_broadcast([128, 8, 8]),
                                    in1=sy.unsqueeze(1).to_broadcast([128, 8, 8]),
                                    op=ALU.add)
                    e.drain()
                    e.max(out=sc8[:], in_=comb[:])
                    e.drain()
                    e.max_index(out=i8buf[:], in_max=sc8[:], in_values=comb[:])
                    e.drain()
                    e.tensor_copy(out=pk8[:, 0:8], in_=i8buf[:])
                    e.drain()
                    e.scalar_tensor_tensor(
                        out=allix[:].rearrange("p (a b) -> p a b", b=8),
                        in0=ixf[:, h * 8:(h + 1) * 8].unsqueeze(2)
                        .to_broadcast([128, 8, 8]),
                        scalar=64.0,
                        in1=ixf[:, (4 + h) * 8:(5 + h) * 8].unsqueeze(1)
                        .to_broadcast([128, 8, 8]),
                        op0=ALU.mult, op1=ALU.add)
                    e.tensor_tensor(out=oh[:].rearrange("p (a b) -> p a b", b=64),
                                    in0=pk8[:, 0:8].unsqueeze(2)
                                    .to_broadcast([128, 8, 64]),
                                    in1=iota64[:].unsqueeze(1)
                                    .to_broadcast([128, 8, 64]),
                                    op=ALU.is_equal)
                    e.drain()
                    e.tensor_tensor(out=oh[:].rearrange("p (a b) -> p a b", b=64),
                                    in0=oh[:].rearrange("p (a b) -> p a b", b=64),
                                    in1=allix[:].unsqueeze(1)
                                    .to_broadcast([128, 8, 64]),
                                    op=ALU.mult)
                    e.drain()
                    e.reduce_sum(out=idxf[:, h * 8:(h + 1) * 8],
                                 in_=oh[:].rearrange("p (a b) -> p a b", b=64),
                                 axis=AX)
                    e.drain()
                    vs.inc(e.tensor_scalar(out=srow[:, 0:1], in0=sc8[:, 0:1],
                                           scalar1=-1.0, scalar2=None,
                                           op0=ALU.mult))

                @p.b.scalar
                def _(e, ti=ti, h=h):
                    wait(e, (vs, vs.n))
                    sv.inc(e.activation(out=soft[:, h * 8:(h + 1) * 8],
                                        in_=sc8[:], func=AF.Exp,
                                        bias=srow[:, 0:1], scale=1.0))

                @p.b.vector
                def _(e, ti=ti, h=h):
                    wait(e, (sv, sv.n))
                    e.reduce_sum(out=srow[:, 1:2],
                                 in_=soft[:, h * 8:(h + 1) * 8], axis=AX)
                    e.drain()
                    e.reciprocal(out=srow[:, 2:3], in_=srow[:, 1:2])
                    vs.inc(e.drain())

                @p.b.vector
                def _(e, ti=ti, h=h):
                    wait(e, (vs, vs.n))
                    vs.inc(e.tensor_scalar(out=soft[:, h * 8:(h + 1) * 8],
                                           in0=soft[:, h * 8:(h + 1) * 8],
                                           scalar1=srow[:, 2:3], scalar2=None,
                                           op0=ALU.mult))

            @p.b.vector
            def _(e, ti=ti):
                wait(e, (vs, vs.n))
                e.tensor_copy(out=idxu[:], in_=idxf[:])
                vs.inc(e.drain())
            idx_tok = p.sem("vs").n

            for i in range(DT):
                @p.b.tensor
                def _(e, ti=ti, i=i):
                    wait(e, (vs, vs.n))
                    if not (ti == 0 and i == 0):
                        wait(e, (apt, apt.n))
                    tv.inc(e.transpose(
                        out=pT_[:],
                        in_=xfT[:, i * T + ti * 128: i * T + (ti + 1) * 128],
                        identity=c["identity"][:]))

                @p.b.vector
                def _(e, ti=ti, i=i):
                    wait(e, (tv, tv.n))
                    apt.inc(e.tensor_copy(out=xftok[:, i * 128:(i + 1) * 128],
                                          in_=pT_[:]))

            for j4 in range(8):
                kb = (j4 % 2) * 4
                for k in range(4):
                    j = j4 * 4 + k

                    @p.b.gpsimd
                    def _(e, ti=ti, j=j, k=k, kb=kb, idx_tok=idx_tok):
                        wait(e, (vs, idx_tok), (qG, smalls_tok))
                        if j >= 8:
                            wait(e, (vs, due_tok[kb + k]))
                        gq.inc(e.indirect_dma_start(
                            out=dueb[kb + k][:], out_offset=None,
                            in_=due,
                            in_offset=IndirectOffsetOnAxis(idxu[:, j:j + 1], 0)),
                            16)
                g_tok = gq.n

                @p.b.vector
                def _(e, ti=ti, j4=j4, kb=kb, g_tok=g_tok):
                    wait(e, (gq, g_tok), (sv, sv.n))
                    for k in range(4):
                        e.scalar_tensor_tensor(out=dscr[:],
                                               in0=dueb[kb + k][:, 0:1024],
                                               scalar=1.0, in1=xftok[:],
                                               op0=ALU.mult, op1=ALU.mult,
                                               accum_out=wrp[:, k:k + 1])
                        e.drain()
                    vs.inc(e.drain())

                @p.b.scalar
                def _(e, ti=ti, j4=j4):
                    wait(e, (vs, vs.n))
                    sv.inc(e.activation(out=wrp[:, 4:8], in_=wrp[:, 0:4],
                                        func=AF.Silu))

                @p.b.vector
                def _(e, ti=ti, j4=j4):
                    wait(e, (sv, sv.n))
                    e.tensor_tensor(out=wrp[:, 8:12], in0=wrp[:, 4:8],
                                    in1=soft[:, j4 * 4:(j4 + 1) * 4],
                                    op=ALU.mult)
                    vs.inc(e.drain())

                @p.b.vector
                def _(e, ti=ti, j4=j4, kb=kb):
                    wait(e, (vs, vs.n))
                    for k in range(4):
                        j = j4 * 4 + k
                        if j == 0:
                            m = e.tensor_scalar(out=estb[:],
                                                in0=dueb[kb + k][:, 1024:2048],
                                                scalar1=wrp[:, 8 + k:9 + k],
                                                scalar2=None, op0=ALU.mult)
                        else:
                            m = e.scalar_tensor_tensor(
                                out=estb[:], in0=dueb[kb + k][:, 1024:2048],
                                scalar=wrp[:, 8 + k:9 + k], in1=estb[:],
                                op0=ALU.mult, op1=ALU.add)
                        vs.inc(m)
                    for k in range(4):
                        due_tok[kb + k] = vs.n

            for i in range(DT):
                @p.b.tensor
                def _(e, ti=ti, i=i):
                    wait(e, (vs, vs.n), (apt, apt.n))
                    tv.inc(e.transpose(out=pT_[:],
                                       in_=estb[:, i * 128:(i + 1) * 128],
                                       identity=c["identity"][:]))

                @p.b.vector
                def _(e, ti=ti, i=i):
                    wait(e, (tv, tv.n))
                    if i == 0 and ti > 0:
                        wait(e, (qG, qG.n))
                    apt.inc(e.tensor_copy(out=esT[:, i * 128:(i + 1) * 128],
                                          in_=pT_[:]))

            @p.b.gpsimd
            def _(e, ti=ti):
                wait(e, (apt, apt.n))
                qG.inc(e.dma_start(
                    out=macc.rearrange("(i p) t -> p i t", p=128)[
                        :, :, ti * 128:(ti + 1) * 128],
                    in_=esT[:].rearrange("p (i t) -> p i t", t=128)), 16)
    pkm_tok = p.sem("qG").n

    with ExitStack() as ph:
        ec = ph.enter_context
        wg_ = ec(nc.sbuf_tensor("l1f_wg", [128, DT * 1024], BF16))
        wu_ = ec(nc.sbuf_tensor("l1f_wu", [128, DT * 1024], BF16))
        wd_ = ec(nc.sbuf_tensor("l1f_wd", [128, DT * 1024], BF16))
        hg = ec(nc.sbuf_tensor("l1f_hg", [128, DT * CH], BF16))
        hu = ec(nc.sbuf_tensor("l1f_hu", [128, DT * CH], BF16))
        stg = [ec(nc.sbuf_tensor("l1f_stg%d" % i, [128, DT * CH], F32))
               for i in range(2)]
        pG2 = [ec(nc.psum_tensor("l1f_pG%d" % i, [128, CH], F32))
               for i in range(2)]
        pU2 = [ec(nc.psum_tensor("l1f_pU%d" % i, [128, CH], F32))
               for i in range(2)]
        pF2_ = [ec(nc.psum_tensor("l1f_pF%d" % i, [128, CH], F32))
                for i in range(2)]
        gsil_tok = [0, 0]
        uev_tok = [0, 0]
        fev_tok = [0, 0]
        macc4 = macc.rearrange("(i p) (n f) -> p i n f", p=128, f=CH)
        stg_tok = [pkm_tok, pkm_tok]
        step_tok = []

        for Iq in range(4):
            @p.b.gpsimd
            def _(e, Iq=Iq):
                if Iq == 0:
                    wait(e, (tv, tv.n), (vs, vs.n), (sv, sv.n), (apt, apt.n))
                elif Iq >= 1:
                    wait(e, (tv, step_tok[Iq - 1]))
                qG.inc(e.dma_start(
                    out=wg_[:].rearrange("p (i n) -> p i n", n=1024),
                    in_=w["Wg"].rearrange("(i p) n -> p i n", p=128)[
                        :, :, Iq * 1024:(Iq + 1) * 1024]), 16)
                qG.inc(e.dma_start(
                    out=wu_[:].rearrange("p (i n) -> p i n", n=1024),
                    in_=w["Wu"].rearrange("(i p) n -> p i n", p=128)[
                        :, :, Iq * 1024:(Iq + 1) * 1024]), 16)
                qG.inc(e.dma_start(
                    out=wd_[:].rearrange("p (i n) -> p i n", n=1024),
                    in_=w["Wd"][Iq * 1024:(Iq + 1) * 1024].rearrange(
                        "(i p) n -> p i n", p=128)), 16)
            w_tok = qG.n

            for ci in range(NCH):
                spar = (Iq * NCH + ci) % 2
                for ii in range(8):
                    @p.b.tensor
                    def _(e, Iq=Iq, ci=ci, ii=ii, w_tok=w_tok):
                        wait(e, (qG, w_tok), (sv, gsil_tok[ii % 2]),
                             (vs, uev_tok[ii % 2]))
                        for i in range(DT):
                            e.matmul(
                                out=pG2[ii % 2][:],
                                lhsT=wg_[:, i * 1024 + ii * 128:
                                         i * 1024 + (ii + 1) * 128],
                                rhs=xfT[:, i * T + ci * CH: i * T + (ci + 1) * CH],
                                start=(i == 0), stop=(i == DT - 1))
                        for i in range(DT):
                            m = e.matmul(
                                out=pU2[ii % 2][:],
                                lhsT=wu_[:, i * 1024 + ii * 128:
                                         i * 1024 + (ii + 1) * 128],
                                rhs=xfT[:, i * T + ci * CH: i * T + (ci + 1) * CH],
                                start=(i == 0), stop=(i == DT - 1))
                        tv.inc(m)

                    @p.b.scalar
                    def _(e, Iq=Iq, ii=ii):
                        wait(e, (tv, tv.n), (qG, smalls_tok))
                        sv.inc(e.activation(out=hg[:, ii * CH:(ii + 1) * CH],
                                            in_=pG2[ii % 2][:], func=AF.Silu,
                                            bias=bgc[:, Iq * 8 + ii:
                                                     Iq * 8 + ii + 1],
                                            scale=1.0))
                    gsil_tok[ii % 2] = p.sem("sv").n

                    @p.b.vector
                    def _(e, Iq=Iq, ii=ii):
                        wait(e, (tv, tv.n))
                        vs.inc(e.tensor_scalar(out=hu[:, ii * CH:(ii + 1) * CH],
                                               in0=pU2[ii % 2][:],
                                               scalar1=buc[:, Iq * 8 + ii:
                                                           Iq * 8 + ii + 1],
                                               scalar2=None, op0=ALU.add))
                    uev_tok[ii % 2] = p.sem("vs").n

                @p.b.vector
                def _(e, ci=ci):
                    wait(e, (sv, sv.n))
                    vs.inc(e.scalar_tensor_tensor(out=hg[:], in0=hg[:],
                                                  scalar=1.0, in1=hu[:],
                                                  op0=ALU.mult, op1=ALU.mult))

                for jd in range(DT):
                    @p.b.tensor
                    def _(e, Iq=Iq, ci=ci, jd=jd):
                        wait(e, (vs, max(vs.n if jd == 0 else 0,
                                         fev_tok[jd % 2])))
                        for ii in range(8):
                            m = e.matmul(
                                out=pF2_[jd % 2][:],
                                lhsT=wd_[:, ii * 1024 + jd * 128:
                                         ii * 1024 + (jd + 1) * 128],
                                rhs=hg[:, ii * CH:(ii + 1) * CH],
                                start=(ii == 0), stop=(ii == 7))
                        tv.inc(m)

                    @p.b.vector
                    def _(e, Iq=Iq, ci=ci, jd=jd, spar=spar):
                        wait(e, (tv, tv.n), (qG, stg_tok[spar]))
                        if Iq == 0:
                            m = e.tensor_scalar(
                                out=stg[spar][:, jd * CH:(jd + 1) * CH],
                                in0=pF2_[jd % 2][:], scalar1=bdc[:, jd:jd + 1],
                                scalar2=None, op0=ALU.add)
                        else:
                            m = e.tensor_copy(
                                out=stg[spar][:, jd * CH:(jd + 1) * CH],
                                in_=pF2_[jd % 2][:])
                        vs.inc(m)
                    fev_tok[jd % 2] = p.sem("vs").n

                @p.b.gpsimd
                def _(e, Iq=Iq, ci=ci, spar=spar):
                    wait(e, (vs, vs.n), (qG, pkm_tok))
                    qG.inc(e.dma_start(
                        out=macc4[:, :, ci, :],
                        in_=stg[spar][:].rearrange("p (i f) -> p i f", f=CH),
                        accum_op=ALU.add), 16)
                stg_tok[spar] = qG.n
            step_tok.append(p.sem("tv").n)

        @p.b.gpsimd
        def _(e):
            wait(e, (qG, qG.n))

    emit_sandwich_tail(p, c, T, macc, xin, xout, tag="l1f_tail")


L1F_SHAPES = {
    "Wg": ([D, 4096], F32), "bgc": ([128, 32], F32),
    "Wu": ([D, 4096], F32), "buc": ([128, 32], F32),
    "Wd": ([4096, D], F32), "bdc": ([128, 8], F32),
    "Wq": ([D, 512], F32), "wsim": ([512, 512], F32),
    "down_embed": ([4096, D], F32), "up_embed": ([4096, D], F32),
    "iota64": ([128, 64], F32),
}


def pack_l1f(inp, pref="L1_ffn_"):
    g = lambda n: np.asarray(inp[pref + n], dtype=np.float32)
    w = {}
    w["Wg"] = g("Wg")
    w["bgc"] = _bias_cols(g("bg"), 32)
    w["Wu"] = g("Wu")
    w["buc"] = _bias_cols(g("bu"), 32)
    w["Wd"] = g("Wd")
    w["bdc"] = _bias_cols(g("bd"), 8)
    w["Wq"] = g("Wq")
    keys = g("keys")
    ws = np.zeros((512, 512), dtype=np.float32)
    for pp in range(2):
        for h in range(4):
            b = (pp * 4 + h) * 64
            ws[b:b + 64, b:b + 64] = keys[h, :, pp, :].T
    w["wsim"] = ws
    w["down_embed"] = g("down_embed")
    w["up_embed"] = g("up_embed")
    w["iota64"] = np.broadcast_to(np.arange(64, dtype=np.float32), (128, 64)).copy()
    return w


def build_l1f_test(T):
    nc = new_bass()
    xin = nc.dram_tensor("xin", [D, T], F32, kind="ExternalInput").ap()
    cident = nc.dram_tensor("cident", [128, 128], BF16, kind="ExternalInput").ap()
    xo = nc.dram_tensor("xout", [D, T], F32, kind="ExternalOutput").ap()
    macc = nc.dram_tensor("macc", [D, T], F32, kind="Internal").ap()
    due = nc.dram_tensor("due", [4096, 2048], BF16, kind="Internal").ap()
    w = declare_weights(nc, L1F_SHAPES)
    with ExitStack() as st:
        block = st.enter_context(nc.Block())
        p = P(nc, block, st)
        c = emit_consts(p, cident)
        emit_l1_ffn(p, c, T, xin, xo, w, macc, due)
    return nc



TFULL = 4096
_prog_cache = {}


def _build_block(name, T):
    key = (name, T)
    if key in _prog_cache:
        return _prog_cache[key]
    nc = new_bass()
    xin = nc.dram_tensor("xin", [D, T], F32, kind="ExternalInput").ap()
    cident = nc.dram_tensor("cident", [128, 128], BF16, kind="ExternalInput").ap()
    xo = nc.dram_tensor("xout", [D, T], F32, kind="ExternalOutput").ap()
    with ExitStack() as st:
        block = st.enter_context(nc.Block())
        p = P(nc, block, st)
        c = emit_consts(p, cident)
        if name == "l0a":
            w = declare_weights(nc, L0A_SHAPES)
            emit_l0_attn(p, c, T, xin, xo, w)
        elif name == "l0f":
            cidentf = nc.dram_tensor("cidentf", [128, 128], F32,
                                     kind="ExternalInput").ap()
            c["identf"] = p.sbuf("c_identf", [128, 128], F32)
            qS = p.sem("qS")

            @p.b.sync
            def _(e):
                qS.inc(e.dma_start(out=c["identf"][:], in_=cidentf), 16)
            macc = nc.dram_tensor("macc", [D, T], F32, kind="Internal").ap()
            w = declare_weights(nc, L0F_SHAPES)
            emit_l0_ffn(p, c, T, xin, xo, w, macc)
        elif name == "l1a":
            macc = nc.dram_tensor("macc", [D, T], F32, kind="Internal").ap()
            w = declare_weights(nc, L1A_SHAPES)
            emit_l1_attn(p, c, T, xin, xo, w, macc)
        elif name == "l1f":
            macc = nc.dram_tensor("macc", [D, T], F32, kind="Internal").ap()
            due = nc.dram_tensor("due", [4096, 2048], BF16, kind="Internal").ap()
            w = declare_weights(nc, L1F_SHAPES)
            emit_l1_ffn(p, c, T, xin, xo, w, macc, due)
    _prog_cache[key] = nc
    return nc


def _build_fused(T):
    key = ("fused", T)
    if key in _prog_cache:
        return _prog_cache[key]
    nc = new_bass()
    xin = nc.dram_tensor("xin", [D, T], F32, kind="ExternalInput").ap()
    cident = nc.dram_tensor("cident", [128, 128], BF16, kind="ExternalInput").ap()
    cidentf = nc.dram_tensor("cidentf", [128, 128], F32, kind="ExternalInput").ap()
    xo = nc.dram_tensor("xout", [D, T], F32, kind="ExternalOutput").ap()
    x1 = nc.dram_tensor("x1", [D, T], F32, kind="Internal").ap()
    x2 = nc.dram_tensor("x2", [D, T], F32, kind="Internal").ap()
    x3 = nc.dram_tensor("x3", [D, T], F32, kind="Internal").ap()
    macc = nc.dram_tensor("macc", [D, T], F32, kind="Internal").ap()
    due = nc.dram_tensor("due", [4096, 2048], BF16, kind="Internal").ap()
    w0a = declare_weights(nc, L0A_SHAPES, prefix="w0a_")
    w0f = declare_weights(nc, L0F_SHAPES, prefix="w0f_")
    w1a = declare_weights(nc, L1A_SHAPES, prefix="w1a_")
    w1f = declare_weights(nc, L1F_SHAPES, prefix="w1f_")
    with ExitStack() as st:
        block = st.enter_context(nc.Block())
        p = P(nc, block, st)
        c = emit_consts(p, cident)
        c["identf"] = p.sbuf("c_identf", [128, 128], F32)
        qS = p.sem("qS")

        @p.b.sync
        def _(e):
            qS.inc(e.dma_start(out=c["identf"][:], in_=cidentf), 16)

        for emitfn, args in (
            (emit_l0_attn, (p, c, T, xin, x1, w0a)),
            (emit_l0_ffn, (p, c, T, x1, x2, w0f, macc)),
            (emit_l1_attn, (p, c, T, x2, x3, w1a, macc)),
            (emit_l1_ffn, (p, c, T, x3, xo, w1f, macc, due)),
        ):
            emit_block_barrier(p)
            with ExitStack() as bs:
                p.scopes.append(bs)
                try:
                    emitfn(*args)
                finally:
                    p.scopes.pop()
    _prog_cache[key] = nc
    return nc


def kernel(**inputs):
    import os
    if not os.environ.get("KERNEL_UNFUSED"):
        return _kernel_fused(**inputs)
    return _kernel_unfused(**inputs)


def _kernel_fused(**inputs):
    T = TFULL
    x = np.asarray(inputs["L0_attn_x"], dtype=np.float32)
    ncores = x.shape[0]
    shards = [np.ascontiguousarray(x[i].T) for i in range(ncores)]
    base = {}
    for pref, pk in (("w0a_", pack_l0a(inputs)), ("w0f_", pack_l0f(inputs)),
                     ("w1a_", pack_l1a(inputs)), ("w1f_", pack_l1f(inputs))):
        for k, v in pk.items():
            base[pref + k] = v
    base["cident"] = np.eye(128).astype(ml_dtypes.bfloat16)
    base["cidentf"] = np.eye(128, dtype=np.float32)
    nc = _build_fused(T)
    in_maps = []
    for i in range(ncores):
        m = dict(base)
        m["xin"] = shards[i]
        in_maps.append(m)
    res = run_bass_kernel_spmd(nc, in_maps, core_ids=list(range(ncores)))
    return np.stack([np.asarray(res.results[i]["xout"]).T
                     for i in range(ncores)], axis=0).astype(np.float32)


def _kernel_unfused(**inputs):
    T = TFULL
    x = np.asarray(inputs["L0_attn_x"], dtype=np.float32)
    ncores = x.shape[0]
    shards = [np.ascontiguousarray(x[i].T) for i in range(ncores)]
    cident = np.eye(128).astype(ml_dtypes.bfloat16)
    cidentf = np.eye(128, dtype=np.float32)
    packs = {
        "l0a": pack_l0a(inputs),
        "l0f": pack_l0f(inputs),
        "l1a": pack_l1a(inputs),
        "l1f": pack_l1f(inputs),
    }
    for name in ("l0a", "l0f", "l1a", "l1f"):
        nc = _build_block(name, T)
        base = dict(packs[name])
        base["cident"] = cident
        if name == "l0f":
            base["cidentf"] = cidentf
        in_maps = []
        for i in range(ncores):
            m = dict(base)
            m["xin"] = shards[i]
            in_maps.append(m)
        res = run_bass_kernel_spmd(nc, in_maps, core_ids=list(range(ncores)))
        shards = [np.asarray(res.results[i]["xout"]) for i in range(ncores)]
    out = np.stack([s.T for s in shards], axis=0).astype(np.float32)
    return out


# version 10 (final)
# speedup vs baseline: 1.2598x; 1.2598x over previous
import numpy as np
import ml_dtypes
from contextlib import ExitStack

import concourse.bass as bass
import concourse.mybir as mybir
from concourse.bass import IndirectOffsetOnAxis
from concourse.bass_utils import run_bass_kernel_spmd

F32 = mybir.dt.float32
BF16 = mybir.dt.bfloat16
U32 = mybir.dt.uint32
AF = mybir.ActivationFunctionType
ALU = mybir.AluOpType
AX = mybir.AxisListType.X

D = 1024
DT = 8
CH = 512
LN_EPS = 1e-5


class Sem:
    __slots__ = ("h", "n")

    def __init__(self, h):
        self.h = h
        self.n = 0

    def inc(self, inst, k=1):
        inst.then_inc(self.h, k)
        self.n += k
        return self.n


def wait(eng, *toks):
    for t in toks:
        if isinstance(t, tuple):
            eng.wait_ge(t[0].h, t[1])
        else:
            eng.wait_ge(t.h, t.n)


class P:
    def __init__(self, nc, block, stack):
        self.nc = nc
        self.b = block
        self.stack = stack
        self.scopes = [stack]
        self._sems = {}

    def sem(self, name):
        if name not in self._sems:
            h = self.stack.enter_context(self.nc.semaphore(name))
            self._sems[name] = Sem(h)
        return self._sems[name]

    def sbuf(self, name, shape, dtype):
        return self.scopes[-1].enter_context(self.nc.sbuf_tensor(name, shape, dtype))


def emit_block_barrier(p):
    tv, vs, sv, apt = p.sem("tv"), p.sem("vs"), p.sem("sv"), p.sem("apt")
    vt, qS, qG = p.sem("vt"), p.sem("qS"), p.sem("qG")
    gq = p.sem("gq")
    toks = (tv, vs, sv, apt, vt, (qS, qS.n), (qG, qG.n), (gq, gq.n))
    for sec in (p.b.sync, p.b.gpsimd, p.b.vector, p.b.scalar, p.b.tensor):
        @sec
        def _(e):
            wait(e, *toks)


def new_bass():
    return bass.Bass("TRN2", target_bir_lowering=False, debug=False,
                     detect_race_conditions=False)



def emit_consts(p, cident_ap):
    c = {}
    c["ones128"] = p.sbuf("c_ones128", [128, 1], BF16)
    c["ones_row"] = p.sbuf("c_ones_row", [1, CH], BF16)
    c["ones1x128"] = p.sbuf("c_ones1x128", [1, 128], BF16)
    c["ones128f"] = p.sbuf("c_ones128f", [128, 1], F32)
    c["ones1x128f"] = p.sbuf("c_ones1x128f", [1, 128], F32)
    c["identity"] = p.sbuf("c_identity", [128, 128], BF16)
    s = p.sem("const")
    qS = p.sem("qS")

    @p.b.sync
    def _(e):
        qS.inc(e.dma_start(out=c["identity"][:], in_=cident_ap), 16)

    @p.b.vector
    def _(e):
        e.memset(c["ones128"][:], 1.0)
        e.memset(c["ones_row"][:], 1.0)
        e.memset(c["ones128f"][:], 1.0)
        e.memset(c["ones1x128f"][:], 1.0)
        s.inc(e.memset(c["ones1x128"][:], 1.0))
    return c



def make_ln_banks(p, stack, tag="lnb"):
    nc = p.nc
    ec = stack.enter_context
    return {
        "ps1": ec(nc.psum_tensor(tag + "_ps1", [1, CH], F32)),
        "ps2": ec(nc.psum_tensor(tag + "_ps2", [1, CH], F32)),
        "pA": ec(nc.psum_tensor(tag + "_pA", [128, CH], F32)),
        "pB": ec(nc.psum_tensor(tag + "_pB", [128, CH], F32)),
        "xcb": ec(nc.sbuf_tensor(tag + "_xcb", [128, DT * CH], BF16)),
        "sq": ec(nc.sbuf_tensor(tag + "_sq", [128, DT * CH], BF16)),
        "rows": ec(nc.sbuf_tensor(tag + "_rows", [1, 7 * CH], F32)),
        "rowsb": ec(nc.sbuf_tensor(tag + "_rowsb", [1, 2 * CH], BF16)),
        "tmp": ec(nc.sbuf_tensor(tag + "_tmp", [128, CH], BF16)),
    }


def emit_ln_chunk(p, c, banks, src, dst, gb=None, nch=CH, f32=False):
    vt = p.sem("vt")
    tv = p.sem("tv")
    vs = p.sem("vs")
    sv = p.sem("sv")
    apt = p.sem("apt")
    ps1, ps2, pA, pB = banks["ps1"], banks["ps2"], banks["pA"], banks["pB"]
    xcb, sq, rows, rowsb, tmp = (banks["xcb"], banks["sq"], banks["rows"],
                                 banks["rowsb"], banks["tmp"])
    ones_col = c["ones128f"] if f32 else c["ones128"]
    ones_lhs = c["ones1x128f"] if f32 else c["ones1x128"]
    mu = rows[:, 0 * CH:0 * CH + nch]
    m2 = rows[:, 1 * CH:1 * CH + nch]
    var = rows[:, 2 * CH:2 * CH + nch]
    vre = rows[:, 3 * CH:3 * CH + nch]
    srt = rows[:, 4 * CH:4 * CH + nch]
    rr = rows[:, 5 * CH:5 * CH + nch]
    B0 = rows[:, 6 * CH:6 * CH + nch]
    rb = rowsb[:, 0:nch]
    B0b = rowsb[:, CH:CH + nch]

    @p.b.vector
    def _(e):
        wait(e, (tv, tv.n))
        for i in range(DT):
            e.tensor_copy(out=xcb[:, i * CH:i * CH + nch], in_=src(i))
        vt.inc(e.tensor_tensor(out=sq[:], in0=xcb[:], in1=xcb[:], op=ALU.mult))

    @p.b.tensor
    def _(e):
        wait(e, (vt, vt.n))
        for i in range(DT):
            e.matmul(out=ps1[:, 0:nch], lhsT=ones_col[:],
                     rhs=xcb[:, i * CH:i * CH + nch],
                     start=(i == 0), stop=(i == DT - 1))
        for i in range(DT):
            m = e.matmul(out=ps2[:, 0:nch], lhsT=ones_col[:],
                         rhs=sq[:, i * CH:i * CH + nch],
                         start=(i == 0), stop=(i == DT - 1))
        tv.inc(m)

    @p.b.vector
    def _(e):
        wait(e, (tv, tv.n))
        e.tensor_scalar(out=mu, in0=ps1[:, 0:nch], scalar1=1.0 / D,
                        scalar2=None, op0=ALU.mult)
        e.tensor_scalar(out=m2, in0=ps2[:, 0:nch], scalar1=1.0 / D,
                        scalar2=None, op0=ALU.mult)
        e.scalar_tensor_tensor(out=var, in0=mu, scalar=-1.0, in1=mu,
                               op0=ALU.mult, op1=ALU.mult)
        e.tensor_add(out=var, in0=m2, in1=var)
        vs.inc(e.tensor_scalar(out=vre, in0=var, scalar1=LN_EPS, scalar2=None,
                               op0=ALU.add))

    @p.b.scalar
    def _(e):
        wait(e, (vs, vs.n))
        sv.inc(e.activation(out=srt, in_=vre, func=AF.Sqrt))

    @p.b.vector
    def _(e):
        wait(e, (sv, sv.n))
        e.reciprocal(out=rr, in_=srt)
        e.drain()
        e.scalar_tensor_tensor(out=B0, in0=mu, scalar=-1.0, in1=rr,
                               op0=ALU.mult, op1=ALU.mult)
        if not f32:
            e.tensor_copy(out=rb, in_=rr)
            e.tensor_copy(out=B0b, in_=B0)
        vt.inc(e.drain())

    for i in range(DT):
        @p.b.tensor
        def _(e, i=i):
            wait(e, (vt, vt.n))
            if i > 0:
                wait(e, (apt, apt.n))
            if gb is None:
                e.matmul(out=pA[:, 0:nch], lhsT=ones_lhs[:],
                         rhs=rr if f32 else rb, start=True, stop=True)
                m = e.matmul(out=pB[:, 0:nch], lhsT=ones_lhs[:],
                             rhs=B0 if f32 else B0b, start=True, stop=True)
            else:
                g_row, b_row = gb
                gseg = g_row[:, i * 128:(i + 1) * 128]
                bseg = b_row[:, i * 128:(i + 1) * 128]
                e.matmul(out=pA[:, 0:nch], lhsT=gseg, rhs=rb, start=True, stop=True)
                e.matmul(out=pB[:, 0:nch], lhsT=gseg, rhs=B0b, start=True, stop=False)
                m = e.matmul(out=pB[:, 0:nch], lhsT=bseg,
                             rhs=c["ones_row"][:, 0:nch], start=False, stop=True)
            tv.inc(m)

        @p.b.vector
        def _(e, i=i):
            wait(e, (tv, tv.n))
            e.tensor_tensor(out=tmp[:, 0:nch], in0=xcb[:, i * CH:i * CH + nch],
                            in1=pA[:, 0:nch], op=ALU.mult)
            apt.inc(e.tensor_tensor(out=dst(i), in0=tmp[:, 0:nch],
                                    in1=pB[:, 0:nch], op=ALU.add))


def emit_preln(p, c, T, xin_hbm, dst, banks, gb=None, xc=None):
    NCH = T // CH
    qS = p.sem("qS")
    apt = p.sem("apt")
    src4 = xin_hbm.rearrange("(i p) (n f) -> p i n f", p=128, f=CH)
    apt_base = apt.n
    for ci in range(NCH):
        @p.b.sync
        def _(e, ci=ci):
            if ci >= 2:
                wait(e, (apt, apt_base + (ci - 1) * DT))
            qS.inc(e.dma_start(
                out=xc[ci % 2][:].rearrange("p (i f) -> p i f", f=CH),
                in_=src4[:, :, ci, :]), 16)
        ld_tok = qS.n

        def src(i, ci=ci):
            return xc[ci % 2][:, i * CH:(i + 1) * CH]

        def dst_sl(i, ci=ci):
            return dst[:, i * T + ci * CH: i * T + (ci + 1) * CH]

        @p.b.vector
        def _(e, tok=ld_tok):
            wait(e, (qS, tok))
        emit_ln_chunk(p, c, banks, src, dst_sl, gb=gb)



def _np_ln(x, g=None, b=None, axis=0):
    mu = x.mean(axis=axis, keepdims=True)
    var = ((x - mu) ** 2).mean(axis=axis, keepdims=True)
    y = (x - mu) / np.sqrt(var + LN_EPS)
    if g is not None:
        y = y * g[:, None] + b[:, None]
    return y



def emit_l0_attn(p, c, T, xin, xout, w, phases=("A", "BC", "D1", "D2")):
    nc = p.nc
    NCH = T // CH
    NQ = T // 128
    qS = p.sem("qS")
    qG = p.sem("qG")
    vt = p.sem("vt")
    tv = p.sem("tv")
    vs = p.sem("vs")
    sv = p.sem("sv")
    apt = p.sem("apt")

    aT = p.sbuf("l0a_aT", [128, DT * T], BF16)
    grow = p.sbuf("l0a_grows", [1, 4 * D], BF16)
    ln1g, ln1b = grow[:, 0:D], grow[:, D:2 * D]
    ln2g, ln2b = grow[:, 2 * D:3 * D], grow[:, 3 * D:4 * D]
    qkvb = p.sbuf("l0a_qkvb", [128, 24], F32)
    qbs = p.sbuf("l0a_qbs", [128, 8], F32)
    mask4_0 = p.sbuf("l0a_mask40", [128, 1024], BF16)
    mask4_g = p.sbuf("l0a_mask4g", [128, 1024], BF16)

    @p.b.gpsimd
    def _(e):
        qG.inc(e.dma_start(out=grow[:, 0:D], in_=w["ln1g"]), 16)
        qG.inc(e.dma_start(out=grow[:, D:2 * D], in_=w["ln1b"]), 16)
        qG.inc(e.dma_start(out=grow[:, 2 * D:3 * D], in_=w["ln2g"]), 16)
        qG.inc(e.dma_start(out=grow[:, 3 * D:4 * D], in_=w["ln2b"]), 16)
        qG.inc(e.dma_start(out=qkvb[:, 0:8], in_=w["qb2"]), 16)
        qG.inc(e.dma_start(out=qkvb[:, 8:16], in_=w["kb2"]), 16)
        qG.inc(e.dma_start(out=qkvb[:, 16:24], in_=w["vb2"]), 16)
    smalls_tok = qG.n

    @p.b.sync
    def _(e):
        qS.inc(e.dma_start(out=mask4_0[:], in_=w["mask4_0"]), 16)
        qS.inc(e.dma_start(out=mask4_g[:], in_=w["mask4_g"]), 16)
    mask_tok = qS.n

    @p.b.vector
    def _(e):
        wait(e, (qG, smalls_tok))
        vt.inc(e.tensor_scalar(out=qbs[:], in0=qkvb[:, 0:8], scalar1=0.125,
                               scalar2=None, op0=ALU.mult))
    qbs_tok = vt.n

    with ExitStack() as ph_u:
        uT = ph_u.enter_context(nc.sbuf_tensor("l0a_uT", [128, DT * T], BF16))

        with ExitStack() as ph:
            banks = make_ln_banks(p, ph, tag="l0a_ln")
            xc = [ph.enter_context(nc.sbuf_tensor("l0a_xc%d" % i,
                                                  [128, DT * CH], F32))
                  for i in range(2)]
            emit_preln(p, c, T, xin, uT, banks, gb=None, xc=xc)
        uT_tok = p.sem("apt").n

        if "DUMPU" in phases:
            @p.b.gpsimd
            def _(e):
                wait(e, (apt, uT_tok))
                qG.inc(e.dma_start(
                    out=xout.rearrange("(i p) t -> p i t", p=128),
                    in_=uT[:].rearrange("p (i t) -> p i t", t=T)), 16)

            @p.b.gpsimd
            def _(e):
                wait(e, (qG, qG.n))
            return

        with ExitStack() as ph:
            ec = ph.enter_context
            wbuf = [ec(nc.sbuf_tensor("l0a_w%d" % i, [128, 3 * DT * 128], BF16))
                    for i in range(2)]
            QT = ec(nc.sbuf_tensor("l0a_QT", [128, T], BF16))
            KT = ec(nc.sbuf_tensor("l0a_KT", [128, T], BF16))
            VT = ec(nc.sbuf_tensor("l0a_VT", [128, T], BF16))
            Vtok = ec(nc.sbuf_tensor("l0a_Vtok", [128, NQ * 128], BF16))
            Pn4 = ec(nc.sbuf_tensor("l0a_Pn4", [128, 1024], BF16))
            PTs = ec(nc.sbuf_tensor("l0a_PTs", [128, 1024], BF16))
            nmrow = ec(nc.sbuf_tensor("l0a_nm", [128, 12], F32))
            pS4 = [ec(nc.psum_tensor("l0a_pS%d" % i, [128, 512], F32))
                   for i in range(2)]
            pT = ec(nc.psum_tensor("l0a_pT", [128, 1024], BF16))
            pT2 = [ec(nc.psum_tensor("l0a_pT2%d" % i, [128, 128], BF16))
                   for i in range(2)]
            pO = ec(nc.psum_tensor("l0a_pO", [128, 512], F32))
            pQ, pK, pV = pS4[0], pS4[1], pO
            ptc_tok = [0]
            oev_tok = [0]

            import os as _os
            for pp in range(8 if "BC" in phases else 0):
                @p.b.gpsimd
                def _(e, pp=pp):
                    if pp >= 2:
                        wait(e, (tv, tv.n))
                    else:
                        wait(e, (apt, uT_tok))
                    wsl = slice(pp * 128, (pp + 1) * 128)
                    for j, wn in enumerate(("wq", "wk", "wv")):
                        src = w[wn].rearrange("(i p) n -> p i n", p=128)[:, :, wsl]
                        qG.inc(e.dma_start(
                            out=wbuf[pp % 2][:, j * DT * 128:(j + 1) * DT * 128]
                            .rearrange("p (i n) -> p i n", n=128),
                            in_=src), 16)
                wtok = qG.n

                for ci in range(NCH):
                    @p.b.tensor
                    def _(e, pp=pp, ci=ci, wtok=wtok):
                        wait(e, (qG, wtok), (apt, uT_tok), (vt, vt.n))
                        if ci == 0:
                            wait(e, (sv, sv.n), (vs, vs.n))
                        wb = wbuf[pp % 2]
                        for j, pdst in enumerate((pQ, pK, pV)):
                            for i in range(DT):
                                m = e.matmul(
                                    out=pdst[:],
                                    lhsT=wb[:, j * DT * 128 + i * 128:
                                            j * DT * 128 + (i + 1) * 128],
                                    rhs=uT[:, i * T + ci * CH:
                                           i * T + (ci + 1) * CH],
                                    start=(i == 0), stop=(i == DT - 1))
                            tv.inc(m)

                    @p.b.vector
                    def _(e, pp=pp, ci=ci):
                        wait(e, (tv, tv.n), (vt, qbs_tok))
                        sl = slice(ci * CH, (ci + 1) * CH)
                        e.tensor_scalar(out=QT[:, sl], in0=pQ[:], scalar1=0.125,
                                        scalar2=qbs[:, pp:pp + 1], op0=ALU.mult,
                                        op1=ALU.add)
                        e.tensor_scalar(out=KT[:, sl], in0=pK[:],
                                        scalar1=qkvb[:, 8 + pp:9 + pp],
                                        scalar2=None, op0=ALU.add)
                        vt.inc(e.tensor_scalar(out=VT[:, sl], in0=pV[:],
                                               scalar1=qkvb[:, 16 + pp:17 + pp],
                                               scalar2=None, op0=ALU.add))

                for tt in range(0 if _os.environ.get("L0A_NOVT") else NQ):
                    @p.b.tensor
                    def _(e, tt=tt, pp=pp):
                        wait(e, (vt, vt.n))
                        if tt >= 2:
                            wait(e, (apt, apt.n - 1))
                        tv.inc(e.transpose(
                            out=pT2[tt % 2][:],
                            in_=VT[:, tt * 128:(tt + 1) * 128],
                            identity=c["identity"][:]))

                    @p.b.vector
                    def _(e, tt=tt, pp=pp):
                        wait(e, (tv, tv.n))
                        apt.inc(e.tensor_copy(
                            out=Vtok[:, tt * 128:(tt + 1) * 128],
                            in_=pT2[tt % 2][:]))

                for hh in range(2):
                    hsl = slice(hh * 64, (hh + 1) * 64)
                    for qg in range(NQ // 4):
                        qbl = [qg * 4 + k for k in range(4)]

                        @p.b.tensor
                        def _(e, hh=hh, qg=qg, qbl=qbl, hsl=hsl):
                            wait(e, (sv, sv.n), (qS, mask_tok))
                            if qg == 0 and hh == 0:
                                wait(e, (apt, apt.n), (vs, vs.n))
                            for k, qb in enumerate(qbl):
                                kcols = slice(0, 256) if qb == 0 else \
                                    slice((qb - 1) * 128, (qb + 1) * 128)
                                m = e.matmul(
                                    out=pS4[k // 2][:, (k % 2) * 256:
                                                    (k % 2) * 256 + 256],
                                    lhsT=QT[hsl, qb * 128:(qb + 1) * 128],
                                    rhs=KT[hsl, kcols],
                                    start=True, stop=True)
                                tv.inc(m)

                        @p.b.vector
                        def _(e, qg=qg):
                            wait(e, (tv, tv.n))
                            e.reduce_max(out=nmrow[:, 0:2],
                                         in_=pS4[0][:].rearrange(
                                             "p (a b) -> p a b", b=256),
                                         axis=AX, negate=True)
                            e.reduce_max(out=nmrow[:, 2:4],
                                         in_=pS4[1][:].rearrange(
                                             "p (a b) -> p a b", b=256),
                                         axis=AX, negate=True)
                            vs.inc(e.drain())

                        @p.b.scalar
                        def _(e, qg=qg):
                            wait(e, (vs, vs.n))
                            for k in range(4):
                                m = e.activation(
                                    out=Pn4[:, k * 256:(k + 1) * 256],
                                    in_=pS4[k // 2][:, (k % 2) * 256:
                                                    (k % 2) * 256 + 256],
                                    func=AF.Exp, bias=nmrow[:, k:k + 1],
                                    scale=1.0)
                            sv.inc(m)

                        @p.b.vector
                        def _(e, qg=qg):
                            wait(e, (sv, sv.n))
                            mk = mask4_0 if qg == 0 else mask4_g
                            e.tensor_tensor(out=Pn4[:], in0=Pn4[:], in1=mk[:],
                                            op=ALU.mult)
                            e.drain()
                            e.reduce_sum(out=nmrow[:, 4:8],
                                         in_=Pn4[:].rearrange(
                                             "p (a b) -> p a b", b=256),
                                         axis=AX)
                            e.drain()
                            e.reciprocal(out=nmrow[:, 8:12], in_=nmrow[:, 4:8])
                            vs.inc(e.drain())

                        @p.b.vector
                        def _(e, qg=qg):
                            wait(e, (vs, vs.n))
                            for k in range(4):
                                m = e.tensor_scalar(
                                    out=Pn4[:, k * 256:(k + 1) * 256],
                                    in0=Pn4[:, k * 256:(k + 1) * 256],
                                    scalar1=nmrow[:, 8 + k:9 + k],
                                    scalar2=None, op0=ALU.mult)
                            vs.inc(m)

                        @p.b.tensor
                        def _(e, qg=qg):
                            wait(e, (vs, vs.n), (sv, ptc_tok[0]))
                            for z in range(8):
                                m = e.transpose(
                                    out=pT[:, z * 128:(z + 1) * 128],
                                    in_=Pn4[:, z * 128:(z + 1) * 128],
                                    identity=c["identity"][:])
                            tv.inc(m)

                        @p.b.scalar
                        def _(e, qg=qg):
                            wait(e, (tv, tv.n))
                            sv.inc(e.copy(out=PTs[:], in_=pT[:]))
                        ptc_tok[0] = p.sem("sv").n

                        @p.b.tensor
                        def _(e, hh=hh, qg=qg, qbl=qbl, hsl=hsl):
                            wait(e, (sv, sv.n), (vs, oev_tok[0]))
                            tp = (0, 64) if hh == 1 else None
                            for k, qb in enumerate(qbl):
                                for bb in range(2):
                                    tt = bb if qb == 0 else qb - 1 + bb
                                    lhs = Vtok[:, tt * 128 + hh * 64:
                                               tt * 128 + hh * 64 + 64]
                                    m = e.matmul(
                                        out=pO[hsl, k * 128:(k + 1) * 128],
                                        lhsT=lhs,
                                        rhs=PTs[:, (k * 2 + bb) * 128:
                                                (k * 2 + bb) * 128 + 128],
                                        start=(bb == 0), stop=(bb == 1),
                                        tile_position=tp)
                                tv.inc(m)

                        @p.b.vector
                        def _(e, hh=hh, qg=qg, pp=pp, hsl=hsl):
                            wait(e, (tv, tv.n))
                            m = e.tensor_copy(
                                out=aT[hsl, pp * T + qg * 512:
                                       pp * T + (qg + 1) * 512],
                                in_=pO[hsl, :])
                            vs.inc(m)
                            apt.inc(e.drain())
                        oev_tok[0] = p.sem("vs").n

        if "DUMPA" in phases:
            @p.b.gpsimd
            def _(e):
                wait(e, (apt, apt.n), (tv, tv.n), (vs, vs.n), (sv, sv.n))
                qG.inc(e.dma_start(
                    out=xout.rearrange("(i p) t -> p i t", p=128),
                    in_=aT[:].rearrange("p (i t) -> p i t", t=T)), 16)

            @p.b.gpsimd
            def _(e):
                wait(e, (qG, qG.n))
            return
        if "D1" not in phases:
            return
        with ExitStack() as ph:
            banks = make_ln_banks(p, ph, tag="l0a_ln1")
            zA = ph.enter_context(nc.sbuf_tensor("l0a_zA", [128, DT * CH], F32))

            @p.b.tensor
            def _(e):
                wait(e, (qG, smalls_tok))

            for ci in range(NCH):
                @p.b.vector
                def _(e, ci=ci):
                    import os
                    wait(e, (apt, apt.n))
                    for i in range(DT):
                        if os.environ.get("D1ZCOPY"):
                            e.tensor_copy(
                                out=zA[:, i * CH:(i + 1) * CH],
                                in_=aT[:, i * T + ci * CH: i * T + (ci + 1) * CH])
                        else:
                            e.scalar_tensor_tensor(
                                out=zA[:, i * CH:(i + 1) * CH],
                                in0=uT[:, i * T + ci * CH: i * T + (ci + 1) * CH],
                                scalar=1.0,
                                in1=aT[:, i * T + ci * CH: i * T + (ci + 1) * CH],
                                op0=ALU.mult, op1=ALU.add)

                def src(i, ci=ci):
                    import os
                    if os.environ.get("D1NOADD"):
                        return aT[:, i * T + ci * CH: i * T + (ci + 1) * CH]
                    return zA[:, i * CH:(i + 1) * CH]

                def dstf(i, ci=ci):
                    return aT[:, i * T + ci * CH: i * T + (ci + 1) * CH]

                import os
                emit_ln_chunk(p, c, banks, src, dstf, gb=None if os.environ.get('D1PLAIN') else (ln1g, ln1b))

    if "DUMPX" in phases:
        @p.b.gpsimd
        def _(e):
            wait(e, (apt, apt.n), (tv, tv.n), (vs, vs.n), (sv, sv.n))
            qG.inc(e.dma_start(
                out=xout.rearrange("(i p) t -> p i t", p=128),
                in_=aT[:].rearrange("p (i t) -> p i t", t=T)), 16)

        @p.b.gpsimd
        def _(e):
            wait(e, (qG, qG.n))
        return
    if "D2" not in phases:
        return
    with ExitStack() as ph:
        ec = ph.enter_context
        banks = make_ln_banks(p, ph, tag="l0a_ln2")
        wff1 = ec(nc.sbuf_tensor("l0a_wff1", [128, DT * 2048], BF16))
        wff2 = [ec(nc.sbuf_tensor("l0a_wff2_%d" % i, [128, 16 * 128], BF16))
                for i in range(2)]
        ffb = ec(nc.sbuf_tensor("l0a_ffb", [128, 24], F32))
        hT = ec(nc.sbuf_tensor("l0a_hT", [128, 16 * CH], BF16))
        zA = ec(nc.sbuf_tensor("l0a_zA2", [128, DT * CH], F32))
        zB = zA
        xc2 = ec(nc.sbuf_tensor("l0a_xc2", [128, DT * CH], F32))
        pH2 = [ec(nc.psum_tensor("l0a_pH%d" % i, [128, CH], F32))
               for i in range(2)]
        pF2 = [ec(nc.psum_tensor("l0a_pF%d" % i, [128, CH], F32))
               for i in range(2)]
        hsil_tok = [0, 0]
        fev_tok = [0, 0]

        @p.b.gpsimd
        def _(e):
            wait(e, (tv, tv.n), (apt, apt.n), (vs, vs.n), (sv, sv.n))
            qG.inc(e.dma_start(out=wff1[:].rearrange("p (i n) -> p i n", n=2048),
                               in_=w["ff1w"].rearrange("(i p) n -> p i n", p=128)), 16)
            qG.inc(e.dma_start(out=ffb[:, 0:16], in_=w["ff1b2"]), 16)
            qG.inc(e.dma_start(out=ffb[:, 16:24], in_=w["ff2b2"]), 16)
        ff_tok = qG.n
        wff2_tok = [0, 0]
        wff2_rd = [0, 0]

        src4 = xin.rearrange("(i p) (n f) -> p i n f", p=128, f=CH)
        dst4 = xout.rearrange("(i p) (n f) -> p i n f", p=128, f=CH)

        for ci in range(NCH):
            @p.b.sync
            def _(e, ci=ci):
                if ci > 0:
                    wait(e, (qS, qS.n), (vt, vt.n))
                else:
                    wait(e, (tv, tv.n), (vs, vs.n), (sv, sv.n), (apt, apt.n))
                qS.inc(e.dma_start(
                    out=xc2[:].rearrange("p (i f) -> p i f", f=CH),
                    in_=src4[:, :, ci, :]), 16)
            x_tok = qS.n

            for ii in range(16):
                @p.b.tensor
                def _(e, ci=ci, ii=ii):
                    wait(e, (qG, ff_tok), (sv, hsil_tok[ii % 2]))
                    if ii == 0:
                        wait(e, (apt, apt.n))
                    for i in range(DT):
                        m = e.matmul(
                            out=pH2[ii % 2][:],
                            lhsT=wff1[:, i * 2048 + ii * 128:
                                      i * 2048 + (ii + 1) * 128],
                            rhs=aT[:, i * T + ci * CH: i * T + (ci + 1) * CH],
                            start=(i == 0), stop=(i == DT - 1))
                    tv.inc(m)

                @p.b.scalar
                def _(e, ci=ci, ii=ii):
                    wait(e, (tv, tv.n), (qG, ff_tok))
                    sv.inc(e.activation(out=hT[:, ii * CH:(ii + 1) * CH],
                                        in_=pH2[ii % 2][:], func=AF.Relu,
                                        bias=ffb[:, ii:ii + 1], scale=1.0))
                hsil_tok[ii % 2] = p.sem("sv").n

            for jd in range(DT):
                jp = jd % 2

                @p.b.gpsimd
                def _(e, ci=ci, jd=jd, jp=jp):
                    import os as _o
                    if _o.environ.get("L0A_SERW"):
                        wait(e, (tv, tv.n))
                    elif wff2_rd[jp]:
                        wait(e, (tv, wff2_rd[jp]))
                    qG.inc(e.dma_start(
                        out=wff2[jp][:],
                        in_=w["ff2wp"][jd]), 16)
                wff2_tok[jp] = qG.n

                @p.b.tensor
                def _(e, ci=ci, jd=jd, jp=jp, wtok=wff2_tok[jp]):
                    wait(e, (sv, sv.n))
                    wait(e, (vs, fev_tok[jd % 2]))
                    wait(e, (qG, wtok))
                    for ii in range(16):
                        m = e.matmul(
                            out=pF2[jd % 2][:],
                            lhsT=wff2[jp][:, ii * 128:(ii + 1) * 128],
                            rhs=hT[:, ii * CH:(ii + 1) * CH],
                            start=(ii == 0), stop=(ii == 15))
                    tv.inc(m)
                wff2_rd[jp] = p.sem("tv").n

                @p.b.vector
                def _(e, ci=ci, jd=jd, x_tok=x_tok):
                    wait(e, (tv, tv.n), (qS, x_tok))
                    vs.inc(e.scalar_tensor_tensor(
                        out=zA[:, jd * CH:(jd + 1) * CH], in0=pF2[jd % 2][:],
                        scalar=ffb[:, 16 + jd:17 + jd],
                        in1=aT[:, jd * T + ci * CH: jd * T + (ci + 1) * CH],
                        op0=ALU.add, op1=ALU.add))
                fev_tok[jd % 2] = p.sem("vs").n

            if _os.environ.get("L0A_D2FF"):
                if _os.environ.get("L0A_DUMPH") and ci == NCH - 1:
                    @p.b.gpsimd
                    def _(e, ci=ci):
                        wait(e, (sv, sv.n))
                        qG.inc(e.dma_start(out=xout[0:128, :],
                                           in_=hT[:, 0:T]), 16)
                        qG.inc(e.dma_start(out=xout[128:256, :],
                                           in_=aT[:, 0:T]), 16)
                    continue

                @p.b.sync
                def _(e, ci=ci):
                    wait(e, (vs, vs.n))
                    qS.inc(e.dma_start(out=dst4[:, :, ci, :],
                                       in_=zA[:].rearrange("p (i f) -> p i f",
                                                           f=CH)), 16)
                continue
            emit_ln_chunk(p, c, banks,
                          lambda i: zA[:, i * CH:(i + 1) * CH],
                          lambda i: zB[:, i * CH:(i + 1) * CH],
                          gb=(ln2g, ln2b))
            emit_ln_chunk(p, c, banks,
                          lambda i: zB[:, i * CH:(i + 1) * CH],
                          lambda i: zB[:, i * CH:(i + 1) * CH],
                          gb=None)

            @p.b.vector
            def _(e, ci=ci, x_tok=x_tok):
                wait(e, (qS, x_tok), (apt, apt.n))
                for i in range(DT):
                    m = e.scalar_tensor_tensor(out=zA[:, i * CH:(i + 1) * CH],
                                               in0=zB[:, i * CH:(i + 1) * CH],
                                               scalar=1.0,
                                               in1=xc2[:, i * CH:(i + 1) * CH],
                                               op0=ALU.mult, op1=ALU.add)
                vt.inc(m)

            @p.b.sync
            def _(e, ci=ci):
                wait(e, (vt, vt.n))
                qS.inc(e.dma_start(out=dst4[:, :, ci, :],
                                   in_=zA[:].rearrange("p (i f) -> p i f", f=CH)),
                       16)

    @p.b.sync
    def _(e):
        wait(e, (qS, qS.n))



def _bias_cols(b, ncols):
    return np.ascontiguousarray(np.asarray(b, dtype=np.float32)
                                .reshape(ncols, 128).T)


def pack_l0a(inp, pref="L0_attn_"):
    g = lambda n: np.asarray(inp[pref + n], dtype=np.float32)
    w = {}
    for nm, key in (("wq", "q_proj"), ("wk", "k_proj"), ("wv", "v_proj")):
        w[nm] = np.ascontiguousarray(g(key).transpose(1, 0, 2).reshape(D, D))
    for nm, key in (("qb2", "q_bias"), ("kb2", "k_bias"), ("vb2", "v_bias")):
        w[nm] = np.ascontiguousarray(
            g(key).reshape(8, 2, 64).transpose(1, 2, 0).reshape(128, 8))
    for nm, key in (("ln1g", "ln1_g"), ("ln1b", "ln1_b"),
                    ("ln2g", "ln2_g"), ("ln2b", "ln2_b")):
        w[nm] = g(key).reshape(1, D)
    w["ff1w"] = g("ff1_w")
    w["ff2wp"] = np.ascontiguousarray(
        g("ff2_w").reshape(16, 128, 8, 128).transpose(2, 1, 0, 3)
        .reshape(8, 128, 16 * 128))
    w["ff1b2"] = _bias_cols(g("ff1_b"), 16)
    w["ff2b2"] = _bias_cols(g("ff2_b"), 8)
    causal = np.tril(np.ones((128, 128), dtype=np.float32))
    gen = np.concatenate([np.ones((128, 128), np.float32), causal], axis=1)
    q0 = np.concatenate([causal, np.zeros((128, 128), np.float32)], axis=1)
    w["mask4_g"] = np.concatenate([gen] * 4, axis=1).astype(ml_dtypes.bfloat16)
    w["mask4_0"] = np.concatenate([q0, gen, gen, gen],
                                  axis=1).astype(ml_dtypes.bfloat16)
    return w


L0A_SHAPES = {
    "wq": ([D, D], F32), "wk": ([D, D], F32), "wv": ([D, D], F32),
    "qb2": ([128, 8], F32), "kb2": ([128, 8], F32), "vb2": ([128, 8], F32),
    "ln1g": ([1, D], F32), "ln1b": ([1, D], F32),
    "ln2g": ([1, D], F32), "ln2b": ([1, D], F32),
    "ff1w": ([D, 2048], F32), "ff2wp": ([8, 128, 2048], F32),
    "ff1b2": ([128, 16], F32), "ff2b2": ([128, 8], F32),
    "mask4_0": ([128, 1024], BF16), "mask4_g": ([128, 1024], BF16),
}


def declare_weights(nc, shapes, prefix=""):
    return {k: nc.dram_tensor(prefix + k, sh, dt, kind="ExternalInput").ap()
            for k, (sh, dt) in shapes.items()}


def build_l0a_test(T, phases=("A", "BC", "D1", "D2")):
    nc = new_bass()
    xin = nc.dram_tensor("xin", [D, T], F32, kind="ExternalInput").ap()
    cident = nc.dram_tensor("cident", [128, 128], BF16, kind="ExternalInput").ap()
    xo = nc.dram_tensor("xout", [D, T], F32, kind="ExternalOutput").ap()
    w = declare_weights(nc, L0A_SHAPES)
    with ExitStack() as st:
        block = st.enter_context(nc.Block())
        p = P(nc, block, st)
        c = emit_consts(p, cident)
        emit_l0_attn(p, c, T, xin, xo, w, phases=phases)
    return nc



def emit_sandwich_tail(p, c, T, blk_hbm, xin, xout, tag):
    nc = p.nc
    NCH = T // CH
    qS = p.sem("qS")
    vt = p.sem("vt")
    apt = p.sem("apt")
    with ExitStack() as ph:
        banks = make_ln_banks(p, ph, tag=tag + "_ln")
        bc = [ph.enter_context(nc.sbuf_tensor(tag + "_bc%d" % i, [128, DT * CH], F32))
              for i in range(2)]
        xc = ph.enter_context(nc.sbuf_tensor(tag + "_xc", [128, DT * CH], F32))
        zB = ph.enter_context(nc.sbuf_tensor(tag + "_zB", [128, DT * CH], F32))
        blk4 = blk_hbm.rearrange("(i p) (n f) -> p i n f", p=128, f=CH)
        src4 = xin.rearrange("(i p) (n f) -> p i n f", p=128, f=CH)
        dst4 = xout.rearrange("(i p) (n f) -> p i n f", p=128, f=CH)
        qG = p.sem("qG")
        tv = p.sem("tv")
        vs = p.sem("vs")
        sv = p.sem("sv")
        for ci in range(NCH):
            @p.b.sync
            def _(e, ci=ci):
                if ci < 2:
                    wait(e, (qG, qG.n), (tv, tv.n), (vs, vs.n), (sv, sv.n),
                         (apt, apt.n))
                else:
                    wait(e, (apt, apt.n))
                qS.inc(e.dma_start(
                    out=bc[ci % 2][:].rearrange("p (i f) -> p i f", f=CH),
                    in_=blk4[:, :, ci, :]), 16)
                if ci > 0:
                    wait(e, (vt, vt.n))
                qS.inc(e.dma_start(
                    out=xc[:].rearrange("p (i f) -> p i f", f=CH),
                    in_=src4[:, :, ci, :]), 16)
            ld_tok = qS.n

            @p.b.vector
            def _(e, tok=ld_tok):
                wait(e, (qS, tok))
            emit_ln_chunk(p, c, banks,
                          lambda i, ci=ci: bc[ci % 2][:, i * CH:(i + 1) * CH],
                          lambda i: zB[:, i * CH:(i + 1) * CH], gb=None)

            @p.b.vector
            def _(e, ci=ci):
                wait(e, (apt, apt.n))
                for i in range(DT):
                    m = e.scalar_tensor_tensor(
                        out=zB[:, i * CH:(i + 1) * CH],
                        in0=zB[:, i * CH:(i + 1) * CH], scalar=1.0,
                        in1=xc[:, i * CH:(i + 1) * CH],
                        op0=ALU.mult, op1=ALU.add)
                vt.inc(m)

            @p.b.sync
            def _(e, ci=ci):
                wait(e, (vt, vt.n))
                qS.inc(e.dma_start(out=dst4[:, :, ci, :],
                                   in_=zB[:].rearrange("p (i f) -> p i f", f=CH)),
                       16)
    @p.b.sync
    def _(e):
        wait(e, (qS, qS.n))



def emit_l0_ffn(p, c, T, xin, xout, w, macc):
    nc = p.nc
    NCH = T // CH
    NQ = T // 128
    qS = p.sem("qS")
    qG = p.sem("qG")
    vt = p.sem("vt")
    tv = p.sem("tv")
    vs = p.sem("vs")
    sv = p.sem("sv")
    apt = p.sem("apt")

    xfT = p.sbuf("l0f_xfT", [128, DT * T], BF16)
    wT = p.sbuf("l0f_wT", [8, T], BF16)
    bia = p.sbuf("l0f_bia", [128, 256 + 32 + 2 + 8], F32)
    be1c = bia[:, 0:256]
    bs1c = bia[:, 256:288]
    br1c = bia[:, 288:290]
    bs2c = bia[:, 290:298]
    br2c = p.sbuf("l0f_br2c", [8, 1], F32)
    be2a = p.sbuf("l0f_be2a", [8, 1024], BF16)
    csel8 = p.sbuf("l0f_csel8", [8, 8 * 128], BF16)

    @p.b.gpsimd
    def _(e):
        qG.inc(e.dma_start(out=be1c, in_=w["be1c"]), 16)
        qG.inc(e.dma_start(out=bs1c, in_=w["bs1c"]), 16)
        qG.inc(e.dma_start(out=br1c, in_=w["br1c"]), 16)
        qG.inc(e.dma_start(out=bs2c, in_=w["bs2c"]), 16)
        qG.inc(e.dma_start(out=br2c[:], in_=w["br2c"]), 16)
        qG.inc(e.dma_start(out=be2a[:], in_=w["be2a"]), 16)
        qG.inc(e.dma_start(out=csel8[:], in_=w["csel8"]), 16)
    smalls_tok = qG.n

    with ExitStack() as ph:
        banks = make_ln_banks(p, ph, tag="l0f_ln")
        xc = [ph.enter_context(nc.sbuf_tensor("l0f_xc%d" % i, [128, DT * CH], F32))
              for i in range(2)]
        emit_preln(p, c, T, xin, xfT, banks, gb=None, xc=xc)
    xf_tok = p.sem("apt").n

    with ExitStack() as ph:
        ec = ph.enter_context
        wr1f = ec(nc.sbuf_tensor("l0f_wr1f", [128, DT * 256], F32))
        wr2f = ec(nc.sbuf_tensor("l0f_wr2f", [128, 16], F32))
        xc3 = [ec(nc.sbuf_tensor("l0f_xc3%d" % i, [128, DT * CH], F32))
               for i in range(2)]
        xf32 = ec(nc.sbuf_tensor("l0f_xf32", [128, DT * CH], F32))
        globals()["_dbg_xf32"] = xf32
        r1c = ec(nc.sbuf_tensor("l0f_r1c", [128, 2 * CH], F32))
        globals()["_dbg_r1c"] = r1c
        lgT = ec(nc.sbuf_tensor("l0f_lgT", [8, T], F32))
        lgt = ec(nc.sbuf_tensor("l0f_lgt", [128, NQ * 8], F32))
        globals()['_dbg_lgt'] = lgt
        gsc = ec(nc.sbuf_tensor("l0f_gsc", [128, 40], F32))
        gates = gsc[:, 0:8]
        top8 = gsc[:, 8:16]
        ind1 = gsc[:, 16:24]
        ind2 = gsc[:, 24:32]
        nm = gsc[:, 32:40]
        wtok = ec(nc.sbuf_tensor("l0f_wtok", [128, 8], BF16))
        banksf = {
            "ps1": ec(nc.psum_tensor("l0f_fps1", [1, CH], F32)),
            "ps2": ec(nc.psum_tensor("l0f_fps2", [1, CH], F32)),
            "pA": ec(nc.psum_tensor("l0f_fpA", [128, CH], F32)),
            "pB": ec(nc.psum_tensor("l0f_fpB", [128, CH], F32)),
            "xcb": ec(nc.sbuf_tensor("l0f_fxcb", [128, DT * CH], F32)),
            "sq": ec(nc.sbuf_tensor("l0f_fsq", [128, DT * CH], F32)),
            "rows": ec(nc.sbuf_tensor("l0f_frows", [1, 7 * CH], F32)),
            "rowsb": ec(nc.sbuf_tensor("l0f_frowsb", [1, 2 * CH], BF16)),
            "tmp": ec(nc.sbuf_tensor("l0f_ftmp", [128, CH], F32)),
        }
        pR = ec(nc.psum_tensor("l0f_pR", [128, CH], F32))
        pL = ec(nc.psum_tensor("l0f_pL", [8, CH], F32))
        pX = ec(nc.psum_tensor("l0f_pX", [128, 8], F32))
        pXb = ec(nc.psum_tensor("l0f_pXb", [8, 128], BF16))

        @p.b.sync
        def _(e):
            wait(e, (apt, xf_tok))
            qS.inc(e.dma_start(out=wr1f[:].rearrange("p (i n) -> p i n", n=256),
                               in_=w["Wr1"].rearrange("(i p) n -> p i n", p=128)),
                   16)
            qS.inc(e.dma_start(out=wr2f[:].rearrange("p (i n) -> p i n", n=8),
                               in_=w["Wr2"].rearrange("(i p) n -> p i n", p=128)),
                   16)
        r_tok = qS.n

        src4r = xin.rearrange("(i p) (n f) -> p i n f", p=128, f=CH)
        for ci in range(NCH):
            @p.b.sync
            def _(e, ci=ci):
                if ci >= 2:
                    wait(e, (tv, tv.n))
                else:
                    wait(e, (apt, xf_tok))
                qS.inc(e.dma_start(
                    out=xc3[ci % 2][:].rearrange("p (i f) -> p i f", f=CH),
                    in_=src4r[:, :, ci, :]), 16)
            ld_tok = qS.n

            @p.b.vector
            def _(e, tok=ld_tok, ci=ci):
                wait(e, (qS, tok))
                if ci > 0:
                    wait(e, (tv, tv.n))

            emit_ln_chunk(p, c, banksf,
                          lambda i, ci=ci: xc3[ci % 2][:, i * CH:(i + 1) * CH],
                          lambda i: xf32[:, i * CH:(i + 1) * CH],
                          gb=None, f32=True)

            for it in range(2):
                @p.b.tensor
                def _(e, it=it, ci=ci):
                    wait(e, (apt, apt.n), (qS, r_tok), (sv, sv.n))
                    for i in range(DT):
                        m = e.matmul(
                            out=pR[:],
                            lhsT=wr1f[:, i * 256 + it * 128:
                                      i * 256 + (it + 1) * 128],
                            rhs=xf32[:, i * CH:(i + 1) * CH],
                            start=(i == 0), stop=(i == DT - 1))
                    tv.inc(m)

                @p.b.scalar
                def _(e, it=it, ci=ci):
                    wait(e, (tv, tv.n), (qG, smalls_tok))
                    sv.inc(e.activation(out=r1c[:, it * CH:(it + 1) * CH],
                                        in_=pR[:], func=AF.Relu,
                                        bias=br1c[:, it:it + 1], scale=1.0))

            @p.b.tensor
            def _(e, ci=ci):
                wait(e, (sv, sv.n), (vs, vs.n))
                for it in range(2):
                    m = e.matmul(out=pL[:],
                                 lhsT=wr2f[:, it * 8:(it + 1) * 8],
                                 rhs=r1c[:, it * CH:(it + 1) * CH],
                                 start=(it == 0), stop=(it == 1))
                tv.inc(m)

            @p.b.vector
            def _(e, ci=ci):
                wait(e, (tv, tv.n))
                vs.inc(e.tensor_scalar(out=lgT[:, ci * CH:(ci + 1) * CH],
                                       in0=pL[:], scalar1=br2c[:, 0:1],
                                       scalar2=None, op0=ALU.add))

        for tt in range(NQ):
            @p.b.tensor
            def _(e, tt=tt):
                wait(e, (vs, vs.n))
                if tt >= 1:
                    wait(e, (apt, apt.n))
                tv.inc(e.transpose(out=pX[:], in_=lgT[:, tt * 128:(tt + 1) * 128],
                                   identity=c["identf"][0:8, 0:8]))

            @p.b.vector
            def _(e, tt=tt):
                wait(e, (tv, tv.n))
                apt.inc(e.tensor_copy(out=lgt[:, tt * 8:(tt + 1) * 8], in_=pX[:]))

        for tt in range(NQ):
            @p.b.vector
            def _(e, tt=tt):
                wait(e, (apt, apt.n))
                sl = lgt[:, tt * 8:(tt + 1) * 8]
                e.reduce_max(out=nm[:, 0:1], in_=sl, axis=AX, negate=True)
                vs.inc(e.drain())

            @p.b.scalar
            def _(e, tt=tt):
                wait(e, (vs, vs.n))
                sv.inc(e.activation(out=gates, in_=lgt[:, tt * 8:(tt + 1) * 8],
                                    func=AF.Exp, bias=nm[:, 0:1], scale=1.0))

            @p.b.vector
            def _(e, tt=tt):
                wait(e, (sv, sv.n))
                e.reduce_sum(out=nm[:, 1:2], in_=gates, axis=AX)
                e.drain()
                e.reciprocal(out=nm[:, 2:3], in_=nm[:, 1:2])
                vs.inc(e.drain())

            @p.b.vector
            def _(e, tt=tt):
                wait(e, (vs, vs.n))
                e.tensor_scalar(out=gates, in0=gates, scalar1=nm[:, 2:3],
                                scalar2=None, op0=ALU.mult)
                e.drain()
                e.max(out=top8, in_=gates)
                vs.inc(e.drain())

            @p.b.scalar
            def _(e, tt=tt):
                wait(e, (vs, vs.n))
                e.activation(out=nm[:, 3:4], in_=top8[:, 0:1], func=AF.Exp,
                             scale=0.5)
                sv.inc(e.activation(out=nm[:, 4:5], in_=top8[:, 1:2], func=AF.Exp,
                                    scale=0.5))

            @p.b.vector
            def _(e, tt=tt):
                wait(e, (sv, sv.n))
                e.tensor_add(out=nm[:, 5:6], in0=nm[:, 3:4], in1=nm[:, 4:5])
                e.drain()
                e.reciprocal(out=nm[:, 6:7], in_=nm[:, 5:6])
                vs.inc(e.drain())

            @p.b.vector
            def _(e, tt=tt):
                wait(e, (vs, vs.n))
                e.tensor_scalar(out=nm[:, 3:4], in0=nm[:, 3:4],
                                scalar1=nm[:, 6:7], scalar2=None, op0=ALU.mult)
                e.tensor_scalar(out=nm[:, 4:5], in0=nm[:, 4:5],
                                scalar1=nm[:, 6:7], scalar2=None, op0=ALU.mult)
                e.tensor_scalar(out=ind1, in0=gates, scalar1=top8[:, 0:1],
                                scalar2=None, op0=ALU.is_equal)
                e.tensor_scalar(out=ind2, in0=gates, scalar1=top8[:, 1:2],
                                scalar2=None, op0=ALU.is_equal)
                e.scalar_tensor_tensor(out=ind2, in0=ind1, scalar=-1.0, in1=ind2,
                                       op0=ALU.mult, op1=ALU.add)
                e.tensor_scalar_max(ind2, ind2, 0.0)
                vs.inc(e.drain())

            @p.b.vector
            def _(e, tt=tt):
                wait(e, (vs, vs.n))
                e.tensor_scalar(out=ind2, in0=ind2, scalar1=nm[:, 4:5],
                                scalar2=None, op0=ALU.mult)
                e.scalar_tensor_tensor(out=wtok[:], in0=ind1, scalar=nm[:, 3:4],
                                       in1=ind2, op0=ALU.mult, op1=ALU.add)
                vs.inc(e.drain())

            @p.b.tensor
            def _(e, tt=tt):
                wait(e, (vs, vs.n))
                if tt >= 1:
                    wait(e, (apt, apt.n))
                tv.inc(e.transpose(out=pXb[0:8, 0:128], in_=wtok[:],
                                   identity=c["identity"][:]))

            @p.b.vector
            def _(e, tt=tt):
                wait(e, (tv, tv.n))
                apt.inc(e.tensor_copy(out=wT[0:8, tt * 128:(tt + 1) * 128],
                                      in_=pXb[0:8, 0:128]))

        import os as _os
        if _os.environ.get("L0F_DBGR"):
            dbg = nc.dram_tensor("dbg", [512, 64], F32,
                                 kind="ExternalOutput").ap()

            @p.b.gpsimd
            def _(e):
                wait(e, (apt, apt.n), (tv, tv.n), (vs, vs.n), (sv, sv.n))
                qG.inc(e.dma_start(out=dbg[0:128, 0:NQ * 8], in_=lgt[:]), 16)
                qG.inc(e.dma_start(out=dbg[128:136, 0:NQ * 8],
                                   in_=wT[0:8, 0:NQ * 8]), 16)
                qG.inc(e.dma_start(out=dbg[256:384, 0:40], in_=gsc[:]), 16)
                qG.inc(e.dma_start(out=dbg[384:512, 0:8], in_=wtok[:]), 16)

    with ExitStack() as ph:
        ec = ph.enter_context
        wE1 = [ec(nc.sbuf_tensor("l0f_wE1_%d" % i, [128, DT * 1024], BF16))
               for i in range(2)]
        wE2 = [ec(nc.sbuf_tensor("l0f_wE2_%d" % i, [128, DT * 1024], BF16))
               for i in range(2)]
        hq = ec(nc.sbuf_tensor("l0f_hq", [128, DT * CH], BF16))
        stage = [ec(nc.sbuf_tensor("l0f_st%d" % i, [128, DT * CH], F32))
                 for i in range(2)]
        wbc = ec(nc.sbuf_tensor("l0f_wbc", [128, T], BF16))
        pH = [ec(nc.psum_tensor("l0f_pH%d" % i, [128, CH], F32)) for i in range(4)]
        pF2 = [ec(nc.psum_tensor("l0f_pF%d" % i, [128, CH], F32)) for i in range(2)]
        pW = ec(nc.psum_tensor("l0f_pW", [128, CH], F32))
        silu_tok = [0, 0]
        evac_tok = [0, 0]

        macc4 = macc.rearrange("(i p) (n f) -> p i n f", p=128, f=CH)
        region_tok = [0] * NCH
        step_tok = []
        stage_tok = [0, 0]

        for ee in range(9):
            is_sh = (ee == 0)
            exp_i = ee - 1

            if not is_sh:
                for ci in range(NCH):
                    @p.b.tensor
                    def _(e, ci=ci, exp_i=exp_i):
                        wait(e, (apt, apt.n), (vs, vs.n))
                        tv.inc(e.matmul(out=pW[:],
                                        lhsT=csel8[:, exp_i * 128:
                                                   (exp_i + 1) * 128],
                                        rhs=wT[0:8, ci * CH:(ci + 1) * CH],
                                        start=True, stop=True))

                    @p.b.vector
                    def _(e, ci=ci):
                        wait(e, (tv, tv.n))
                        apt.inc(e.tensor_copy(out=wbc[:, ci * CH:(ci + 1) * CH],
                                              in_=pW[:]))

            for Iq in range(4):
                step = ee * 4 + Iq
                par = step % 2

                @p.b.gpsimd
                def _(e, Iq=Iq, par=par, is_sh=is_sh, exp_i=exp_i, step=step):
                    if step >= 2:
                        wait(e, (tv, step_tok[step - 2]))
                    else:
                        wait(e, (apt, apt.n), (tv, tv.n), (vs, vs.n), (sv, sv.n))
                    if is_sh:
                        s1 = w["Ws1"].rearrange("(i p) n -> p i n", p=128)[
                            :, :, Iq * 1024:(Iq + 1) * 1024]
                        s2 = w["Ws2"][Iq * 1024:(Iq + 1) * 1024].rearrange(
                            "(i p) n -> p i n", p=128)
                    else:
                        s1 = w["We1"][exp_i].rearrange("(i p) n -> p i n", p=128)[
                            :, :, Iq * 1024:(Iq + 1) * 1024]
                        s2 = w["We2"][exp_i][Iq * 1024:(Iq + 1) * 1024].rearrange(
                            "(i p) n -> p i n", p=128)
                    qG.inc(e.dma_start(
                        out=wE1[par][:].rearrange("p (i n) -> p i n", n=1024),
                        in_=s1), 16)
                    qG.inc(e.dma_start(
                        out=wE2[par][:].rearrange("p (i n) -> p i n", n=1024),
                        in_=s2), 16)
                w_tok = qG.n

                for tc in range(NCH):
                    spar = (step * NCH + tc) % 2
                    for half in range(2):
                        @p.b.tensor
                        def _(e, tc=tc, half=half, par=par, w_tok=w_tok,
                              step=step):
                            wait(e, (qG, w_tok), (sv, silu_tok[0]))
                            if step == 0 and tc == 0 and half == 0:
                                wait(e, (apt, apt.n), (vs, vs.n))
                            for ii in range(half * 4, half * 4 + 4):
                                for i in range(DT):
                                    m = e.matmul(
                                        out=pH[ii % 4][:],
                                        lhsT=wE1[par][:, i * 1024 + ii * 128:
                                                      i * 1024 + (ii + 1) * 128],
                                        rhs=xfT[:, i * T + tc * CH:
                                                i * T + (tc + 1) * CH],
                                        start=(i == 0), stop=(i == DT - 1))
                                tv.inc(m)

                        @p.b.scalar
                        def _(e, Iq=Iq, half=half, is_sh=is_sh, exp_i=exp_i,
                              base_tv=p.sem("tv").n):
                            wait(e, (qG, smalls_tok))
                            for k, ii in enumerate(range(half * 4, half * 4 + 4)):
                                wait(e, (tv, base_tv - 3 + k))
                                if is_sh:
                                    bcol = bs1c[:, Iq * 8 + ii:Iq * 8 + ii + 1]
                                else:
                                    bcol = be1c[:, exp_i * 32 + Iq * 8 + ii:
                                                exp_i * 32 + Iq * 8 + ii + 1]
                                sv.inc(e.activation(
                                    out=hq[:, ii * CH:(ii + 1) * CH],
                                    in_=pH[ii % 4][:], func=AF.Silu,
                                    bias=bcol, scale=1.0))
                        silu_tok[0] = p.sem("sv").n

                    for jp2 in range(4):
                        extra = is_sh and Iq == 0

                        @p.b.tensor
                        def _(e, tc=tc, jp2=jp2, par=par, extra=extra):
                            wait(e, (sv, sv.n), (vs, evac_tok[0]))
                            for jd in (jp2 * 2, jp2 * 2 + 1):
                                pf_ = pF2[jd % 2]
                                for ii in range(8):
                                    m = e.matmul(
                                        out=pf_[:],
                                        lhsT=wE2[par][:, ii * 1024 + jd * 128:
                                                      ii * 1024 + (jd + 1) * 128],
                                        rhs=hq[:, ii * CH:(ii + 1) * CH],
                                        start=(ii == 0),
                                        stop=(ii == 7 and not extra))
                                if extra:
                                    m = e.matmul(
                                        out=pf_[:],
                                        lhsT=be2a[:, jd * 128:(jd + 1) * 128],
                                        rhs=wT[:, tc * CH:(tc + 1) * CH],
                                        start=False, stop=True)
                                tv.inc(m)

                        @p.b.vector
                        def _(e, tc=tc, jp2=jp2, spar=spar, is_sh=is_sh, Iq=Iq,
                              base_tv=p.sem("tv").n):
                            wait(e, (qG, stage_tok[spar]))
                            for k, jd in enumerate((jp2 * 2, jp2 * 2 + 1)):
                                wait(e, (tv, base_tv - 1 + k))
                                pf_ = pF2[jd % 2]
                                dst = stage[spar][:, jd * CH:(jd + 1) * CH]
                                if is_sh and Iq == 0:
                                    m = e.tensor_scalar(
                                        out=dst, in0=pf_[:],
                                        scalar1=bs2c[:, jd:jd + 1], scalar2=None,
                                        op0=ALU.add)
                                elif is_sh:
                                    m = e.tensor_copy(out=dst, in_=pf_[:])
                                else:
                                    m = e.tensor_tensor(
                                        out=dst, in0=wbc[:, tc * CH:(tc + 1) * CH],
                                        in1=pf_[:], op=ALU.mult)
                                vs.inc(m)
                        evac_tok[0] = p.sem("vs").n

                    @p.b.gpsimd
                    def _(e, tc=tc, spar=spar, step=step):
                        wait(e, (vs, vs.n), (qG, region_tok[tc]))
                        qG.inc(e.dma_start(
                            out=macc4[:, :, tc, :],
                            in_=stage[spar][:].rearrange("p (i f) -> p i f", f=CH),
                            accum_op=(ALU.bypass if step == 0 else ALU.add)), 16)
                    region_tok[tc] = qG.n
                    stage_tok[spar] = qG.n
                step_tok.append(p.sem("tv").n)

        @p.b.gpsimd
        def _(e):
            wait(e, (qG, qG.n))

    import os
    if os.environ.get("L0F_DUMPM"):
        @p.b.gpsimd
        def _(e):
            wait(e, (qG, qG.n), (tv, tv.n), (vs, vs.n), (sv, sv.n), (apt, apt.n))
            qG.inc(e.dma_start(out=xout, in_=macc), 16)

        @p.b.gpsimd
        def _(e):
            wait(e, (qG, qG.n))
        return
    if os.environ.get("L0F_DUMPXF"):
        @p.b.gpsimd
        def _(e):
            wait(e, (qG, qG.n), (tv, tv.n), (vs, vs.n), (sv, sv.n), (apt, apt.n))
            qG.inc(e.dma_start(out=xout[0:128, 0:CH], in_=_dbg_xf32[:, 0:CH]), 16)
            qG.inc(e.dma_start(out=xout[128:256, 0:CH], in_=_dbg_r1c[:, 0:CH]), 16)

        @p.b.gpsimd
        def _(e):
            wait(e, (qG, qG.n))
        return
    if os.environ.get("L0F_DUMPL"):
        @p.b.gpsimd
        def _(e):
            wait(e, (qG, qG.n), (tv, tv.n), (vs, vs.n), (sv, sv.n), (apt, apt.n))
            qG.inc(e.dma_start(out=xout[0:128, 0:NQ * 8], in_=_dbg_lgt[:]), 16)

        @p.b.gpsimd
        def _(e):
            wait(e, (qG, qG.n))
        return
    if os.environ.get("L0F_DUMPW"):
        @p.b.gpsimd
        def _(e):
            wait(e, (qG, qG.n), (tv, tv.n), (vs, vs.n), (sv, sv.n), (apt, apt.n))
            qG.inc(e.dma_start(out=xout[0:8, :], in_=wT[:]), 16)

        @p.b.gpsimd
        def _(e):
            wait(e, (qG, qG.n))
        return
    emit_sandwich_tail(p, c, T, macc, xin, xout, tag="l0f_tail")


L0F_SHAPES = {
    "Wr1": ([D, 256], F32), "br1c": ([128, 2], F32),
    "Wr2": ([256, 8], F32), "br2c": ([8, 1], F32),
    "Ws1": ([D, 4096], F32), "bs1c": ([128, 32], F32),
    "Ws2": ([4096, D], F32), "bs2c": ([128, 8], F32),
    "We1": ([8, D, 4096], F32), "be1c": ([128, 256], F32),
    "We2": ([8, 4096, D], F32), "be2a": ([8, D], F32),
    "csel8": ([8, 8 * 128], BF16),
}


def pack_l0f(inp, pref="L0_ffn_"):
    g = lambda n: np.asarray(inp[pref + n], dtype=np.float32)
    w = {}
    w["Wr1"] = g("Wr1")
    w["br1c"] = _bias_cols(g("br1"), 2)
    w["Wr2"] = g("Wr2")
    w["br2c"] = g("br2").reshape(8, 1)
    w["Ws1"] = g("Ws1")
    w["bs1c"] = _bias_cols(g("bs1"), 32)
    w["Ws2"] = g("Ws2")
    w["bs2c"] = _bias_cols(g("bs2"), 8)
    w["We1"] = g("We1")
    w["be1c"] = np.ascontiguousarray(
        g("be1").reshape(8, 32, 128).transpose(2, 0, 1).reshape(128, 256))
    w["We2"] = g("We2")
    w["be2a"] = g("be2")
    sel = np.zeros((8, 8, 128), dtype=np.float32)
    for e in range(8):
        sel[e, e, :] = 1.0
    w["csel8"] = sel.reshape(8, 8 * 128).astype(ml_dtypes.bfloat16)
    return w


def build_l0f_test(T):
    nc = new_bass()
    xin = nc.dram_tensor("xin", [D, T], F32, kind="ExternalInput").ap()
    cident = nc.dram_tensor("cident", [128, 128], BF16, kind="ExternalInput").ap()
    cidentf = nc.dram_tensor("cidentf", [128, 128], F32, kind="ExternalInput").ap()
    xo = nc.dram_tensor("xout", [D, T], F32, kind="ExternalOutput").ap()
    macc = nc.dram_tensor("macc", [D, T], F32, kind="Internal").ap()
    w = declare_weights(nc, L0F_SHAPES)
    with ExitStack() as st:
        block = st.enter_context(nc.Block())
        p = P(nc, block, st)
        c = emit_consts(p, cident)
        c["identf"] = p.sbuf("c_identf", [128, 128], F32)
        qS = p.sem("qS")

        @p.b.sync
        def _(e):
            qS.inc(e.dma_start(out=c["identf"][:], in_=cidentf), 16)

        emit_l0_ffn(p, c, T, xin, xo, w, macc)
    return nc



def emit_l1_attn(p, c, T, xin, xout, w, macc):
    nc = p.nc
    NCH = T // CH
    NQ = T // 128
    qS = p.sem("qS")
    qG = p.sem("qG")
    vt = p.sem("vt")
    tv = p.sem("tv")
    vs = p.sem("vs")
    sv = p.sem("sv")
    apt = p.sem("apt")

    hlf65 = p.sbuf("l1a_hlf65", [128, 65], BF16)
    rf2 = p.sbuf("l1a_rf2", [128, 64], BF16)
    boutc = p.sbuf("l1a_boutc", [128, 8], F32)
    ph_ao = ExitStack()
    aoT = ph_ao.enter_context(nc.sbuf_tensor("l1a_aoT", [128, DT * T], BF16))
    ph_xf = ExitStack()
    xfT = ph_xf.enter_context(nc.sbuf_tensor("l1a_xfT", [128, DT * T], BF16))

    @p.b.gpsimd
    def _(e):
        qG.inc(e.dma_start(out=rf2[0:64, :], in_=w["rfT"]), 16)
        qG.inc(e.dma_start(out=rf2[64:128, :], in_=w["rfT"]), 16)
        qG.inc(e.dma_start(out=boutc[:], in_=w["boutc"]), 16)
    smalls_tok = qG.n

    @p.b.vector
    def _(e):
        e.memset(hlf65[:], 0.0)
        e.memset(hlf65[0:64, 0:1], 1.0)
        vt.inc(e.memset(hlf65[64:128, 64:65], 1.0))

    with ExitStack() as ph:
        banks = make_ln_banks(p, ph, tag="l1a_ln")
        xc = [ph.enter_context(nc.sbuf_tensor("l1a_xc%d" % i, [128, DT * CH], F32))
              for i in range(2)]
        emit_preln(p, c, T, xin, xfT, banks, gb=None, xc=xc)
    xf_tok = p.sem("apt").n

    with ExitStack() as ph:
        ec = ph.enter_context
        wb = [ec(nc.sbuf_tensor("l1a_w%d" % i, [128, 3 * DT * 128], BF16))
              for i in range(2)]
        QTc = ec(nc.sbuf_tensor("l1a_QTc", [128, CH], BF16))
        KTc = ec(nc.sbuf_tensor("l1a_KTc", [128, CH], BF16))
        sqc = ec(nc.sbuf_tensor("l1a_sqc", [128, CH], BF16))
        pjk = ec(nc.sbuf_tensor("l1a_pjk", [128, CH], BF16))
        Vtok = ec(nc.sbuf_tensor("l1a_Vtok", [128, NQ * 130], BF16))
        ephiq = ec(nc.sbuf_tensor("l1a_ephiq", [128, T], BF16))
        ektok = ec(nc.sbuf_tensor("l1a_ektok", [128, NQ * 128], BF16))
        aout = ec(nc.sbuf_tensor("l1a_aout", [128, NQ * 128], BF16))
        nrow = [ec(nc.sbuf_tensor("l1a_nrow%d" % i, [1, 4 * CH], F32))
                for i in range(2)]
        ncols = ec(nc.sbuf_tensor("l1a_ncols", [128, 4 * NQ], F32))
        kvs = ec(nc.sbuf_tensor("l1a_kvs", [128, 65], BF16))
        dcol = ec(nc.sbuf_tensor("l1a_dcol", [128, 16], F32))
        ones11f = ec(nc.sbuf_tensor("l1a_ones11f", [1, 1], F32))
        pQ = ec(nc.psum_tensor("l1a_pQ", [128, CH], F32))
        pK = ec(nc.psum_tensor("l1a_pK", [128, CH], F32))
        pV = ec(nc.psum_tensor("l1a_pV", [128, CH], F32))
        pN = ec(nc.psum_tensor("l1a_pN", [128, CH], F32))
        pP = ec(nc.psum_tensor("l1a_pP", [128, CH], F32))
        pT_ = ec(nc.psum_tensor("l1a_pT", [128, 512], BF16))
        pA = ec(nc.psum_tensor("l1a_pA", [128, 512], F32))

        @p.b.vector
        def _(e):
            vt.inc(e.memset(ones11f[:], 1.0))

        for hp in range(8):
            @p.b.gpsimd
            def _(e, hp=hp):
                if hp < 2:
                    wait(e, (apt, xf_tok))
                else:
                    wait(e, (tv, tv.n))
                for j in range(3):
                    src = w["wqkvT"].rearrange("(i p) n -> p i n", p=128)[
                        :, :, slice(j * 1024 + hp * 128, j * 1024 + (hp + 1) * 128)]
                    qG.inc(e.dma_start(
                        out=wb[hp % 2][:, j * DT * 128:(j + 1) * DT * 128]
                        .rearrange("p (i n) -> p i n", n=128),
                        in_=src), 16)
            w_tok = qG.n

            @p.b.vector
            def _(e, hp=hp):
                wait(e, (tv, tv.n))
                vt.inc(e.memset(Vtok[:], 1.0))

            for ci in range(NCH):
                @p.b.tensor
                def _(e, hp=hp, ci=ci, w_tok=w_tok):
                    wait(e, (qG, w_tok), (apt, xf_tok), (vt, vt.n),
                         (sv, sv.n), (vs, vs.n))
                    for j, pdst in enumerate((pQ, pK, pV)):
                        for i in range(DT):
                            m = e.matmul(
                                out=pdst[:],
                                lhsT=wb[hp % 2][:, j * DT * 128 + i * 128:
                                                j * DT * 128 + (i + 1) * 128],
                                rhs=xfT[:, i * T + ci * CH: i * T + (ci + 1) * CH],
                                start=(i == 0), stop=(i == DT - 1))
                        tv.inc(m)

                @p.b.vector
                def _(e, ci=ci):
                    wait(e, (tv, tv.n))
                    e.tensor_copy(out=QTc[:], in_=pQ[:])
                    e.tensor_copy(out=KTc[:], in_=pK[:])
                    vs.inc(e.tensor_tensor(out=sqc[:], in0=QTc[:], in1=QTc[:],
                                           op=ALU.mult))

                @p.b.tensor
                def _(e, ci=ci):
                    wait(e, (vs, vs.n))
                    tv.inc(e.matmul(out=pN[0:65, :], lhsT=hlf65[:], rhs=sqc[:],
                                    start=True, stop=True))

                @p.b.vector
                def _(e, ci=ci):
                    wait(e, (tv, tv.n))
                    e.tensor_copy(out=nrow[0][:, 0:CH],
                                  in_=pN[0:1, :])
                    e.tensor_copy(out=nrow[1][:, 0:CH],
                                  in_=pN[64:65, :])
                    vs.inc(e.tensor_tensor(out=sqc[:], in0=KTc[:], in1=KTc[:],
                                           op=ALU.mult))

                @p.b.tensor
                def _(e, ci=ci):
                    wait(e, (vs, vs.n))
                    tv.inc(e.matmul(out=pN[0:65, :], lhsT=hlf65[:], rhs=sqc[:],
                                    start=True, stop=True))

                @p.b.vector
                def _(e, ci=ci):
                    wait(e, (tv, tv.n))
                    e.tensor_copy(out=nrow[0][:, 2 * CH:3 * CH],
                                  in_=pN[0:1, :])
                    vs.inc(e.tensor_copy(out=nrow[1][:, 2 * CH:3 * CH],
                                         in_=pN[64:65, :]))

                @p.b.scalar
                def _(e, ci=ci):
                    wait(e, (vs, vs.n))
                    for h2 in range(2):
                        m = e.activation(out=nrow[h2][:, CH:2 * CH],
                                         in_=nrow[h2][:, 0:CH], func=AF.Sqrt)
                        m = e.activation(out=nrow[h2][:, 3 * CH:4 * CH],
                                         in_=nrow[h2][:, 2 * CH:3 * CH],
                                         func=AF.Sqrt)
                    sv.inc(m)

                @p.b.vector
                def _(e, ci=ci):
                    wait(e, (sv, sv.n))
                    for h2 in range(2):
                        for sl in (CH, 3 * CH):
                            m = e.tensor_scalar(
                                out=nrow[h2][:, sl: sl + CH],
                                in0=nrow[h2][:, sl: sl + CH],
                                scalar1=-1.0, scalar2=None, op0=ALU.mult)
                    vs.inc(m)

                @p.b.tensor
                def _(e, ci=ci):
                    wait(e, (vs, vs.n), (apt, apt.n))
                    for s4 in range(4):
                        h2 = s4 % 2
                        sl = CH if s4 < 2 else 3 * CH
                        for st4 in range(CH // 128):
                            m = e.matmul(
                                out=pP[:, s4 * 4 + st4: s4 * 4 + st4 + 1],
                                lhsT=nrow[h2][:, sl + st4 * 128:
                                              sl + (st4 + 1) * 128],
                                rhs=ones11f[:], start=True, stop=True)
                    tv.inc(m)

                @p.b.vector
                def _(e, ci=ci):
                    wait(e, (tv, tv.n))
                    dst = ncols[:].rearrange("p (s t) -> p s t", t=NQ)[
                        :, :, ci * 4:(ci + 1) * 4]
                    vs.inc(e.tensor_copy(
                        out=dst, in_=pP[:, 0:16].rearrange("p (s t) -> p s t", t=4)))

                for h2 in range(2):
                    @p.b.tensor
                    def _(e, h2=h2, ci=ci):
                        wait(e, (vs, vs.n), (sv, sv.n))
                        hs = slice(h2 * 64, (h2 + 1) * 64)
                        tp = (64, 64) if h2 == 1 else None
                        tv.inc(e.matmul(out=pP[hs, 0:CH], lhsT=rf2[hs, :],
                                        rhs=QTc[hs, :], start=True, stop=True,
                                        tile_position=tp))

                    @p.b.scalar
                    def _(e, h2=h2, ci=ci):
                        wait(e, (tv, tv.n))
                        hs = slice(h2 * 64, (h2 + 1) * 64)
                        sv.inc(e.activation(out=ephiq[hs, ci * CH:(ci + 1) * CH],
                                            in_=pP[hs, 0:CH], func=AF.Exp,
                                            scale=0.125))

                for h2 in range(2):
                    @p.b.tensor
                    def _(e, h2=h2, ci=ci):
                        wait(e, (sv, sv.n), (vs, vs.n))
                        hs = slice(h2 * 64, (h2 + 1) * 64)
                        tp = (64, 64) if h2 == 1 else None
                        tv.inc(e.matmul(out=pP[hs, 0:CH], lhsT=rf2[hs, :],
                                        rhs=KTc[hs, :], start=True, stop=True,
                                        tile_position=tp))

                    @p.b.vector
                    def _(e, h2=h2, ci=ci):
                        wait(e, (tv, tv.n))
                        hs = slice(h2 * 64, (h2 + 1) * 64)
                        vs.inc(e.tensor_copy(out=pjk[hs, :], in_=pP[hs, 0:CH]))

                @p.b.vector
                def _(e, ci=ci):
                    wait(e, (tv, tv.n))
                    vs.inc(e.tensor_copy(out=QTc[:], in_=pV[:]))

                for st4 in range(CH // 128):
                    @p.b.tensor
                    def _(e, ci=ci, st4=st4):
                        wait(e, (vs, vs.n), (apt, apt.n), (sv, sv.n))
                        tv.inc(e.transpose(out=pT_[:, 0:128],
                                           in_=QTc[:, st4 * 128:(st4 + 1) * 128],
                                           identity=c["identity"][:]))

                    @p.b.vector
                    def _(e, ci=ci, st4=st4):
                        wait(e, (tv, tv.n))
                        ti = ci * 4 + st4
                        dstv = Vtok[:, ti * 130: ti * 130 + 130].rearrange(
                            "p (h x) -> p h x", x=65)[:, :, 0:64]
                        apt.inc(e.tensor_copy(
                            out=dstv,
                            in_=pT_[:, 0:128].rearrange(
                                "p (h x) -> p h x", x=64)))

                for h2 in range(2):
                    @p.b.tensor
                    def _(e, h2=h2, ci=ci):
                        wait(e, (sv, sv.n), (apt, apt.n))
                        hs = slice(h2 * 64, (h2 + 1) * 64)
                        for st4 in range(CH // 128):
                            tv.inc(e.transpose(
                                out=pT_[:, st4 * 64:(st4 + 1) * 64],
                                in_=pjk[hs, st4 * 128:(st4 + 1) * 128],
                                identity=c["identity"][hs, hs]))

                    @p.b.scalar
                    def _(e, h2=h2, ci=ci):
                        wait(e, (tv, tv.n), (vs, vs.n))
                        for st4 in range(CH // 128):
                            ti = ci * 4 + st4
                            sv.inc(e.activation(
                                out=ektok[:, ti * 128 + h2 * 64:
                                          ti * 128 + h2 * 64 + 64],
                                in_=pT_[:, st4 * 64:(st4 + 1) * 64],
                                func=AF.Exp,
                                bias=ncols[:, (2 + h2) * NQ + ti:
                                           (2 + h2) * NQ + ti + 1],
                                scale=0.125))

            for h2 in range(2):
                @p.b.tensor
                def _(e, h2=h2, hp=hp):
                    wait(e, (sv, sv.n), (apt, apt.n), (vs, vs.n))
                    tp = (0, 64) if h2 == 1 else None
                    for ti in range(NQ):
                        m = e.matmul(
                            out=pA[h2 * 64:(h2 + 1) * 64, 0:65],
                            lhsT=ektok[:, ti * 128 + h2 * 64:
                                       ti * 128 + h2 * 64 + 64],
                            rhs=Vtok[:, ti * 130 + h2 * 65:
                                     ti * 130 + (h2 + 1) * 65],
                            start=(ti == 0), stop=(ti == NQ - 1),
                            tile_position=tp)
                    tv.inc(m)

                @p.b.vector
                def _(e, h2=h2):
                    wait(e, (tv, tv.n))
                    hs = slice(h2 * 64, (h2 + 1) * 64)
                    vs.inc(e.tensor_copy(out=kvs[hs, :], in_=pA[hs, 0:65]))

            for h2 in range(2):
                hs = slice(h2 * 64, (h2 + 1) * 64)
                for tg in range(NQ // 4):
                    @p.b.tensor
                    def _(e, h2=h2, tg=tg, hs=hs):
                        wait(e, (vs, vs.n))
                        for k in range(4):
                            ti = tg * 4 + k
                            tv.inc(e.matmul(
                                out=pA[:, k * 128:k * 128 + 65],
                                lhsT=ephiq[hs, ti * 128:(ti + 1) * 128],
                                rhs=kvs[hs, 0:65], start=True, stop=True))

                    @p.b.scalar
                    def _(e, h2=h2, tg=tg):
                        wait(e, (tv, tv.n), (vs, vs.n))
                        sv.inc(e.activation(
                            out=dcol[:, 0:4],
                            in_=ncols[:, h2 * NQ + tg * 4:h2 * NQ + tg * 4 + 4],
                            func=AF.Exp))

                    @p.b.vector
                    def _(e, h2=h2, tg=tg):
                        wait(e, (sv, sv.n))
                        for k in range(4):
                            e.tensor_copy(out=dcol[:, 4 + k:5 + k],
                                          in_=pA[:, k * 128 + 64:k * 128 + 65])
                        e.drain()
                        e.tensor_tensor(out=dcol[:, 4:8], in0=dcol[:, 4:8],
                                        in1=dcol[:, 0:4], op=ALU.mult)
                        e.drain()
                        e.tensor_scalar(out=dcol[:, 4:8], in0=dcol[:, 4:8],
                                        scalar1=1e-6, scalar2=None,
                                        op0=ALU.add)
                        e.drain()
                        e.reciprocal(out=dcol[:, 8:12], in_=dcol[:, 4:8])
                        e.drain()
                        e.drain()
                        e.tensor_tensor(out=dcol[:, 12:16], in0=dcol[:, 0:4],
                                        in1=dcol[:, 8:12], op=ALU.mult)
                        vs.inc(e.drain())

                    @p.b.vector
                    def _(e, h2=h2, tg=tg):
                        wait(e, (vs, vs.n))
                        for k in range(4):
                            ti = tg * 4 + k
                            m = e.tensor_scalar(
                                out=aout[:, ti * 128 + h2 * 64:
                                         ti * 128 + h2 * 64 + 64],
                                in0=pA[:, k * 128:k * 128 + 64],
                                scalar1=dcol[:, 12 + k:13 + k],
                                scalar2=None, op0=ALU.mult)
                        vs.inc(m)

            for tg in range(NQ // 4):
                @p.b.tensor
                def _(e, tg=tg, hp=hp):
                    wait(e, (vs, vs.n), (apt, apt.n))
                    for z in range(4):
                        tv.inc(e.transpose(
                            out=pT_[:, z * 128:(z + 1) * 128],
                            in_=aout[:, (tg * 4 + z) * 128:
                                      (tg * 4 + z + 1) * 128],
                            identity=c["identity"][:]))

                @p.b.vector
                def _(e, tg=tg, hp=hp):
                    wait(e, (tv, tv.n))
                    apt.inc(e.tensor_copy(
                        out=aoT[:, hp * T + tg * 512: hp * T + (tg + 1) * 512],
                        in_=pT_[:]))

    ph_xf.close()

    with ExitStack() as ph:
        ec = ph.enter_context
        wo = ec(nc.sbuf_tensor("l1a_wo", [128, DT * 1024], BF16))
        stg = [ec(nc.sbuf_tensor("l1a_stg%d" % i, [128, DT * CH], F32))
               for i in range(2)]
        pO = ec(nc.psum_tensor("l1a_pO", [128, CH], F32))
        macc4 = macc.rearrange("(i p) (n f) -> p i n f", p=128, f=CH)
        stg_tok = [0, 0]

        @p.b.gpsimd
        def _(e):
            wait(e, (tv, tv.n), (vs, vs.n), (sv, sv.n), (apt, apt.n))
            qG.inc(e.dma_start(out=wo[:].rearrange("p (i n) -> p i n", n=1024),
                               in_=w["woutT"].rearrange("(i p) n -> p i n", p=128)),
                   16)
        wo_tok = qG.n

        for ci in range(NCH):
            spar = ci % 2
            for jd in range(DT):
                @p.b.tensor
                def _(e, ci=ci, jd=jd, wo_tok=wo_tok):
                    wait(e, (qG, wo_tok), (vs, vs.n))
                    for i in range(DT):
                        m = e.matmul(
                            out=pO[:],
                            lhsT=wo[:, i * 1024 + jd * 128:
                                    i * 1024 + (jd + 1) * 128],
                            rhs=aoT[:, i * T + ci * CH: i * T + (ci + 1) * CH],
                            start=(i == 0), stop=(i == DT - 1))
                    tv.inc(m)

                @p.b.vector
                def _(e, ci=ci, jd=jd, spar=spar):
                    wait(e, (tv, tv.n), (qG, stg_tok[spar]), (qG, smalls_tok))
                    vs.inc(e.tensor_scalar(
                        out=stg[spar][:, jd * CH:(jd + 1) * CH], in0=pO[:],
                        scalar1=boutc[:, jd:jd + 1], scalar2=None, op0=ALU.add))

            @p.b.gpsimd
            def _(e, ci=ci, spar=spar):
                wait(e, (vs, vs.n))
                qG.inc(e.dma_start(
                    out=macc4[:, :, ci, :],
                    in_=stg[spar][:].rearrange("p (i f) -> p i f", f=CH)), 16)
            stg_tok[spar] = qG.n

    ph_ao.close()
    emit_sandwich_tail(p, c, T, macc, xin, xout, tag="l1a_tail")


L1A_SHAPES = {
    "wqkvT": ([D, 3 * D], F32), "rfT": ([64, 64], F32),
    "woutT": ([D, D], F32), "boutc": ([128, 8], F32),
}


def pack_l1a(inp, pref="L1_attn_"):
    g = lambda n: np.asarray(inp[pref + n], dtype=np.float32)
    return {
        "wqkvT": np.ascontiguousarray(g("w_qkv").T),
        "rfT": np.ascontiguousarray(g("random_features").T),
        "woutT": np.ascontiguousarray(g("w_out").T),
        "boutc": _bias_cols(g("b_out"), 8),
    }


def build_l1a_test(T):
    nc = new_bass()
    xin = nc.dram_tensor("xin", [D, T], F32, kind="ExternalInput").ap()
    cident = nc.dram_tensor("cident", [128, 128], BF16, kind="ExternalInput").ap()
    xo = nc.dram_tensor("xout", [D, T], F32, kind="ExternalOutput").ap()
    macc = nc.dram_tensor("macc", [D, T], F32, kind="Internal").ap()
    w = declare_weights(nc, L1A_SHAPES)
    with ExitStack() as st:
        block = st.enter_context(nc.Block())
        p = P(nc, block, st)
        c = emit_consts(p, cident)
        emit_l1_attn(p, c, T, xin, xo, w, macc)
    return nc

def emit_l1_ffn(p, c, T, xin, xout, w, macc, due):
    nc = p.nc
    NCH = T // CH
    NQ = T // 128
    qS = p.sem("qS")
    qG = p.sem("qG")
    vt = p.sem("vt")
    tv = p.sem("tv")
    vs = p.sem("vs")
    sv = p.sem("sv")
    apt = p.sem("apt")
    gq = p.sem("gq")

    xfT = p.sbuf("l1f_xfT", [128, DT * T], BF16)
    iota64 = p.sbuf("l1f_iota64", [128, 64], F32)
    bia = p.sbuf("l1f_bia", [128, 72], F32)
    bgc = bia[:, 0:32]
    buc = bia[:, 32:64]
    bdc = bia[:, 64:72]

    @p.b.gpsimd
    def _(e):
        qG.inc(e.dma_start(out=iota64[:], in_=w["iota64"]), 16)
        qG.inc(e.dma_start(out=bgc, in_=w["bgc"]), 16)
        qG.inc(e.dma_start(out=buc, in_=w["buc"]), 16)
        qG.inc(e.dma_start(out=bdc, in_=w["bdc"]), 16)
        qG.inc(e.dma_start(out=due[:, 0:1024], in_=w["down_embed"]), 16)
        qG.inc(e.dma_start(out=due[:, 1024:2048], in_=w["up_embed"]), 16)
    smalls_tok = qG.n

    with ExitStack() as ph:
        banks = make_ln_banks(p, ph, tag="l1f_ln")
        xc = [ph.enter_context(nc.sbuf_tensor("l1f_xc%d" % i, [128, DT * CH], F32))
              for i in range(2)]
        emit_preln(p, c, T, xin, xfT, banks, gb=None, xc=xc)
    xf_tok = p.sem("apt").n

    with ExitStack() as ph:
        ec = ph.enter_context
        wq_ = ec(nc.sbuf_tensor("l1f_wq", [128, DT * 512], BF16))
        wsim = ec(nc.sbuf_tensor("l1f_wsim", [128, 4 * 512], BF16))
        qT = ec(nc.sbuf_tensor("l1f_qT", [128, 4 * T], BF16))
        simt = ec(nc.sbuf_tensor("l1f_simt", [128, 512], F32))
        vals = ec(nc.sbuf_tensor("l1f_vals", [128, 64], F32))
        ixf = ec(nc.sbuf_tensor("l1f_ixf", [128, 64], F32))
        i8buf = ec(nc.sbuf_tensor("l1f_i8", [128, 8], U32))
        comb = ec(nc.sbuf_tensor("l1f_comb", [128, 64], F32))
        oh = ec(nc.sbuf_tensor("l1f_oh", [128, 512], F32))
        allix = ec(nc.sbuf_tensor("l1f_allix", [128, 64], F32))
        sc8 = ec(nc.sbuf_tensor("l1f_sc8", [128, 8], F32))
        pk8 = ec(nc.sbuf_tensor("l1f_pk8", [128, 24], F32))
        sel = ec(nc.sbuf_tensor("l1f_sel", [128, 16], F32))
        idxf = ec(nc.sbuf_tensor("l1f_idxf", [128, 32], F32))
        idxu = ec(nc.sbuf_tensor("l1f_idxu", [128, 32], U32))
        soft = ec(nc.sbuf_tensor("l1f_soft", [128, 32], F32))
        srow = ec(nc.sbuf_tensor("l1f_srow", [128, 4], F32))
        xftok = ec(nc.sbuf_tensor("l1f_xftok", [128, 1024], BF16))
        dueb = [ec(nc.sbuf_tensor("l1f_due%d" % i, [128, 2048], BF16))
                for i in range(8)]
        dscr = ec(nc.sbuf_tensor("l1f_dscr", [128, 1024], BF16))
        wrp = ec(nc.sbuf_tensor("l1f_wrp", [128, 12], F32))
        estb = ec(nc.sbuf_tensor("l1f_estb", [128, 1024], BF16))
        esT = ec(nc.sbuf_tensor("l1f_esT", [128, 1024], F32))
        pS = ec(nc.psum_tensor("l1f_pS", [128, 512], F32))
        pT_ = ec(nc.psum_tensor("l1f_pT", [128, 128], BF16))
        pQp = ec(nc.psum_tensor("l1f_pQp", [128, 512], F32))

        @p.b.gpsimd
        def _(e):
            wait(e, (apt, xf_tok))
            qG.inc(e.dma_start(out=wq_[:].rearrange("p (i n) -> p i n", n=512),
                               in_=w["Wq"].rearrange("(i p) n -> p i n", p=128)),
                   16)
            qG.inc(e.dma_start(out=wsim[:].rearrange("p (i n) -> p i n", n=512),
                               in_=w["wsim"].rearrange("(i p) n -> p i n", p=128)),
                   16)
        wq_tok = qG.n

        for kt in range(4):
            for ci in range(NCH):
                @p.b.tensor
                def _(e, kt=kt, ci=ci):
                    wait(e, (qG, wq_tok), (vs, vs.n))
                    for i in range(DT):
                        m = e.matmul(
                            out=pQp[:],
                            lhsT=wq_[:, i * 512 + kt * 128:
                                     i * 512 + (kt + 1) * 128],
                            rhs=xfT[:, i * T + ci * CH: i * T + (ci + 1) * CH],
                            start=(i == 0), stop=(i == DT - 1))
                    tv.inc(m)

                @p.b.vector
                def _(e, kt=kt, ci=ci):
                    wait(e, (tv, tv.n))
                    vs.inc(e.tensor_copy(
                        out=qT[:, kt * T + ci * CH: kt * T + (ci + 1) * CH],
                        in_=pQp[:]))

        due_tok = [0] * 8
        for ti in range(NQ):
            @p.b.tensor
            def _(e, ti=ti):
                wait(e, (vs, vs.n))
                for kt in range(4):
                    m = e.matmul(out=pS[:],
                                 lhsT=qT[:, kt * T + ti * 128:
                                         kt * T + (ti + 1) * 128],
                                 rhs=wsim[:, kt * 512:(kt + 1) * 512],
                                 start=(kt == 0), stop=(kt == 3))
                tv.inc(m)

            @p.b.vector
            def _(e, ti=ti):
                wait(e, (tv, tv.n))
                e.tensor_copy(out=simt[:], in_=pS[:])
                vs.inc(e.drain())

            @p.b.vector
            def _(e, ti=ti):
                wait(e, (vs, vs.n))
                for g in range(8):
                    e.max(out=vals[:, g * 8:(g + 1) * 8],
                          in_=simt[:, g * 64:(g + 1) * 64])
                    e.drain()
                    e.max_index(out=i8buf[:],
                                in_max=vals[:, g * 8:(g + 1) * 8],
                                in_values=simt[:, g * 64:(g + 1) * 64])
                    e.drain()
                    e.tensor_copy(out=ixf[:, g * 8:(g + 1) * 8], in_=i8buf[:])
                vs.inc(e.drain())

            for h in range(4):
                @p.b.vector
                def _(e, ti=ti, h=h):
                    wait(e, (vs, vs.n), (sv, sv.n))
                    sx = vals[:, h * 8:(h + 1) * 8]
                    sy = vals[:, (4 + h) * 8:(5 + h) * 8]
                    e.tensor_tensor(out=comb[:].rearrange("p (a b) -> p a b", b=8),
                                    in0=sx.unsqueeze(2).to_broadcast([128, 8, 8]),
                                    in1=sy.unsqueeze(1).to_broadcast([128, 8, 8]),
                                    op=ALU.add)
                    e.drain()
                    e.max(out=sc8[:], in_=comb[:])
                    e.drain()
                    e.max_index(out=i8buf[:], in_max=sc8[:], in_values=comb[:])
                    e.drain()
                    e.tensor_copy(out=pk8[:, 0:8], in_=i8buf[:])
                    e.drain()
                    e.scalar_tensor_tensor(
                        out=allix[:].rearrange("p (a b) -> p a b", b=8),
                        in0=ixf[:, h * 8:(h + 1) * 8].unsqueeze(2)
                        .to_broadcast([128, 8, 8]),
                        scalar=64.0,
                        in1=ixf[:, (4 + h) * 8:(5 + h) * 8].unsqueeze(1)
                        .to_broadcast([128, 8, 8]),
                        op0=ALU.mult, op1=ALU.add)
                    e.tensor_tensor(out=oh[:].rearrange("p (a b) -> p a b", b=64),
                                    in0=pk8[:, 0:8].unsqueeze(2)
                                    .to_broadcast([128, 8, 64]),
                                    in1=iota64[:].unsqueeze(1)
                                    .to_broadcast([128, 8, 64]),
                                    op=ALU.is_equal)
                    e.drain()
                    e.tensor_tensor(out=oh[:].rearrange("p (a b) -> p a b", b=64),
                                    in0=oh[:].rearrange("p (a b) -> p a b", b=64),
                                    in1=allix[:].unsqueeze(1)
                                    .to_broadcast([128, 8, 64]),
                                    op=ALU.mult)
                    e.drain()
                    e.reduce_sum(out=idxf[:, h * 8:(h + 1) * 8],
                                 in_=oh[:].rearrange("p (a b) -> p a b", b=64),
                                 axis=AX)
                    e.drain()
                    vs.inc(e.tensor_scalar(out=srow[:, 0:1], in0=sc8[:, 0:1],
                                           scalar1=-1.0, scalar2=None,
                                           op0=ALU.mult))

                @p.b.scalar
                def _(e, ti=ti, h=h):
                    wait(e, (vs, vs.n))
                    sv.inc(e.activation(out=soft[:, h * 8:(h + 1) * 8],
                                        in_=sc8[:], func=AF.Exp,
                                        bias=srow[:, 0:1], scale=1.0))

                @p.b.vector
                def _(e, ti=ti, h=h):
                    wait(e, (sv, sv.n))
                    e.reduce_sum(out=srow[:, 1:2],
                                 in_=soft[:, h * 8:(h + 1) * 8], axis=AX)
                    e.drain()
                    e.reciprocal(out=srow[:, 2:3], in_=srow[:, 1:2])
                    vs.inc(e.drain())

                @p.b.vector
                def _(e, ti=ti, h=h):
                    wait(e, (vs, vs.n))
                    vs.inc(e.tensor_scalar(out=soft[:, h * 8:(h + 1) * 8],
                                           in0=soft[:, h * 8:(h + 1) * 8],
                                           scalar1=srow[:, 2:3], scalar2=None,
                                           op0=ALU.mult))

            @p.b.vector
            def _(e, ti=ti):
                wait(e, (vs, vs.n))
                e.tensor_copy(out=idxu[:], in_=idxf[:])
                vs.inc(e.drain())
            idx_tok = p.sem("vs").n

            for i in range(DT):
                @p.b.tensor
                def _(e, ti=ti, i=i):
                    wait(e, (vs, vs.n))
                    if not (ti == 0 and i == 0):
                        wait(e, (apt, apt.n))
                    tv.inc(e.transpose(
                        out=pT_[:],
                        in_=xfT[:, i * T + ti * 128: i * T + (ti + 1) * 128],
                        identity=c["identity"][:]))

                @p.b.vector
                def _(e, ti=ti, i=i):
                    wait(e, (tv, tv.n))
                    apt.inc(e.tensor_copy(out=xftok[:, i * 128:(i + 1) * 128],
                                          in_=pT_[:]))

            for j4 in range(8):
                kb = (j4 % 2) * 4
                for k in range(4):
                    j = j4 * 4 + k

                    @p.b.gpsimd
                    def _(e, ti=ti, j=j, k=k, kb=kb, idx_tok=idx_tok):
                        wait(e, (vs, idx_tok), (qG, smalls_tok))
                        if j >= 8:
                            wait(e, (vs, due_tok[kb + k]))
                        gq.inc(e.indirect_dma_start(
                            out=dueb[kb + k][:], out_offset=None,
                            in_=due,
                            in_offset=IndirectOffsetOnAxis(idxu[:, j:j + 1], 0)),
                            16)
                g_tok = gq.n

                @p.b.vector
                def _(e, ti=ti, j4=j4, kb=kb, g_tok=g_tok):
                    wait(e, (gq, g_tok), (sv, sv.n))
                    for k in range(4):
                        e.scalar_tensor_tensor(out=dscr[:],
                                               in0=dueb[kb + k][:, 0:1024],
                                               scalar=1.0, in1=xftok[:],
                                               op0=ALU.mult, op1=ALU.mult,
                                               accum_out=wrp[:, k:k + 1])
                        e.drain()
                    vs.inc(e.drain())

                @p.b.scalar
                def _(e, ti=ti, j4=j4):
                    wait(e, (vs, vs.n))
                    sv.inc(e.activation(out=wrp[:, 4:8], in_=wrp[:, 0:4],
                                        func=AF.Silu))

                @p.b.vector
                def _(e, ti=ti, j4=j4):
                    wait(e, (sv, sv.n))
                    e.tensor_tensor(out=wrp[:, 8:12], in0=wrp[:, 4:8],
                                    in1=soft[:, j4 * 4:(j4 + 1) * 4],
                                    op=ALU.mult)
                    vs.inc(e.drain())

                @p.b.vector
                def _(e, ti=ti, j4=j4, kb=kb):
                    wait(e, (vs, vs.n))
                    for k in range(4):
                        j = j4 * 4 + k
                        if j == 0:
                            m = e.tensor_scalar(out=estb[:],
                                                in0=dueb[kb + k][:, 1024:2048],
                                                scalar1=wrp[:, 8 + k:9 + k],
                                                scalar2=None, op0=ALU.mult)
                        else:
                            m = e.scalar_tensor_tensor(
                                out=estb[:], in0=dueb[kb + k][:, 1024:2048],
                                scalar=wrp[:, 8 + k:9 + k], in1=estb[:],
                                op0=ALU.mult, op1=ALU.add)
                        vs.inc(m)
                    for k in range(4):
                        due_tok[kb + k] = vs.n

            for i in range(DT):
                @p.b.tensor
                def _(e, ti=ti, i=i):
                    wait(e, (vs, vs.n), (apt, apt.n))
                    tv.inc(e.transpose(out=pT_[:],
                                       in_=estb[:, i * 128:(i + 1) * 128],
                                       identity=c["identity"][:]))

                @p.b.vector
                def _(e, ti=ti, i=i):
                    wait(e, (tv, tv.n))
                    if i == 0 and ti > 0:
                        wait(e, (qG, qG.n))
                    apt.inc(e.tensor_copy(out=esT[:, i * 128:(i + 1) * 128],
                                          in_=pT_[:]))

            @p.b.gpsimd
            def _(e, ti=ti):
                wait(e, (apt, apt.n))
                qG.inc(e.dma_start(
                    out=macc.rearrange("(i p) t -> p i t", p=128)[
                        :, :, ti * 128:(ti + 1) * 128],
                    in_=esT[:].rearrange("p (i t) -> p i t", t=128)), 16)
    pkm_tok = p.sem("qG").n

    with ExitStack() as ph:
        ec = ph.enter_context
        wg_ = ec(nc.sbuf_tensor("l1f_wg", [128, DT * 1024], BF16))
        wu_ = ec(nc.sbuf_tensor("l1f_wu", [128, DT * 1024], BF16))
        wd_ = ec(nc.sbuf_tensor("l1f_wd", [128, DT * 1024], BF16))
        hg = ec(nc.sbuf_tensor("l1f_hg", [128, DT * CH], BF16))
        hu = ec(nc.sbuf_tensor("l1f_hu", [128, DT * CH], BF16))
        stg = [ec(nc.sbuf_tensor("l1f_stg%d" % i, [128, DT * CH], F32))
               for i in range(2)]
        pG2 = [ec(nc.psum_tensor("l1f_pG%d" % i, [128, CH], F32))
               for i in range(2)]
        pU2 = [ec(nc.psum_tensor("l1f_pU%d" % i, [128, CH], F32))
               for i in range(2)]
        pF2_ = [ec(nc.psum_tensor("l1f_pF%d" % i, [128, CH], F32))
                for i in range(2)]
        gsil_tok = [0, 0]
        uev_tok = [0, 0]
        fev_tok = [0, 0]
        macc4 = macc.rearrange("(i p) (n f) -> p i n f", p=128, f=CH)
        stg_tok = [pkm_tok, pkm_tok]
        step_tok = []

        for Iq in range(4):
            @p.b.gpsimd
            def _(e, Iq=Iq):
                if Iq == 0:
                    wait(e, (tv, tv.n), (vs, vs.n), (sv, sv.n), (apt, apt.n))
                elif Iq >= 1:
                    wait(e, (tv, step_tok[Iq - 1]))
                qG.inc(e.dma_start(
                    out=wg_[:].rearrange("p (i n) -> p i n", n=1024),
                    in_=w["Wg"].rearrange("(i p) n -> p i n", p=128)[
                        :, :, Iq * 1024:(Iq + 1) * 1024]), 16)
                qG.inc(e.dma_start(
                    out=wu_[:].rearrange("p (i n) -> p i n", n=1024),
                    in_=w["Wu"].rearrange("(i p) n -> p i n", p=128)[
                        :, :, Iq * 1024:(Iq + 1) * 1024]), 16)
                qG.inc(e.dma_start(
                    out=wd_[:].rearrange("p (i n) -> p i n", n=1024),
                    in_=w["Wd"][Iq * 1024:(Iq + 1) * 1024].rearrange(
                        "(i p) n -> p i n", p=128)), 16)
            w_tok = qG.n

            for ci in range(NCH):
                spar = (Iq * NCH + ci) % 2
                for ii in range(8):
                    @p.b.tensor
                    def _(e, Iq=Iq, ci=ci, ii=ii, w_tok=w_tok):
                        wait(e, (qG, w_tok), (sv, gsil_tok[ii % 2]),
                             (vs, uev_tok[ii % 2]))
                        for i in range(DT):
                            e.matmul(
                                out=pG2[ii % 2][:],
                                lhsT=wg_[:, i * 1024 + ii * 128:
                                         i * 1024 + (ii + 1) * 128],
                                rhs=xfT[:, i * T + ci * CH: i * T + (ci + 1) * CH],
                                start=(i == 0), stop=(i == DT - 1))
                        for i in range(DT):
                            m = e.matmul(
                                out=pU2[ii % 2][:],
                                lhsT=wu_[:, i * 1024 + ii * 128:
                                         i * 1024 + (ii + 1) * 128],
                                rhs=xfT[:, i * T + ci * CH: i * T + (ci + 1) * CH],
                                start=(i == 0), stop=(i == DT - 1))
                        tv.inc(m)

                    @p.b.scalar
                    def _(e, Iq=Iq, ii=ii):
                        wait(e, (tv, tv.n), (qG, smalls_tok))
                        sv.inc(e.activation(out=hg[:, ii * CH:(ii + 1) * CH],
                                            in_=pG2[ii % 2][:], func=AF.Silu,
                                            bias=bgc[:, Iq * 8 + ii:
                                                     Iq * 8 + ii + 1],
                                            scale=1.0))
                    gsil_tok[ii % 2] = p.sem("sv").n

                    @p.b.vector
                    def _(e, Iq=Iq, ii=ii):
                        wait(e, (tv, tv.n))
                        vs.inc(e.tensor_scalar(out=hu[:, ii * CH:(ii + 1) * CH],
                                               in0=pU2[ii % 2][:],
                                               scalar1=buc[:, Iq * 8 + ii:
                                                           Iq * 8 + ii + 1],
                                               scalar2=None, op0=ALU.add))
                    uev_tok[ii % 2] = p.sem("vs").n

                @p.b.vector
                def _(e, ci=ci):
                    wait(e, (sv, sv.n))
                    vs.inc(e.scalar_tensor_tensor(out=hg[:], in0=hg[:],
                                                  scalar=1.0, in1=hu[:],
                                                  op0=ALU.mult, op1=ALU.mult))

                for jd in range(DT):
                    @p.b.tensor
                    def _(e, Iq=Iq, ci=ci, jd=jd):
                        wait(e, (vs, max(vs.n if jd == 0 else 0,
                                         fev_tok[jd % 2])))
                        for ii in range(8):
                            m = e.matmul(
                                out=pF2_[jd % 2][:],
                                lhsT=wd_[:, ii * 1024 + jd * 128:
                                         ii * 1024 + (jd + 1) * 128],
                                rhs=hg[:, ii * CH:(ii + 1) * CH],
                                start=(ii == 0), stop=(ii == 7))
                        tv.inc(m)

                    @p.b.vector
                    def _(e, Iq=Iq, ci=ci, jd=jd, spar=spar):
                        wait(e, (tv, tv.n), (qG, stg_tok[spar]))
                        if Iq == 0:
                            m = e.tensor_scalar(
                                out=stg[spar][:, jd * CH:(jd + 1) * CH],
                                in0=pF2_[jd % 2][:], scalar1=bdc[:, jd:jd + 1],
                                scalar2=None, op0=ALU.add)
                        else:
                            m = e.tensor_copy(
                                out=stg[spar][:, jd * CH:(jd + 1) * CH],
                                in_=pF2_[jd % 2][:])
                        vs.inc(m)
                    fev_tok[jd % 2] = p.sem("vs").n

                @p.b.gpsimd
                def _(e, Iq=Iq, ci=ci, spar=spar):
                    wait(e, (vs, vs.n), (qG, pkm_tok))
                    qG.inc(e.dma_start(
                        out=macc4[:, :, ci, :],
                        in_=stg[spar][:].rearrange("p (i f) -> p i f", f=CH),
                        accum_op=ALU.add), 16)
                stg_tok[spar] = qG.n
            step_tok.append(p.sem("tv").n)

        @p.b.gpsimd
        def _(e):
            wait(e, (qG, qG.n))

    emit_sandwich_tail(p, c, T, macc, xin, xout, tag="l1f_tail")


L1F_SHAPES = {
    "Wg": ([D, 4096], F32), "bgc": ([128, 32], F32),
    "Wu": ([D, 4096], F32), "buc": ([128, 32], F32),
    "Wd": ([4096, D], F32), "bdc": ([128, 8], F32),
    "Wq": ([D, 512], F32), "wsim": ([512, 512], F32),
    "down_embed": ([4096, D], F32), "up_embed": ([4096, D], F32),
    "iota64": ([128, 64], F32),
}


def pack_l1f(inp, pref="L1_ffn_"):
    g = lambda n: np.asarray(inp[pref + n], dtype=np.float32)
    w = {}
    w["Wg"] = g("Wg")
    w["bgc"] = _bias_cols(g("bg"), 32)
    w["Wu"] = g("Wu")
    w["buc"] = _bias_cols(g("bu"), 32)
    w["Wd"] = g("Wd")
    w["bdc"] = _bias_cols(g("bd"), 8)
    w["Wq"] = g("Wq")
    keys = g("keys")
    ws = np.zeros((512, 512), dtype=np.float32)
    for pp in range(2):
        for h in range(4):
            b = (pp * 4 + h) * 64
            ws[b:b + 64, b:b + 64] = keys[h, :, pp, :].T
    w["wsim"] = ws
    w["down_embed"] = g("down_embed")
    w["up_embed"] = g("up_embed")
    w["iota64"] = np.broadcast_to(np.arange(64, dtype=np.float32), (128, 64)).copy()
    return w


def build_l1f_test(T):
    nc = new_bass()
    xin = nc.dram_tensor("xin", [D, T], F32, kind="ExternalInput").ap()
    cident = nc.dram_tensor("cident", [128, 128], BF16, kind="ExternalInput").ap()
    xo = nc.dram_tensor("xout", [D, T], F32, kind="ExternalOutput").ap()
    macc = nc.dram_tensor("macc", [D, T], F32, kind="Internal").ap()
    due = nc.dram_tensor("due", [4096, 2048], BF16, kind="Internal").ap()
    w = declare_weights(nc, L1F_SHAPES)
    with ExitStack() as st:
        block = st.enter_context(nc.Block())
        p = P(nc, block, st)
        c = emit_consts(p, cident)
        emit_l1_ffn(p, c, T, xin, xo, w, macc, due)
    return nc



TFULL = 4096
_prog_cache = {}


def _build_block(name, T):
    key = (name, T)
    if key in _prog_cache:
        return _prog_cache[key]
    nc = new_bass()
    xin = nc.dram_tensor("xin", [D, T], F32, kind="ExternalInput").ap()
    cident = nc.dram_tensor("cident", [128, 128], BF16, kind="ExternalInput").ap()
    xo = nc.dram_tensor("xout", [D, T], F32, kind="ExternalOutput").ap()
    with ExitStack() as st:
        block = st.enter_context(nc.Block())
        p = P(nc, block, st)
        c = emit_consts(p, cident)
        if name == "l0a":
            w = declare_weights(nc, L0A_SHAPES)
            emit_l0_attn(p, c, T, xin, xo, w)
        elif name == "l0f":
            cidentf = nc.dram_tensor("cidentf", [128, 128], F32,
                                     kind="ExternalInput").ap()
            c["identf"] = p.sbuf("c_identf", [128, 128], F32)
            qS = p.sem("qS")

            @p.b.sync
            def _(e):
                qS.inc(e.dma_start(out=c["identf"][:], in_=cidentf), 16)
            macc = nc.dram_tensor("macc", [D, T], F32, kind="Internal").ap()
            w = declare_weights(nc, L0F_SHAPES)
            emit_l0_ffn(p, c, T, xin, xo, w, macc)
        elif name == "l1a":
            macc = nc.dram_tensor("macc", [D, T], F32, kind="Internal").ap()
            w = declare_weights(nc, L1A_SHAPES)
            emit_l1_attn(p, c, T, xin, xo, w, macc)
        elif name == "l1f":
            macc = nc.dram_tensor("macc", [D, T], F32, kind="Internal").ap()
            due = nc.dram_tensor("due", [4096, 2048], BF16, kind="Internal").ap()
            w = declare_weights(nc, L1F_SHAPES)
            emit_l1_ffn(p, c, T, xin, xo, w, macc, due)
    _prog_cache[key] = nc
    return nc


def _build_fused(T):
    key = ("fused", T)
    if key in _prog_cache:
        return _prog_cache[key]
    nc = new_bass()
    xin = nc.dram_tensor("xin", [D, T], F32, kind="ExternalInput").ap()
    cident = nc.dram_tensor("cident", [128, 128], BF16, kind="ExternalInput").ap()
    cidentf = nc.dram_tensor("cidentf", [128, 128], F32, kind="ExternalInput").ap()
    xo = nc.dram_tensor("xout", [D, T], F32, kind="ExternalOutput").ap()
    x1 = nc.dram_tensor("x1", [D, T], F32, kind="Internal").ap()
    x2 = nc.dram_tensor("x2", [D, T], F32, kind="Internal").ap()
    x3 = nc.dram_tensor("x3", [D, T], F32, kind="Internal").ap()
    macc = nc.dram_tensor("macc", [D, T], F32, kind="Internal").ap()
    due = nc.dram_tensor("due", [4096, 2048], BF16, kind="Internal").ap()
    w0a = declare_weights(nc, L0A_SHAPES, prefix="w0a_")
    w0f = declare_weights(nc, L0F_SHAPES, prefix="w0f_")
    w1a = declare_weights(nc, L1A_SHAPES, prefix="w1a_")
    w1f = declare_weights(nc, L1F_SHAPES, prefix="w1f_")
    with ExitStack() as st:
        block = st.enter_context(nc.Block())
        p = P(nc, block, st)
        c = emit_consts(p, cident)
        c["identf"] = p.sbuf("c_identf", [128, 128], F32)
        qS = p.sem("qS")

        @p.b.sync
        def _(e):
            qS.inc(e.dma_start(out=c["identf"][:], in_=cidentf), 16)

        for emitfn, args in (
            (emit_l0_attn, (p, c, T, xin, x1, w0a)),
            (emit_l0_ffn, (p, c, T, x1, x2, w0f, macc)),
            (emit_l1_attn, (p, c, T, x2, x3, w1a, macc)),
            (emit_l1_ffn, (p, c, T, x3, xo, w1f, macc, due)),
        ):
            emit_block_barrier(p)
            with ExitStack() as bs:
                p.scopes.append(bs)
                try:
                    emitfn(*args)
                finally:
                    p.scopes.pop()
    _prog_cache[key] = nc
    return nc


def kernel(**inputs):
    import os
    if not os.environ.get("KERNEL_UNFUSED"):
        return _kernel_fused(**inputs)
    return _kernel_unfused(**inputs)


def _kernel_fused(**inputs):
    T = TFULL
    x = np.asarray(inputs["L0_attn_x"], dtype=np.float32)
    ncores = x.shape[0]
    shards = [np.ascontiguousarray(x[i].T) for i in range(ncores)]
    base = {}
    for pref, pk in (("w0a_", pack_l0a(inputs)), ("w0f_", pack_l0f(inputs)),
                     ("w1a_", pack_l1a(inputs)), ("w1f_", pack_l1f(inputs))):
        for k, v in pk.items():
            base[pref + k] = v
    base["cident"] = np.eye(128).astype(ml_dtypes.bfloat16)
    base["cidentf"] = np.eye(128, dtype=np.float32)
    nc = _build_fused(T)
    in_maps = []
    for i in range(ncores):
        m = dict(base)
        m["xin"] = shards[i]
        in_maps.append(m)
    res = run_bass_kernel_spmd(nc, in_maps, core_ids=list(range(ncores)))
    return np.stack([np.asarray(res.results[i]["xout"]).T
                     for i in range(ncores)], axis=0).astype(np.float32)


def _kernel_unfused(**inputs):
    T = TFULL
    x = np.asarray(inputs["L0_attn_x"], dtype=np.float32)
    ncores = x.shape[0]
    shards = [np.ascontiguousarray(x[i].T) for i in range(ncores)]
    cident = np.eye(128).astype(ml_dtypes.bfloat16)
    cidentf = np.eye(128, dtype=np.float32)
    packs = {
        "l0a": pack_l0a(inputs),
        "l0f": pack_l0f(inputs),
        "l1a": pack_l1a(inputs),
        "l1f": pack_l1f(inputs),
    }
    for name in ("l0a", "l0f", "l1a", "l1f"):
        nc = _build_block(name, T)
        base = dict(packs[name])
        base["cident"] = cident
        if name == "l0f":
            base["cidentf"] = cidentf
        in_maps = []
        for i in range(ncores):
            m = dict(base)
            m["xin"] = shards[i]
            in_maps.append(m)
        res = run_bass_kernel_spmd(nc, in_maps, core_ids=list(range(ncores)))
        shards = [np.asarray(res.results[i]["xout"]) for i in range(ncores)]
    out = np.stack([s.T for s in shards], axis=0).astype(np.float32)
    return out
